# version 18
# baseline (speedup 1.0000x reference)
import numpy as np
import ml_dtypes
from contextlib import ExitStack

import concourse.bass as bass
import concourse.mybir as mybir
from concourse.bass_utils import run_bass_kernel_spmd

F32 = mybir.dt.float32
BF16 = mybir.dt.bfloat16
I32 = mybir.dt.int32
AF = mybir.ActivationFunctionType
ALU = mybir.AluOpType
AX = mybir.AxisListType

D = 1024
T = 2048
NSEQ = 2
NCORE = 8
NH = 16
HD = 64
KC = 8
NTILE = 16
SPAN = 512
NSPAN = 4
EPS = 1e-5
MAGIC = 12582912.0
C256 = 256.499969482421875
NEXP = 16
ALPHA_SILU = 1.702
LIMIT = 7.0

SUBBLOCKS = 4

KB = 1024


def host_constants():
    c = {}
    c["ident"] = np.eye(128, dtype=ml_dtypes.bfloat16)
    P = np.zeros((128, 128), dtype=ml_dtypes.bfloat16)
    for m in range(128):
        P[(m // 64) * 64 + ((m % 64) + 32) % 64, m] = 1
    c["perm64"] = P
    inv_freq = (1.0 / 10000.0 ** (np.arange(0, HD, 2, dtype=np.float32) / HD)).astype(np.float32)
    t = np.arange(T, dtype=np.float32)
    freqs = t[:, None] * inv_freq[None, :]
    emb = np.concatenate([np.sin(freqs), np.cos(freqs)], axis=-1).astype(np.float32)
    sinv = emb[:, 0::2].T
    cosv = emb[:, 1::2].T
    C = np.empty((128, T), dtype=np.float32)
    Sg = np.empty((128, T), dtype=np.float32)
    for b in range(4):
        C[b * 32:(b + 1) * 32] = cosv
        Sg[b * 32:(b + 1) * 32] = (-sinv) if (b % 2 == 0) else sinv
    c["ropeC"] = C.astype(ml_dtypes.bfloat16)
    c["ropeS"] = Sg.astype(ml_dtypes.bfloat16)
    a = np.linspace(0.0, 1.0, 256).astype(np.float32)
    c["alpha"] = np.tile(a, (128, 1)).astype(ml_dtypes.bfloat16)
    c["alpha1m"] = np.tile((1.0 - a).astype(np.float32), (128, 1)).astype(ml_dtypes.bfloat16)
    SEL2 = np.zeros((2, 128), dtype=ml_dtypes.bfloat16)
    SEL2[0, 0:64] = 1
    SEL2[1, 64:128] = 1
    c["sel2"] = SEL2
    SELT = np.zeros((128, KC * 2), dtype=ml_dtypes.bfloat16)
    for hp in range(KC):
        for p_ in range(128):
            SELT[p_, hp * 2 + (p_ // 64)] = 1
    c["selt"] = SELT
    c["onesrow"] = np.ones((1, 128), dtype=ml_dtypes.bfloat16)
    return c


def rope_perm_cols():
    p = []
    for h in range(NH):
        p += [h * HD + 2 * i for i in range(32)]
        p += [h * HD + 2 * i + 1 for i in range(32)]
    return np.array(p, dtype=np.int64)


def prep_weights(inp):
    w = {}
    qk = rope_perm_cols()
    wq0 = np.ascontiguousarray(np.asarray(inp["L0_attn_qkv_w"], dtype=np.float32).T)
    perm = np.concatenate([qk, D + qk, 2 * D + np.arange(D)])
    w["w_qkv0"] = np.ascontiguousarray(wq0[:, perm])
    w["b_qkv0"] = np.ascontiguousarray(np.asarray(inp["L0_attn_qkv_b"], dtype=np.float32)[perm])
    w["w_out0"] = np.ascontiguousarray(np.asarray(inp["L0_attn_out_w"], dtype=np.float32).T)
    w["b_out0"] = np.asarray(inp["L0_attn_out_b"], dtype=np.float32)
    w["rhs0"] = np.asarray(inp["L0_ffn_rhs"], dtype=np.float32)
    for k in ("ln1_g", "ln1_b", "bq", "bk", "bv", "bproj", "ln2_g", "ln2_b", "b1", "b2"):
        w[k + "1"] = np.asarray(inp["L1_attn_" + k], dtype=np.float32)
    for k in ("wq", "wk", "wv", "wproj", "w1", "w2"):
        w[k + "1"] = np.ascontiguousarray(np.asarray(inp["L1_attn_" + k], dtype=np.float32))
    pe = np.concatenate([2 * np.arange(D), 2 * np.arange(D) + 1])
    w["nsc"] = np.asarray(inp["L1_ffn_norm_scale"], dtype=np.float32)
    w["wg"] = np.ascontiguousarray(np.asarray(inp["L1_ffn_wg"], dtype=np.float32))
    w["bg"] = np.asarray(inp["L1_ffn_bg"], dtype=np.float32)
    w["we1"] = np.ascontiguousarray(np.asarray(inp["L1_ffn_w1"], dtype=np.float32)[:, :, pe])
    w["be1"] = np.ascontiguousarray(np.asarray(inp["L1_ffn_b1"], dtype=np.float32)[:, pe])
    w["we2"] = np.ascontiguousarray(np.asarray(inp["L1_ffn_w2"], dtype=np.float32))
    w["be2"] = np.asarray(inp["L1_ffn_b2"], dtype=np.float32)
    return w


class Buf:
    __slots__ = ("w", "r", "name")
    def __init__(self, name=""):
        self.w = None
        self.r = []
        self.name = name


class Sched:
    ENGS = ("t", "v", "s", "g", "q")

    def __init__(self, nc, ctx):
        self.nc = nc
        self.thunks = {e: [] for e in self.ENGS}
        self.sems = {}
        names = ["ct", "cv", "cs", "cg"]
        for q in ("dq", "ds", "dg"):
            names += [f"{q}{i}" for i in range(4)]
        for sn in names:
            self.sems[sn] = ctx.enter_context(nc.semaphore(sn))
        self.counts = {sn: 0 for sn in self.sems}
        self.marks = {e: {sn: 0 for sn in self.sems} for e in self.ENGS}
        self.csem = {"t": "ct", "v": "cv", "s": "cs", "g": "cg"}
        self.qbase = {"q": "dq", "s": "ds", "g": "dg"}
        self.qidx = {"q": 0, "s": 0, "g": 0}

    def _wait(self, e, tokens):
        need = {}
        for tk in tokens:
            if tk is None:
                continue
            sn, v = tk
            if self.marks[e][sn] < v and need.get(sn, 0) < v:
                need[sn] = v
        for sn in sorted(need):
            v = need[sn]
            self.marks[e][sn] = v
            sem = self.sems[sn]
            self.thunks[e].append(lambda eng, sem=sem, v=v: eng.wait_ge(sem, v))

    def _deps(self, reads, writes):
        tks = []
        for b in reads:
            if b.w is not None:
                tks.append(b.w)
        for b in writes:
            if b.w is not None:
                tks.append(b.w)
            tks.extend(b.r)
        return tks

    def _record(self, tok, reads, writes):
        for b in reads:
            b.r.append(tok)
        for b in writes:
            b.w = tok
            b.r = []

    def op(self, e, emit, reads=(), writes=()):
        return self.group(e, [emit], reads, writes)

    def group(self, e, emits, reads=(), writes=()):
        self._wait(e, self._deps(reads, writes))
        sn = self.csem[e]
        self.counts[sn] += 1
        v = self.counts[sn]
        sem = self.sems[sn]
        for em in emits[:-1]:
            self.thunks[e].append(em)
        self.thunks[e].append(lambda eng, em=emits[-1], sem=sem: em(eng).then_inc(sem, 1))
        tok = (sn, v)
        self._record(tok, reads, writes)
        return tok

    def dma(self, e, dst_ap, src_ap, reads=(), writes=(), slow=False):
        sn = f"{self.qbase[e]}{self.qidx[e] % 4}"
        self.qidx[e] += 1
        deps = self._deps(reads, writes)
        if self.counts[sn] > 0:
            deps = deps + [(sn, self.counts[sn])]
        self._wait(e, deps)
        self.counts[sn] += 16
        v = self.counts[sn]
        sem = self.sems[sn]
        self.thunks[e].append(
            lambda eng, d=dst_ap, s=src_ap, sem=sem, slow=slow: eng.dma_start(
                d, s, allow_slow_non_contiguous=slow).then_inc(sem, 16))
        tok = (sn, v)
        self._record(tok, reads, writes)
        return tok

    def fork(self, parent, name=""):
        b = Buf(name)
        b.w = parent.w
        b.r = list(parent.r)
        return b

    def absorb(self, parent, children):
        for c in children:
            if c.w is not None:
                parent.r.append(c.w)
            parent.r.extend(c.r)

    def fork_multi(self, parents, name=""):
        b = Buf(name)
        b.w = None
        r = []
        for pa in parents:
            if pa.w is not None:
                r.append(pa.w)
            r.extend(pa.r)
        b.r = r
        return b

    def barrier(self):
        toks = [(sn, self.counts[sn]) for sn in self.sems if self.counts[sn] > 0]
        for e in self.ENGS:
            self._wait(e, toks)

    def replay(self, block):
        def runner(thunks):
            def f(eng):
                for th in thunks:
                    th(eng)
            return f
        block.sync(runner(self.thunks["q"]))
        block.gpsimd(runner(self.thunks["g"]))
        block.tensor(runner(self.thunks["t"]))
        block.vector(runner(self.thunks["v"]))
        block.scalar(runner(self.thunks["s"]))


class Mem:
    def __init__(self, nc):
        self.nc = nc
        self.base = (nc.sbuf_base + 63) & ~63
        self.top = nc.sbuf_top
        assert self.base + OFF_END * KB <= self.top, \
            f"map too big: {self.base}+{OFF_END}K > {self.top}"
        self.n = 0

    def at(self, off_kb, shape, dtype, name=None):
        self.n += 1
        name = name or f"mv{self.n}"
        t = self.nc.alloc_sbuf_tensor_at(name, list(shape), dtype,
                                         offset=self.base + int(off_kb * KB))
        return t

OFF_CONST = 0
OFF_A = 4
OFF_B = 36
OFF_C = 68
OFF_D = 100
OFF_E = 133
OFF_W = 165
OFF_F = 181
OFF_S = 189
OFF_V = 197
OFF_END = 205


class P:
    pass


def build_program():
    nc = bass.Bass("TRN2", target_bir_lowering=False)
    io = {}
    io["x_in"] = nc.dram_tensor("x_in", [NSEQ * T, D], F32, kind="ExternalInput").ap()
    io["y_out"] = nc.dram_tensor("y_out", [NSEQ * T, D], F32, kind="ExternalOutput").ap()
    xh = nc.dram_tensor("x_hbm", [NSEQ * T, D], F32, kind="Internal").ap()
    oh = nc.dram_tensor("out_hbm", [T, D], F32, kind="Internal").ap()
    wspec = [("w_qkv0", [D, 3 * D]), ("b_qkv0", [3 * D]), ("w_out0", [D, D]), ("b_out0", [D]),
             ("rhs0", [D, D]),
             ("ln1_g1", [D]), ("ln1_b1", [D]), ("wq1", [D, D]), ("bq1", [D]),
             ("wk1", [D, D]), ("bk1", [D]), ("wv1", [D, D]), ("bv1", [D]),
             ("wproj1", [D, D]), ("bproj1", [D]), ("ln2_g1", [D]), ("ln2_b1", [D]),
             ("w11", [D, 4 * D]), ("b11", [4 * D]), ("w21", [4 * D, D]), ("b21", [D]),
             ("nsc", [D]), ("wg", [D, NEXP]), ("bg", [NEXP]),
             ("we1", [NEXP, D, 2 * D]), ("be1", [NEXP, 2 * D]),
             ("we2", [NEXP, D, D]), ("be2", [NEXP, D])]
    for name, shape in wspec:
        io[name] = nc.dram_tensor(name, shape, F32, kind="ExternalInput").ap()
    cspec = [("ident", [128, 128]), ("perm64", [128, 128]), ("ropeC", [128, T]),
             ("ropeS", [128, T]), ("alpha", [128, 256]), ("alpha1m", [128, 256]),
             ("sel2", [2, 128]), ("selt", [128, KC * 2]), ("onesrow", [1, 128])]
    for name, shape in cspec:
        io[name] = nc.dram_tensor(name, shape, BF16, kind="ExternalInput").ap()

    with ExitStack() as ctx:
        S = Sched(nc, ctx)
        build_body(nc, S, io, xh, oh, ctx)
        block = ctx.enter_context(nc.Block())
        S.replay(block)
    return nc


def build_body(nc, S, io, xh, oh, ctx):
    mem = Mem(nc)
    p = P()
    p.mem = mem
    p.ident = mem.at(0, [128, 128], BF16, "identc")
    p.perm64 = mem.at(0.25, [128, 128], BF16, "permc")
    p.selt = mem.at(0.5, [128, KC * 2], BF16, "seltc")
    p.onesrow = mem.at(0.75, [1, 128], BF16, "onesrc")
    p.alpha = mem.at(1.0, [128, 256], BF16, "alphac")
    p.alpha1m = mem.at(1.5, [128, 256], BF16, "alpha1mc")
    p.sel2 = mem.at(2.0, [2, 128], BF16, "sel2c")
    p.epsv = mem.at(3.5, [128, 4], F32, "epsv")
    p.bC = Buf("consts")
    for nm in ("ident", "perm64", "selt", "onesrow", "alpha", "alpha1m", "sel2"):
        S.dma("q", getattr(p, nm)[:], io[nm], writes=(p.bC,))
    S.op("v", lambda e: e.memset(p.epsv[:, 0:1], EPS), writes=(p.bC,))
    S.op("v", lambda e: e.memset(p.epsv[:, 1:2], MAGIC), writes=(p.bC,))

    p.bA, p.bB, p.bCc, p.bD, p.bE = Buf("A"), Buf("B"), Buf("C"), Buf("D"), Buf("E")
    p.bW, p.bF, p.bS, p.bV = Buf("W"), Buf("F"), Buf("S"), Buf("V")
    p.A = mem.at(OFF_A, [128, KC, T], BF16, "A_fm")
    p.B = mem.at(OFF_B, [128, KC, T], BF16, "B_fm")
    p.Cc = mem.at(OFF_C, [128, KC, T], BF16, "C_fm")
    p.Dv = mem.at(OFF_D, [128, NTILE, 65 * NH], BF16, "D_vtm")
    p.E = mem.at(OFF_E, [128, KC, T], BF16, "E_fm")
    p.Wb = mem.at(OFF_W, [128, KC, 1024], BF16, "W_small")
    p.Fa = mem.at(OFF_F, [128, 2, D], F32, "F_f32")
    p.Sb = mem.at(OFF_S, [128, 4, D], BF16, "S_tok")
    p.pp = [ctx.enter_context(nc.psum_tensor(f"psp{i}", [128, 1024], F32)) for i in range(3)]
    p.pb = [p.pp[i // 2][:, (i % 2) * 512:(i % 2 + 1) * 512] for i in range(6)]
    p.ptf = [ctx.enter_context(nc.psum_tensor(f"pstp{i}", [128, 512], F32)) for i in range(2)]
    p.pt = [t[:].bitcast(BF16) for t in p.ptf]
    p.pbB = [Buf(f"psb{i}") for i in range(6)]
    p.pend = None
    p.ptB = [Buf(f"pst{i}") for i in range(2)]

    for s in range(NSEQ):
        phase_l0attn(nc, S, io, p, s, xh)
        if SUBBLOCKS >= 2:
            phase_l0ffn(nc, S, io, p, s, xh)
        if SUBBLOCKS >= 3:
            phase_l1attn(nc, S, io, p, s, xh, oh)
        if SUBBLOCKS >= 4:
            phase_l1ffn(nc, S, io, p, s, xh)
        else:
            final_store(nc, S, io, p, s, xh)
    S.barrier()



def ln_stats(nc, S, p, xt_ap, xt_buf, st_ap, st_buf, bnt_ap, bnt_buf):
    S.group("v", [
        lambda e: e.bn_stats(bnt_ap[:, 0:6], xt_ap[:, 0:512]),
        lambda e: e.bn_stats(bnt_ap[:, 6:12], xt_ap[:, 512:1024]),
    ], reads=(xt_buf,), writes=(bnt_buf,))
    S.op("v", lambda e: e.bn_aggr(st_ap, bnt_ap), reads=(bnt_buf,), writes=(st_buf,))


def rstd(nc, S, p, var_ap, r_ap, rbuf, vbuf, eps_i=0):
    S.op("s", lambda e: e.activation(r_ap, var_ap, AF.Ln, bias=p.epsv[:, eps_i:eps_i + 1]),
         reads=(vbuf, p.bC), writes=(rbuf,))
    S.op("s", lambda e: e.activation(r_ap, r_ap, AF.Exp, scale=-0.5),
         reads=(rbuf,), writes=(rbuf,))


def tok_to_fm(nc, S, p, tok_tiles, dst3, dst_buf, sp, nch=KC):
    for c in range(nch):
        pti = c % 2
        ptn, ptb = p.pt[pti], p.ptB[pti]
        emits = []
        rset = []
        for j in range(4):
            ap, bf = tok_tiles[j]
            emits.append(lambda e, ap=ap, c=c, j=j, ptn=ptn:
                         e.transpose(ptn[:, j * 128:(j + 1) * 128],
                                     ap[:, c * 128:(c + 1) * 128], p.ident[:]))
            if bf not in rset:
                rset.append(bf)
        S.group("t", emits, reads=tuple(rset) + (p.bC,), writes=(ptb,))
        S.op("v", lambda e, c=c, ptn=ptn: e.tensor_copy(
            dst3[:, c, sp * SPAN:(sp + 1) * SPAN], ptn[:, 0:512]),
            reads=(ptb,), writes=(dst_buf,))


def tpose4(nc, S, p, srcs, src_bufs, dst_ap, dst_buf, pt_idx=0):
    ptn, ptb = p.pt[pt_idx], p.ptB[pt_idx]
    emits = []
    for j, ap in enumerate(srcs):
        emits.append(lambda e, ap=ap, j=j, ptn=ptn:
                     e.transpose(ptn[:, j * 128:(j + 1) * 128], ap, p.ident[:]))
    rset = []
    for b in src_bufs:
        if b not in rset:
            rset.append(b)
    S.group("t", emits, reads=tuple(rset) + (p.bC,), writes=(ptb,))
    S.op("v", lambda e, ptn=ptn: e.tensor_copy(dst_ap, ptn[:, 0:512]),
         reads=(ptb,), writes=(dst_buf,))


def work_alias(mem, s, idx):
    return mem.at(OFF_F + 4, [128, D], F32, f"wka{s}_{idx}")[:]


def bcast_row(nc, S, p, row_ap, row_buf, dst_ap, dst_buf, n, bank_i=5):
    bank, bankB = p.pb[bank_i], p.pbB[bank_i]
    for o in range(0, n, 512):
        wdt = min(512, n - o)
        S.op("t", lambda e, o=o, wdt=wdt: e.matmul(
            bank[:, 0:wdt], p.onesrow[:, 0:128], row_ap[:, o:o + wdt],
            start=True, stop=True), reads=(row_buf, p.bC), writes=(bankB,))
        S.op("v", lambda e, o=o, wdt=wdt: e.tensor_copy(dst_ap[:, o:o + wdt], bank[:, 0:wdt]),
             reads=(bankB,), writes=(dst_buf,))


def load_w_fm(nc, S, p, io, src, dst3, dstB, kc=KC):
    return S.dma("g", dst3, src.rearrange("(k p) n -> p k n", p=128), writes=(dstB,))



def phase_l0attn(nc, S, io, p, s, xh):
    x_src = io["x_in"]
    base = s * T
    mem = p.mem
    ropeC = mem.at(OFF_E + 16, [128, T], BF16, f"ropeC{s}")
    ropeS = mem.at(OFF_E + 20, [128, T], BF16, f"ropeS{s}")
    wk_st = mem.at(OFF_E, [128, KC, 1024], BF16, f"wkst{s}")
    vv = mem.at(OFF_V, [128, 256], F32, f"vv{s}")
    qkvb = vv[:, 0:24]
    stats = vv[:, 32:64].rearrange("p (t two) -> p t two", two=2)
    rst = vv[:, 64:80]
    cvec = vv[:, 80:96]
    bnt = vv[:, 96:108]
    bnt2 = vv[:, 96:120].rearrange("p (j c) -> p j c", j=2)
    hs2 = mem.at(OFF_V + 1.0, [2, KC, 2, 8], F32, f"hs2{s}")
    crow = mem.at(OFF_V + 1.25, [1, 16], BF16, f"crow{s}")
    brow = mem.at(OFF_V + 1.5, [1, D], BF16, f"brow{s}")
    bout_b = mem.at(OFF_V + 3.5, [128, D], BF16, f"boutb{s}")
    vb_b = mem.at(OFF_V + 5.5, [128, D], BF16, f"vbb{s}")
    etile = mem.at(OFF_S + 4, [128, 4, 512], BF16, f"et{s}")
    dsc = mem.at(OFF_S, [128, T], BF16, f"dsc{s}")
    dd2 = mem.at(OFF_F, [2, 512], F32, f"dd2{s}")
    rb2 = mem.at(OFF_F + 4, [2, T], BF16, f"rb2{s}")
    cq8 = mem.at(OFF_W, [1, 2, T], BF16, f"cq8{s}")
    kaug = mem.at(OFF_A, [128, 2, 2, T], BF16, f"kaug{s}")
    qaug = mem.at(OFF_A + 16, [128, 2, 2, 512], BF16, f"qaug{s}")
    etile2 = mem.at(OFF_S + 4, [128, 2, 1024], BF16, f"et2{s}")
    ctxu = mem.at(OFF_S, [128, 512], BF16, f"ctxu{s}")
    zzbuf = mem.at(OFF_F, [128, 2, 512], F32, f"zzbuf{s}")
    zz2 = mem.at(OFF_F, [2, 2, 512], F32, f"zz2{s}")
    zzrb = mem.at(OFF_F + 6, [2, 512], BF16, f"zzrb{s}")
    rzbs = mem.at(OFF_F + 4, [128, 512], BF16, f"rzbs{s}")
    otok = mem.at(OFF_F, [128, D], F32, f"otok{s}")
    xtile = mem.at(OFF_F + 4, [128, D], F32, f"xtile{s}")

    bW, bE2, bV2, bF2, bS2 = p.bW, p.bE, p.bV, p.bF, p.bS
    bA, bB, bC, bD = p.bA, p.bB, p.bCc, p.bD
    bet = [S.fork(bS2, f"l0et{j}") for j in range(4)]
    nmA = vv[:, 120:136]
    nmP = vv[:, 136:152]
    W3 = p.Wb[:]
    Dv4 = p.Dv[:].rearrange("p t (h c) -> p t h c", c=65)

    S.dma("q", ropeC[:], io["ropeC"], writes=(bE2,))
    S.dma("q", ropeS[:], io["ropeS"], writes=(bE2,))
    S.dma("s", qkvb, io["b_qkv0"].rearrange("(c p) -> p c", p=128), writes=(bV2,), slow=True)
    S.dma("g", brow[:], io["b_out0"].rearrange("(o n) -> o n", o=1), writes=(bV2,))
    bcast_row(nc, S, p, brow[:], bV2, bout_b[:], bV2, D)
    S.dma("g", brow[:], io["b_qkv0"].rearrange("(o n) -> o n", o=1)[:, 2 * D:3 * D], reads=(bV2,), writes=(bV2,))
    bcast_row(nc, S, p, brow[:], bV2, vb_b[:], bV2, D)
    load_w_fm(nc, S, p, io, io["w_qkv0"][:, 0:D], W3, bW)

    xt2 = xtile[:].tensor
    bxt = [S.fork(bF2, f"l0xt{j}") for j in range(2)]
    bst = [S.fork(bV2, f"l0st{t}") for t in range(NTILE)]
    bbn = [S.fork(bV2, f"l0bn{j}") for j in range(2)]
    bsb = [S.fork(bS2, f"l0sb{j}") for j in range(4)]
    xsl = [xtile[:], otok[:]]
    for sp in range(NSPAN):
        for j in range(4):
            ti = sp * 4 + j
            xs = xsl[ti % 2]
            S.dma("q", xs, x_src[base + ti * 128: base + (ti + 1) * 128, :],
                  writes=(bxt[ti % 2],))
            ln_stats(nc, S, p, xs, bxt[ti % 2], stats[:, ti, :], bst[ti],
                     bnt2[:, ti % 2, :], bbn[ti % 2])
            rstd(nc, S, p, stats[:, ti, 1:2], rst[:, ti:ti + 1], bst[ti], bst[ti])
            S.op("v", lambda e, ti=ti: e.scalar_tensor_tensor(
                nmA[:, ti:ti + 1], stats[:, ti, 0:1], -1.0, rst[:, ti:ti + 1],
                ALU.mult, ALU.mult), reads=(bst[ti],), writes=(bst[ti],))
            S.op("s", lambda e, j=j, ti=ti, xs=xs: e.activation(
                p.Sb[:, j, :], xs, AF.Identity, scale=rst[:, ti:ti + 1],
                bias=nmA[:, ti:ti + 1]),
                reads=(bxt[ti % 2], bst[ti]), writes=(bsb[j],))
        for c in range(KC):
            tpose4(nc, S, p, [p.Sb[:, j, c * 128:(c + 1) * 128] for j in range(4)],
                   [bsb[j] for j in range(4)], p.A[:, c, sp * SPAN:(sp + 1) * SPAN],
                   bA, pt_idx=c % 2)

    load_w_fm(nc, S, p, io, io["w_qkv0"][:, D:2 * D], wk_st[:], bE2)
    for nchunk in range(KC):
        for sp in range(NSPAN):
            bi = (nchunk * NSPAN + sp) % 4
            bank, bankB = p.pb[bi], p.pbB[bi]
            S.group("t", [
                (lambda e, k=k, nchunk=nchunk, sp=sp, bank=bank:
                 e.matmul(bank[:], W3[:, k, nchunk * 128:(nchunk + 1) * 128],
                          p.A[:, k, sp * SPAN:(sp + 1) * SPAN],
                          start=(k == 0), stop=(k == KC - 1)))
                for k in range(KC)], reads=(bW, bA), writes=(bankB,))
            S.op("s", lambda e, nchunk=nchunk, sp=sp, bank=bank:
                 e.activation(p.B[:, nchunk, sp * SPAN:(sp + 1) * SPAN], bank[:],
                              AF.Identity, bias=qkvb[:, nchunk:nchunk + 1]),
                 reads=(bankB, bV2), writes=(bB,))
    load_w_fm(nc, S, p, io, io["w_qkv0"][:, 2 * D:3 * D], W3, bW)
    for nchunk in range(KC):
        for sp in range(NSPAN):
            bi = (nchunk * NSPAN + sp) % 4
            bank, bankB = p.pb[bi], p.pbB[bi]
            S.group("t", [
                (lambda e, k=k, nchunk=nchunk, sp=sp, bank=bank:
                 e.matmul(bank[:], wk_st[:, k, nchunk * 128:(nchunk + 1) * 128],
                          p.A[:, k, sp * SPAN:(sp + 1) * SPAN],
                          start=(k == 0), stop=(k == KC - 1)))
                for k in range(KC)], reads=(bE2, bA), writes=(bankB,))
            S.op("s", lambda e, nchunk=nchunk, sp=sp, bank=bank:
                 e.activation(p.Cc[:, nchunk, sp * SPAN:(sp + 1) * SPAN], bank[:],
                              AF.Identity, bias=qkvb[:, 8 + nchunk:8 + nchunk + 1]),
                 reads=(bankB, bV2), writes=(bC,))
    S.op("v", lambda e: e.memset(Dv4[:, :, :, 64:65], 1.0), writes=(bD,))
    for ti in range(NTILE):
        for nh2 in range(2):
            bi = (ti * 2 + nh2) % 4
            bank, bankB = p.pb[bi], p.pbB[bi]
            S.group("t", [
                (lambda e, k=k, ti=ti, nh2=nh2, bank=bank:
                 e.matmul(bank[:], p.A[:, k, ti * 128:(ti + 1) * 128],
                          W3[:, k, nh2 * 512:(nh2 + 1) * 512],
                          start=(k == 0), stop=(k == KC - 1)))
                for k in range(KC)], reads=(bW, bA), writes=(bankB,))
            S.op("v", lambda e, ti=ti, nh2=nh2, bank=bank: e.scalar_tensor_tensor(
                Dv4[:, ti, 8 * nh2:8 * (nh2 + 1), 0:64],
                bank[:].rearrange("p (h c) -> p h c", c=64), 1.0,
                vb_b[:, nh2 * 512:(nh2 + 1) * 512].rearrange("p (h c) -> p h c", c=64),
                ALU.mult, ALU.add), reads=(bankB, bV2), writes=(bD,))

    ropesc = p.Sb[:].rearrange("p f d -> p (f d)")
    for src3, srcB in ((p.B[:], bB), (p.Cc[:], bC)):
        for c in range(KC):
            S.op("v", lambda e, c=c, src3=src3: e.tensor_tensor(
                ropesc[:, 0:T], src3[:, c, :], ropeC[:], ALU.mult),
                reads=(srcB, bE2), writes=(bS2,))
            for q4 in range(4):
                bi = 4 + q4 % 2
                bank, bankB = p.pb[bi], p.pbB[bi]
                S.op("t", lambda e, c=c, q4=q4, src3=src3, bank=bank: e.matmul(
                    bank[:], p.perm64[:], src3[:, c, q4 * 512:(q4 + 1) * 512],
                    start=True, stop=True), reads=(srcB, p.bC), writes=(bankB,))
                S.op("v", lambda e, c=c, q4=q4, bank=bank: e.tensor_tensor(
                    ropesc[:, T + q4 * 512: T + (q4 + 1) * 512], bank[:],
                    ropeS[:, q4 * 512:(q4 + 1) * 512], ALU.mult),
                    reads=(bankB, bE2), writes=(bS2,))
            S.op("v", lambda e, c=c, src3=src3: e.tensor_tensor(
                src3[:, c, :], ropesc[:, 0:T], ropesc[:, T:2 * T], ALU.add),
                reads=(bS2,), writes=(srcB,))

    for idx, (src3, srcB) in enumerate(((p.B[:], bB), (p.Cc[:], bC))):
        for c in range(KC):
            S.op("v", lambda e, c=c, src3=src3: e.tensor_tensor(
                ropesc[:, 0:T], src3[:, c, :], src3[:, c, :], ALU.mult),
                reads=(srcB,), writes=(bS2,))
            for sp in range(NSPAN):
                bi = 4 + sp % 2
                bank, bankB = p.pb[bi], p.pbB[bi]
                S.op("t", lambda e, c=c, sp=sp, bank=bank: e.matmul(
                    bank[0:2, :], p.selt[:, c * 2:(c + 1) * 2],
                    ropesc[:, sp * SPAN:(sp + 1) * SPAN], start=True, stop=True),
                    reads=(bS2, p.bC), writes=(bankB,))
                S.op("v", lambda e, c=c, sp=sp, idx=idx, bank=bank: e.tensor_reduce(
                    hs2[0:2, c, idx, sp:sp + 1],
                    bank[0:2, 0:512], AX.X, ALU.max),
                    reads=(bankB,), writes=(bV2,))
    for idx in range(2):
        S.op("v", lambda e, idx=idx: e.tensor_reduce(
            hs2[:, :, idx, 4:5], hs2[:, :, idx, 0:4], AX.X, ALU.max),
            reads=(bV2,), writes=(bV2,))
    S.op("v", lambda e: e.scalar_tensor_tensor(hs2[:, :, 0, 6:7], hs2[:, :, 0, 4:5],
                                               1.0, hs2[:, :, 1, 4:5],
                                               ALU.mult, ALU.add),
         reads=(bV2,), writes=(bV2,))
    S.op("v", lambda e: e.tensor_scalar_mul(hs2[:, :, 0, 7:8], hs2[:, :, 0, 6:7], -1.0 / 16.0),
         reads=(bV2,), writes=(bV2,))
    S.op("v", lambda e: e.tensor_scalar_mul(hs2[:, :, 0, 5:6], hs2[:, :, 0, 7:8], 4.0),
         reads=(bV2,), writes=(bV2,))
    S.op("v", lambda e: e.tensor_scalar(hs2[:, :, 0, 3:4], hs2[:, :, 0, 7:8], 8.0, 288.0,
                                        ALU.mult, ALU.add),
         reads=(bV2,), writes=(bV2,))

    bF2 = S.fork_multi([bF2, bxt[0], bxt[1]], "l0Fc")
    kA = [S.fork(p.bA, f"l0ka{j}") for j in range(2)]
    qA = [S.fork(p.bA, f"l0qa{j}") for j in range(2)]
    S.op("v", lambda e: e.memset(kaug[64:65, :, :, :], 1.0), writes=(kA[0], kA[1]))
    for hp in range(KC):
        S.dma("g", kaug[0:64, hp % 2, 0, :], p.Cc[0:64, hp, :],
              reads=(bC,), writes=(kA[hp % 2],))
        S.dma("g", kaug[0:64, hp % 2, 1, :], p.Cc[64:128, hp, :],
              reads=(bC,), writes=(kA[hp % 2],))
        S.op("v", lambda e, hp=hp: e.tensor_tensor(
            dsc[:], p.B[:, hp, :], p.Cc[:, hp, :], ALU.mult),
            reads=(bB, bC), writes=(bS2,))
        for sp in range(NSPAN):
            bank, bankB = p.pb[4], p.pbB[4]
            S.op("t", lambda e, hp=hp, sp=sp, bank=bank: e.matmul(
                bank[0:2, :], p.selt[:, hp * 2:(hp + 1) * 2],
                dsc[:, sp * SPAN:(sp + 1) * SPAN], start=True, stop=True),
                reads=(bS2, p.bC), writes=(bankB,))
            S.op("v", lambda e, hp=hp, sp=sp, bank=bank: e.tensor_scalar(
                dd2[:], bank[0:2, :], -0.5, hs2[0:2, hp, 0, 5:6],
                ALU.mult, ALU.add), reads=(bankB, bV2), writes=(bF2,))
            S.op("v", lambda e, hp=hp, sp=sp: e.tensor_scalar(
                rb2[:, sp * SPAN:(sp + 1) * SPAN], dd2[:], hs2[0:2, hp, 0, 3:4], None,
                ALU.min), reads=(bF2, bV2), writes=(bF2,))
        S.dma("s", cq8[:], rb2[:], reads=(bF2,), writes=(bW,))
        for sp in range(NSPAN):
            if sp % 2 == 0:
                cbank = [p.pb[4], p.pb[5]]
                cbankB = [p.pbB[4], p.pbB[5]]
            else:
                cbank = [p.ptf[0][:], p.ptf[1][:]]
                cbankB = [p.ptB[0], p.ptB[1]]
            S.dma("g", qaug[0:64, sp % 2, 0, :],
                  p.B[0:64, hp, sp * SPAN:(sp + 1) * SPAN],
                  reads=(bB,), writes=(qA[sp % 2],))
            S.dma("g", qaug[0:64, sp % 2, 1, :],
                  p.B[64:128, hp, sp * SPAN:(sp + 1) * SPAN],
                  reads=(bB,), writes=(qA[sp % 2],))
            S.dma("g", qaug[64:65, sp % 2, :, :],
                  cq8[0:1, :, sp * SPAN:(sp + 1) * SPAN],
                  reads=(bW,), writes=(qA[sp % 2],))
            def emit_s(tk, hp=hp, sp=sp):
                i2 = tk % 2
                S.group("t", [
                    (lambda e, hh=hh, tk=tk, i2=i2, hp=hp, sp=sp:
                     e.matmul(p.pb[i2 * 2 + hh][:],
                              kaug[0:65, hp % 2, hh, tk * 128:(tk + 1) * 128],
                              qaug[0:65, sp % 2, hh, :],
                              start=True, stop=True))
                    for hh in range(2)],
                    reads=(kA[hp % 2], qA[sp % 2]),
                    writes=(p.pbB[i2 * 2], p.pbB[i2 * 2 + 1]))
                S.op("s", lambda e, i2=i2: e.activation(
                    etile2[:, i2, :], p.pp[i2][:], AF.Exp, scale=0.125),
                    reads=(p.pbB[i2 * 2], p.pbB[i2 * 2 + 1]),
                    writes=(bet[i2 * 2], bet[i2 * 2 + 1]))
            def emit_pv(tk, hh, hp=hp):
                h = 2 * hp + hh
                esl = (tk % 2) * 2 + hh
                et = etile2[:, tk % 2, hh * 512:(hh + 1) * 512]
                S.op("t", lambda e, h=h, tk=tk, et=et, cb=cbank[hh]:
                     e.matmul(cb[0:65, :], Dv4[:, tk, h, :], et,
                              start=(tk == 0), stop=(tk == NTILE - 1)),
                     reads=(bD, bet[esl]), writes=(cbankB[hh],))
            def epi(hp=hp, sp=sp, cbank=cbank, cbankB=cbankB):
                for hh in range(2):
                    S.op("v", lambda e, hh=hh, cb=cbank[hh]: e.tensor_copy(
                        zzbuf[64:65, hh, :], cb[64:65, :]),
                        reads=(cbankB[hh],), writes=(bF2,))
                    S.op("v", lambda e, hh=hh, cb=cbank[hh]: e.tensor_copy(
                        ctxu[hh * 64:(hh + 1) * 64, :], cb[0:64, :]),
                        reads=(cbankB[hh],), writes=(bS2,))
                S.dma("s", zz2[0:2, 0, :], zzbuf[64:65, :, :], reads=(bF2,), writes=(bF2,))
                S.op("v", lambda e: e.reciprocal(zz2[:, 1, :], zz2[:, 0, :]),
                     reads=(bF2,), writes=(bF2,))
                S.op("v", lambda e: e.tensor_copy(zzrb[:], zz2[:, 1, :]),
                     reads=(bF2,), writes=(bF2,))
                bank, bankB = cbank[0], cbankB[0]
                S.op("t", lambda e, bank=bank: e.matmul(
                    bank[:], p.sel2[:], zzrb[:], start=True, stop=True),
                    reads=(bF2, p.bC), writes=(bankB,))
                S.op("v", lambda e, bank=bank: e.tensor_copy(rzbs[:], bank[:]),
                     reads=(bankB,), writes=(bF2,))
                S.op("v", lambda e, hp=hp, sp=sp: e.tensor_tensor(
                    p.E[:, hp, sp * SPAN:(sp + 1) * SPAN],
                    ctxu[:, :], rzbs[:], ALU.mult),
                    reads=(bS2, bF2), writes=(bE2,))
            for tk in range(NTILE):
                emit_s(tk)
                if tk == 2 and p.pend is not None:
                    p.pend()
                    p.pend = None
                if tk > 0:
                    for hh in range(2):
                        emit_pv(tk - 1, hh)
            for hh in range(2):
                emit_pv(NTILE - 1, hh)
            p.pend = epi

    if p.pend is not None:
        p.pend()
        p.pend = None
    load_w_fm(nc, S, p, io, io["w_out0"], W3, bW)
    ot2 = mem.at(OFF_B, [128, 2, D], F32, f"l0ot2{s}")
    bot = [S.fork(p.bB, f"l0ot{j}") for j in range(2)]
    bxt2 = [S.fork(bF2, f"l0px{j}") for j in range(2)]
    bst2 = [S.fork(bV2, f"l0pst{t}") for t in range(NTILE)]
    bbn2 = [S.fork(bV2, f"l0pbn{j}") for j in range(2)]
    for ti in range(NTILE):
        j2 = ti % 2
        oslot = ot2[:, j2, :]
        xs = xsl[j2]
        banks = [p.pb[(ti % 2) * 2], p.pb[(ti % 2) * 2 + 1]]
        banksB = [p.pbB[(ti % 2) * 2], p.pbB[(ti % 2) * 2 + 1]]
        for nh2 in range(2):
            S.group("t", [
                (lambda e, k=k, ti=ti, nh2=nh2, bank=banks[nh2]:
                 e.matmul(bank[:], p.E[:, k, ti * 128:(ti + 1) * 128],
                          W3[:, k, nh2 * 512:(nh2 + 1) * 512],
                          start=(k == 0), stop=(k == KC - 1)))
                for k in range(KC)], reads=(bE2, bW), writes=(banksB[nh2],))
        for nh2 in range(2):
            S.op("v", lambda e, nh2=nh2, oslot=oslot, bank=banks[nh2]: e.tensor_tensor(
                oslot[:, nh2 * 512:(nh2 + 1) * 512], bank[:],
                bout_b[:, nh2 * 512:(nh2 + 1) * 512], ALU.add),
                reads=(banksB[nh2], bV2), writes=(bot[j2],))
        ln_stats(nc, S, p, oslot, bot[j2], stats[:, ti, :], bst2[ti],
                 bnt2[:, j2, :], bbn2[j2])
        rstd(nc, S, p, stats[:, ti, 1:2], rst[:, ti:ti + 1], bst2[ti], bst2[ti])
        S.dma("q", xs, x_src[base + ti * 128: base + (ti + 1) * 128, :],
              writes=(bxt2[j2],))
        S.op("v", lambda e, ti=ti: e.scalar_tensor_tensor(
            nmP[:, ti:ti + 1], stats[:, ti, 0:1], -1.0, rst[:, ti:ti + 1],
            ALU.mult, ALU.mult), reads=(bst2[ti],), writes=(bst2[ti],))
        S.op("s", lambda e, ti=ti, oslot=oslot: e.activation(
            oslot, oslot, AF.Identity, scale=rst[:, ti:ti + 1],
            bias=nmP[:, ti:ti + 1]), reads=(bst2[ti],), writes=(bot[j2],))
        S.op("v", lambda e, oslot=oslot, xs=xs: e.tensor_tensor(
            oslot, oslot, xs, ALU.add), reads=(bxt2[j2],), writes=(bot[j2],))
        S.dma("s", xh[base + ti * 128: base + (ti + 1) * 128, :], oslot,
              reads=(bot[j2],))
    S.barrier()


def phase_l0ffn(nc, S, io, p, s, xh):
    base = s * T
    mem = p.mem
    vv = mem.at(OFF_V, [128, 256], F32, f"f_vv{s}")
    stats = vv[:, 0:32].rearrange("p (t two) -> p t two", two=2)
    rst = vv[:, 32:48]
    bnt = vv[:, 48:60]
    am = vv[:, 60:61]
    ls_all = vv[:, 64:80]
    rls = vv[:, 80:81]
    rscc = vv[:, 96:104]
    rrsc = vv[:, 104:105]
    rrow = mem.at(OFF_V + 1, [1, D], BF16, f"f_rrow{s}")
    rsc_b = mem.at(OFF_V + 3, [128, D], BF16, f"f_rscb{s}")
    xtile = mem.at(OFF_F, [128, D], F32, f"f_xt{s}")
    work = mem.at(OFF_F + 4, [128, D], F32, f"f_wk{s}")
    qi = mem.at(OFF_C + 16, [128, D], I32, f"f_qi{s}")
    rtile = p.Cc[:]
    rhsT3 = p.B[:]
    qrhs = p.Wb[:]
    bA, bB, bC, bW = p.bA, p.bB, p.bCc, p.bW
    bV2, bF2, bS2 = p.bV, p.bF, p.bS

    S.dma("g", rtile[:, :, 0:1024], io["rhs0"].rearrange("(k p) n -> p k n", p=128),
          writes=(bC,))
    for c in range(KC):
        for ktg in (0, 4):
            tpose4(nc, S, p,
                   [rtile[:, kt, c * 128:(c + 1) * 128] for kt in range(ktg, ktg + 4)],
                   [bC] * 4, rhsT3[:, c, ktg * 128:(ktg + 4) * 128], bB, pt_idx=c % 2)
    for c in range(KC):
        S.op("v", lambda e, c=c: e.tensor_reduce(
            rscc[:, c:c + 1], rhsT3[:, c, 0:1024], AX.X, ALU.max,
            apply_absolute_value=True), reads=(bB,), writes=(bV2,))
        S.op("v", lambda e, c=c: e.tensor_scalar(
            rscc[:, c:c + 1], rscc[:, c:c + 1], 1e-12, 1.0 / 127.0,
            ALU.max, ALU.mult), reads=(bV2,), writes=(bV2,))
        S.op("v", lambda e, c=c: e.reciprocal(rrsc[:], rscc[:, c:c + 1]),
             reads=(bV2,), writes=(bV2,))
        S.op("s", lambda e, c=c: e.activation(
            work[:], rhsT3[:, c, 0:1024], AF.Identity, scale=rrsc[:],
            bias=p.epsv[:, 1:2]), reads=(bB, bV2, p.bC), writes=(bF2,))
        S.op("v", lambda e: e.tensor_scalar(
            work[:], work[:], MAGIC, C256, ALU.subtract, ALU.add),
            reads=(bF2,), writes=(bF2,))
        S.op("v", lambda e: e.tensor_copy(qi[:], work[:]), reads=(bF2,), writes=(bC,))
        S.op("v", lambda e, c=c: e.tensor_scalar(
            rhsT3[:, c, 1024:2048], qi[:], 256.0, -127.0, ALU.subtract, ALU.max),
            reads=(bC,), writes=(bB,))
        S.op("v", lambda e, c=c: e.tensor_scalar(
            rhsT3[:, c, 1024:2048], rhsT3[:, c, 1024:2048], 127.0, None, ALU.min),
            reads=(bB,), writes=(bB,))
    for kt in range(KC):
        for cg in (0, 4):
            tpose4(nc, S, p,
                   [rhsT3[:, c, 1024 + kt * 128: 1024 + (kt + 1) * 128]
                    for c in range(cg, cg + 4)],
                   [bB] * 4, qrhs[:, kt, cg * 128:(cg + 4) * 128], bW, pt_idx=kt % 2)
    for c in range(KC):
        S.dma("g", rrow[0:1, c * 128:(c + 1) * 128], rscc[:, c:c + 1],
              reads=(bV2,), writes=(bV2,), slow=True)
    bcast_row(nc, S, p, rrow[:], bV2, rsc_b[:], bV2, D)

    wk2 = mem.at(OFF_B, [128, 2, D], F32, f"f_wk2{s}")
    qi2 = mem.at(OFF_B + 8, [128, 2, D], I32, f"f_qi2{s}")
    rls2 = vv[:, 80:82]
    bwk = [S.fork(p.bB, f"f_wk{j}") for j in range(2)]
    bqi = [S.fork(p.bB, f"f_qi{j}") for j in range(2)]
    bxt = [S.fork(bF2, f"f_xt{j}") for j in range(2)]
    bst = [S.fork(bV2, f"f_st{t}") for t in range(NTILE)]
    bbn = [S.fork(bV2, f"f_bn{j}") for j in range(2)]
    bsb = [S.fork(bS2, f"f_sb{j}") for j in range(4)]
    bnt2 = vv[:, 128:152].rearrange("p (j c) -> p j c", j=2)
    nmc = vv[:, 105:121]
    nmc3 = vv[:, 176:192]
    xsl = [xtile[:], work[:]]
    for sp in range(NSPAN):
        for j in range(4):
            ti = sp * 4 + j
            j2 = ti % 2
            xs = xsl[j2]
            wks = wk2[:, j2, :]
            qis = qi2[:, j2, :]
            S.dma("q", xs, xh[base + ti * 128: base + (ti + 1) * 128, :],
                  writes=(bxt[j2],))
            ln_stats(nc, S, p, xs, bxt[j2], stats[:, ti, :], bst[ti],
                     bnt2[:, j2, :], bbn[j2])
            rstd(nc, S, p, stats[:, ti, 1:2], rst[:, ti:ti + 1], bst[ti], bst[ti])
            S.op("v", lambda e, ti=ti: e.scalar_tensor_tensor(
                nmc[:, ti:ti + 1], stats[:, ti, 0:1], -1.0, rst[:, ti:ti + 1],
                ALU.mult, ALU.mult), reads=(bst[ti],), writes=(bst[ti],))
            S.op("s", lambda e, ti=ti, xs=xs, wks=wks: e.activation(
                wks, xs, AF.Identity, scale=rst[:, ti:ti + 1],
                bias=nmc[:, ti:ti + 1]),
                reads=(bxt[j2], bst[ti]), writes=(bwk[j2],))
            S.op("v", lambda e, ti=ti, wks=wks: e.tensor_reduce(
                ls_all[:, ti:ti + 1], wks, AX.X, ALU.max, apply_absolute_value=True),
                reads=(bwk[j2],), writes=(bst[ti],))
            S.op("v", lambda e, ti=ti: e.tensor_scalar(
                ls_all[:, ti:ti + 1], ls_all[:, ti:ti + 1], 1e-12, 1.0 / 127.0,
                ALU.max, ALU.mult), reads=(bst[ti],), writes=(bst[ti],))
            S.op("v", lambda e, ti=ti, j2=j2: e.reciprocal(
                rls2[:, j2:j2 + 1], ls_all[:, ti:ti + 1]),
                reads=(bst[ti],), writes=(bbn[j2],))
            S.op("s", lambda e, j2=j2, wks=wks: e.activation(
                wks, wks, AF.Identity, scale=rls2[:, j2:j2 + 1],
                bias=p.epsv[:, 1:2]),
                reads=(bbn[j2], p.bC), writes=(bwk[j2],))
            S.op("v", lambda e, wks=wks: e.tensor_scalar(
                wks, wks, MAGIC, C256, ALU.subtract, ALU.add),
                reads=(), writes=(bwk[j2],))
            S.op("v", lambda e, wks=wks, qis=qis: e.tensor_copy(qis, wks),
                 reads=(bwk[j2],), writes=(bqi[j2],))
            S.op("v", lambda e, j=j, qis=qis: e.tensor_scalar(
                p.Sb[:, j, :], qis, 256.0, -127.0, ALU.subtract, ALU.max),
                reads=(bqi[j2],), writes=(bsb[j],))
            S.op("v", lambda e, j=j: e.tensor_scalar(
                p.Sb[:, j, :], p.Sb[:, j, :], 127.0, None, ALU.min),
                reads=(), writes=(bsb[j],))
        for c in range(KC):
            tpose4(nc, S, p, [p.Sb[:, j, c * 128:(c + 1) * 128] for j in range(4)],
                   [bsb[j] for j in range(4)], p.A[:, c, sp * SPAN:(sp + 1) * SPAN],
                   bA, pt_idx=c % 2)

    for ti in range(NTILE):
        j2 = ti % 2
        xs = xsl[j2]
        wks = wk2[:, j2, :]
        banks = [p.pb[(ti % 2) * 2], p.pb[(ti % 2) * 2 + 1]]
        banksB = [p.pbB[(ti % 2) * 2], p.pbB[(ti % 2) * 2 + 1]]
        for nh2 in range(2):
            S.group("t", [
                (lambda e, k=k, ti=ti, nh2=nh2, bank=banks[nh2]:
                 e.matmul(bank[:], p.A[:, k, ti * 128:(ti + 1) * 128],
                          qrhs[:, k, nh2 * 512:(nh2 + 1) * 512],
                          start=(k == 0), stop=(k == KC - 1)))
                for k in range(KC)], reads=(bA, bW), writes=(banksB[nh2],))
        for nh2 in range(2):
            S.op("v", lambda e, ti=ti, nh2=nh2, wks=wks, bank=banks[nh2]:
                 e.scalar_tensor_tensor(
                     wks[:, nh2 * 512:(nh2 + 1) * 512], bank[:], ls_all[:, ti:ti + 1],
                     rsc_b[:, nh2 * 512:(nh2 + 1) * 512], ALU.mult, ALU.mult),
                 reads=(banksB[nh2], bst[ti], bV2), writes=(bwk[j2],))
        ln_stats(nc, S, p, wks, bwk[j2], stats[:, ti, :], bst[ti],
                 bnt2[:, j2, :], bbn[j2])
        rstd(nc, S, p, stats[:, ti, 1:2], rst[:, ti:ti + 1], bst[ti], bst[ti])
        S.dma("q", xs, xh[base + ti * 128: base + (ti + 1) * 128, :], writes=(bxt[j2],))
        S.op("v", lambda e, ti=ti: e.scalar_tensor_tensor(
            nmc3[:, ti:ti + 1], stats[:, ti, 0:1], -1.0, rst[:, ti:ti + 1],
            ALU.mult, ALU.mult), reads=(bst[ti],), writes=(bst[ti],))
        S.op("s", lambda e, ti=ti, wks=wks: e.activation(
            wks, wks, AF.Identity, scale=rst[:, ti:ti + 1],
            bias=nmc3[:, ti:ti + 1]), reads=(bst[ti],), writes=(bwk[j2],))
        S.op("v", lambda e, wks=wks, xs=xs: e.tensor_tensor(wks, wks, xs, ALU.add),
             reads=(bxt[j2],), writes=(bwk[j2],))
        S.dma("s", xh[base + ti * 128: base + (ti + 1) * 128, :], wks,
              reads=(bwk[j2],))
    S.barrier()


def phase_l1attn(nc, S, io, p, s, xh, oh):
    base = s * T
    mem = p.mem
    vv = mem.at(OFF_V, [128, 256], F32, f"a_vv{s}")
    stats0 = vv[:, 0:32].rearrange("p (t two) -> p t two", two=2)
    rst0 = vv[:, 32:48]
    A1 = vv[:, 48:64]
    stats2 = vv[:, 64:96].rearrange("p (t two) -> p t two", two=2)
    rst2 = vv[:, 96:112]
    bnt = vv[:, 112:124]
    qkb = vv[:, 128:144]
    sc1 = vv[:, 144:160]
    nmA = vv[:, 160:176]
    nmD = vv[:, 216:232]
    hs2 = mem.at(OFF_V + 1, [2, KC, 2, 8], F32, f"a_hs2{s}")
    bcs0 = mem.at(OFF_V + 1.5, [128, D], BF16, f"a_bc0{s}")
    bcs1 = mem.at(OFF_V + 3.5, [128, D], BF16, f"a_bc1{s}")
    brow = mem.at(OFF_V + 5.5, [1, D], BF16, f"a_brow{s}")
    wk_st = mem.at(OFF_E, [128, KC, 1024], BF16, f"a_wkst{s}")
    dsc = mem.at(OFF_S, [128, T], BF16, f"a_dsc{s}")
    wctx = mem.at(OFF_S, [128, 512], BF16, f"a_wctx{s}")
    wn = mem.at(OFF_S + 1, [128, 512], BF16, f"a_wn{s}")
    etile = mem.at(OFF_S + 4, [128, 4, 512], BF16, f"a_et{s}")
    dd2 = mem.at(OFF_F, [2, 512], F32, f"a_dd2{s}")
    rb2 = mem.at(OFF_F + 4, [2, T], BF16, f"a_rb2{s}")
    zzbuf = mem.at(OFF_F, [128, 2, 512], F32, f"a_zzb{s}")
    zz2 = mem.at(OFF_F, [2, 2, 512], F32, f"a_zz2{s}")
    zzrb = mem.at(OFF_F + 6, [2, 512], BF16, f"a_zzrb{s}")
    rzbs = mem.at(OFF_F + 4, [128, 512], BF16, f"a_rzbs{s}")
    cq8 = mem.at(OFF_W, [1, 2, T], BF16, f"a_cq8{s}")
    xtile = mem.at(OFF_F, [128, D], F32, f"a_xt{s}")
    work = mem.at(OFF_F + 4, [128, D], F32, f"a_wk2{s}")
    x2fm = mem.at(OFF_W, [128, KC, 512], BF16, f"a_x2fm{s}")
    wm1a = p.A[:]
    wm1b = p.B[:]
    wm2c = p.Cc[:].rearrange("p c (two n) -> p c two n", two=2)
    wm2d = p.Dv[:].rearrange("p t x -> p (t x)")
    hsp = p.E[:].rearrange("p c (x n) -> p (c x) n", x=4)
    W3 = p.Wb[:]
    Dv4 = p.Dv[:].rearrange("p t (h c) -> p t h c", c=65)
    bA, bB, bC, bD, bE2 = p.bA, p.bB, p.bCc, p.bD, p.bE
    bW, bV2, bF2, bS2 = p.bW, p.bV, p.bF, p.bS

    def bc_tile(row_src, dst, n=D):
        S.dma("g", brow[0:1, 0:n], row_src, reads=(bV2,), writes=(bV2,))
        bcast_row(nc, S, p, brow[:], bV2, dst, bV2, n)

    bc_tile(io["ln1_g1"].rearrange("(o n) -> o n", o=1), bcs0[:])
    bc_tile(io["ln1_b1"].rearrange("(o n) -> o n", o=1), bcs1[:])
    S.dma("s", qkb[:, 0:8], io["bq1"].rearrange("(c p) -> p c", p=128),
          reads=(bV2,), writes=(bV2,), slow=True)
    S.dma("s", qkb[:, 8:16], io["bk1"].rearrange("(c p) -> p c", p=128),
          reads=(bV2,), writes=(bV2,), slow=True)
    load_w_fm(nc, S, p, io, io["wq1"], W3, bW)
    bxt = [S.fork(bF2, f"a_xt{j}") for j in range(2)]
    bst = [S.fork(bV2, f"a_st{t}") for t in range(NTILE)]
    bbn = [S.fork(bV2, f"a_bn{j}") for j in range(2)]
    bsb = [S.fork(bS2, f"a_sb{j}") for j in range(4)]
    bnt2 = vv[:, 192:216].rearrange("p (j c) -> p j c", j=2)
    xsl = [xtile[:], work[:]]
    for sp in range(NSPAN):
        for j in range(4):
            ti = sp * 4 + j
            j2 = ti % 2
            xs = xsl[j2]
            S.dma("q", xs, xh[base + ti * 128: base + (ti + 1) * 128, :],
                  writes=(bxt[j2],))
            ln_stats(nc, S, p, xs, bxt[j2], stats0[:, ti, :], bst[ti],
                     bnt2[:, j2, :], bbn[j2])
            rstd(nc, S, p, stats0[:, ti, 1:2], rst0[:, ti:ti + 1], bst[ti], bst[ti])
            S.op("v", lambda e, ti=ti, j2=j2: e.tensor_tensor(
                sc1[:, 3 * j2:3 * j2 + 1], rst0[:, ti:ti + 1], rst0[:, ti:ti + 1],
                ALU.mult), reads=(bst[ti],), writes=(bbn[j2],))
            S.op("v", lambda e, ti=ti, j2=j2: e.tensor_tensor(
                sc1[:, 3 * j2 + 1:3 * j2 + 2], sc1[:, 3 * j2:3 * j2 + 1],
                stats0[:, ti, 1:2], ALU.mult), reads=(bst[ti],), writes=(bbn[j2],))
            rstd(nc, S, p, sc1[:, 3 * j2 + 1:3 * j2 + 2], sc1[:, 3 * j2 + 2:3 * j2 + 3],
                 bbn[j2], bbn[j2])
            S.op("v", lambda e, ti=ti, j2=j2: e.tensor_tensor(
                A1[:, ti:ti + 1], rst0[:, ti:ti + 1], sc1[:, 3 * j2 + 2:3 * j2 + 3],
                ALU.mult), reads=(bbn[j2], bst[ti]), writes=(bst[ti],))
            S.op("v", lambda e, ti=ti: e.scalar_tensor_tensor(
                nmA[:, ti:ti + 1], stats0[:, ti, 0:1], -1.0, A1[:, ti:ti + 1],
                ALU.mult, ALU.mult), reads=(bst[ti],), writes=(bst[ti],))
            S.op("s", lambda e, j=j, ti=ti, xs=xs: e.activation(
                p.Sb[:, j, :], xs, AF.Identity, scale=A1[:, ti:ti + 1],
                bias=nmA[:, ti:ti + 1]),
                reads=(bxt[j2], bst[ti]), writes=(bsb[j],))
            S.op("v", lambda e, j=j: e.tensor_tensor(
                p.Sb[:, j, :], p.Sb[:, j, :], bcs0[:], ALU.mult),
                reads=(bV2,), writes=(bsb[j],))
            S.op("v", lambda e, j=j: e.tensor_tensor(
                p.Sb[:, j, :], p.Sb[:, j, :], bcs1[:], ALU.add),
                reads=(bV2,), writes=(bsb[j],))
        for c in range(KC):
            tpose4(nc, S, p, [p.Sb[:, j, c * 128:(c + 1) * 128] for j in range(4)],
                   [bsb[j] for j in range(4)], p.A[:, c, sp * SPAN:(sp + 1) * SPAN],
                   bA, pt_idx=c % 2)

    load_w_fm(nc, S, p, io, io["wk1"], wk_st[:], bE2)
    for nchunk in range(KC):
        for sp in range(NSPAN):
            bi = (nchunk * NSPAN + sp) % 4
            bank, bankB = p.pb[bi], p.pbB[bi]
            S.group("t", [
                (lambda e, k=k, nchunk=nchunk, sp=sp, bank=bank:
                 e.matmul(bank[:], W3[:, k, nchunk * 128:(nchunk + 1) * 128],
                          p.A[:, k, sp * SPAN:(sp + 1) * SPAN],
                          start=(k == 0), stop=(k == KC - 1)))
                for k in range(KC)], reads=(bW, bA), writes=(bankB,))
            S.op("s", lambda e, nchunk=nchunk, sp=sp, bank=bank:
                 e.activation(p.B[:, nchunk, sp * SPAN:(sp + 1) * SPAN], bank[:],
                              AF.Identity, bias=qkb[:, nchunk:nchunk + 1]),
                 reads=(bankB, bV2), writes=(bB,))
    load_w_fm(nc, S, p, io, io["wv1"], W3, bW)
    for nchunk in range(KC):
        for sp in range(NSPAN):
            bi = (nchunk * NSPAN + sp) % 4
            bank, bankB = p.pb[bi], p.pbB[bi]
            S.group("t", [
                (lambda e, k=k, nchunk=nchunk, sp=sp, bank=bank:
                 e.matmul(bank[:], wk_st[:, k, nchunk * 128:(nchunk + 1) * 128],
                          p.A[:, k, sp * SPAN:(sp + 1) * SPAN],
                          start=(k == 0), stop=(k == KC - 1)))
                for k in range(KC)], reads=(bE2, bA), writes=(bankB,))
            S.op("s", lambda e, nchunk=nchunk, sp=sp, bank=bank:
                 e.activation(p.Cc[:, nchunk, sp * SPAN:(sp + 1) * SPAN], bank[:],
                              AF.Identity, bias=qkb[:, 8 + nchunk:8 + nchunk + 1]),
                 reads=(bankB, bV2), writes=(bC,))
    S.op("v", lambda e: e.memset(Dv4[:, :, :, 64:65], 1.0), writes=(bD,))
    S.dma("g", brow[:], io["bv1"].rearrange("(o n) -> o n", o=1),
          reads=(bV2,), writes=(bV2,))
    for ti in range(NTILE):
        for nh2 in range(2):
            bi = (ti * 2 + nh2) % 4
            bank, bankB = p.pb[bi], p.pbB[bi]
            S.group("t", [
                (lambda e, nh2=nh2, bank=bank: e.matmul(
                    bank[:], p.onesrow[:, 0:128], brow[0:1, nh2 * 512:(nh2 + 1) * 512],
                    start=True, stop=False))] + [
                (lambda e, k=k, ti=ti, nh2=nh2, bank=bank:
                 e.matmul(bank[:], p.A[:, k, ti * 128:(ti + 1) * 128],
                          W3[:, k, nh2 * 512:(nh2 + 1) * 512],
                          start=False, stop=(k == KC - 1)))
                for k in range(KC)], reads=(bW, bA, bV2, p.bC), writes=(bankB,))
            S.op("v", lambda e, ti=ti, nh2=nh2, bank=bank: e.tensor_copy(
                Dv4[:, ti, 8 * nh2:8 * (nh2 + 1), 0:64],
                bank[:].rearrange("p (h c) -> p h c", c=64)),
                reads=(bankB,), writes=(bD,))

    for idx, (src3, srcB) in enumerate(((p.B[:], bB), (p.Cc[:], bC))):
        for c in range(KC):
            S.op("v", lambda e, c=c, src3=src3: e.tensor_tensor(
                dsc[:], src3[:, c, :], src3[:, c, :], ALU.mult),
                reads=(srcB,), writes=(bS2,))
            for sp in range(NSPAN):
                bi = 4 + sp % 2
                bank, bankB = p.pb[bi], p.pbB[bi]
                S.op("t", lambda e, c=c, sp=sp, bank=bank: e.matmul(
                    bank[0:2, :], p.selt[:, c * 2:(c + 1) * 2],
                    dsc[:, sp * SPAN:(sp + 1) * SPAN], start=True, stop=True),
                    reads=(bS2, p.bC), writes=(bankB,))
                S.op("v", lambda e, c=c, sp=sp, idx=idx, bank=bank: e.tensor_reduce(
                    hs2[0:2, c, idx, sp:sp + 1],
                    bank[0:2, 0:512], AX.X, ALU.max),
                    reads=(bankB,), writes=(bV2,))
    for idx in range(2):
        S.op("v", lambda e, idx=idx: e.tensor_reduce(
            hs2[:, :, idx, 4:5], hs2[:, :, idx, 0:4], AX.X, ALU.max),
            reads=(bV2,), writes=(bV2,))
    S.op("v", lambda e: e.scalar_tensor_tensor(hs2[:, :, 0, 6:7], hs2[:, :, 0, 4:5],
                                               1.0, hs2[:, :, 1, 4:5],
                                               ALU.mult, ALU.add),
         reads=(bV2,), writes=(bV2,))
    S.op("v", lambda e: e.tensor_scalar_mul(hs2[:, :, 0, 7:8], hs2[:, :, 0, 6:7], -1.0 / 16.0),
         reads=(bV2,), writes=(bV2,))
    S.op("v", lambda e: e.tensor_scalar_mul(hs2[:, :, 0, 5:6], hs2[:, :, 0, 7:8], 4.0),
         reads=(bV2,), writes=(bV2,))
    S.op("v", lambda e: e.tensor_scalar(hs2[:, :, 0, 3:4], hs2[:, :, 0, 7:8], 8.0, 288.0,
                                        ALU.mult, ALU.add),
         reads=(bV2,), writes=(bV2,))

    bF2 = S.fork_multi([bF2, bxt[0], bxt[1]], "aFc")
    betw = [S.fork(bS2, f"a_et{j}") for j in range(4)]
    kaug = mem.at(OFF_A, [128, 2, 2, T], BF16, f"a_kaug{s}")
    qaug = mem.at(OFF_A + 16, [128, 2, 2, 512], BF16, f"a_qaug{s}")
    etile2 = mem.at(OFF_S + 4, [128, 2, 1024], BF16, f"a_et2{s}")
    kA = [S.fork(p.bA, f"a_ka{j}") for j in range(2)]
    qA = [S.fork(p.bA, f"a_qa{j}") for j in range(2)]
    S.op("v", lambda e: e.memset(kaug[64:65, :, :, :], 1.0), writes=(kA[0], kA[1]))
    wins = []
    for w in range(8):
        i = 256 * w
        st = max(0, i - 128)
        en = min(T, i + 384)
        oe = min(i + 128, T)
        wins.append((st, en, oe))

    for hp in range(KC):
        S.dma("g", kaug[0:64, hp % 2, 0, :], p.Cc[0:64, hp, :],
              reads=(bC,), writes=(kA[hp % 2],))
        S.dma("g", kaug[0:64, hp % 2, 1, :], p.Cc[64:128, hp, :],
              reads=(bC,), writes=(kA[hp % 2],))
        S.op("v", lambda e, hp=hp: e.tensor_tensor(
            dsc[:], p.B[:, hp, :], p.Cc[:, hp, :], ALU.mult),
            reads=(bB, bC), writes=(bS2,))
        for sp in range(NSPAN):
            bank, bankB = p.pb[4], p.pbB[4]
            S.op("t", lambda e, hp=hp, sp=sp, bank=bank: e.matmul(
                bank[0:2, :], p.selt[:, hp * 2:(hp + 1) * 2],
                dsc[:, sp * SPAN:(sp + 1) * SPAN], start=True, stop=True),
                reads=(bS2, p.bC), writes=(bankB,))
            S.op("v", lambda e, hp=hp, sp=sp, bank=bank: e.tensor_scalar(
                dd2[:], bank[0:2, :], -0.5, hs2[0:2, hp, 0, 5:6],
                ALU.mult, ALU.add), reads=(bankB, bV2), writes=(bF2,))
            S.op("v", lambda e, hp=hp, sp=sp: e.tensor_scalar(
                rb2[:, sp * SPAN:(sp + 1) * SPAN], dd2[:], hs2[0:2, hp, 0, 3:4], None,
                ALU.min), reads=(bF2, bV2), writes=(bF2,))
        S.dma("s", cq8[:], rb2[:], reads=(bF2,), writes=(bW,))
        for wi, (st, en, oe) in enumerate(wins):
            Wd = en - st
            ntk = Wd // 128
            if wi % 2 == 0:
                cbank = [p.pb[4], p.pb[5]]
                cbankB = [p.pbB[4], p.pbB[5]]
            else:
                cbank = [p.ptf[0][:], p.ptf[1][:]]
                cbankB = [p.ptB[0], p.ptB[1]]
            S.dma("g", qaug[0:64, wi % 2, 0, 0:Wd],
                  p.B[0:64, hp, st:en], reads=(bB,), writes=(qA[wi % 2],))
            S.dma("g", qaug[0:64, wi % 2, 1, 0:Wd],
                  p.B[64:128, hp, st:en], reads=(bB,), writes=(qA[wi % 2],))
            S.dma("g", qaug[64:65, wi % 2, :, 0:Wd],
                  cq8[0:1, :, st:en], reads=(bW,), writes=(qA[wi % 2],))
            def emit_s(tkb, hp=hp, wi=wi, st=st, Wd=Wd):
                i2 = tkb % 2
                tk0 = st + tkb * 128
                S.group("t", [
                    (lambda e, hh=hh, i2=i2, tk0=tk0, hp=hp, wi=wi, Wd=Wd:
                     e.matmul(p.pb[i2 * 2 + hh][:, 0:Wd],
                              kaug[0:65, hp % 2, hh, tk0:tk0 + 128],
                              qaug[0:65, wi % 2, hh, 0:Wd],
                              start=True, stop=True))
                    for hh in range(2)],
                    reads=(kA[hp % 2], qA[wi % 2]),
                    writes=(p.pbB[i2 * 2], p.pbB[i2 * 2 + 1]))
                if Wd == 512:
                    S.op("s", lambda e, i2=i2: e.activation(
                        etile2[:, i2, :], p.pp[i2][:], AF.Exp, scale=0.125),
                        reads=(p.pbB[i2 * 2], p.pbB[i2 * 2 + 1]),
                        writes=(betw[i2 * 2], betw[i2 * 2 + 1]))
                else:
                    for hh in range(2):
                        S.op("s", lambda e, hh=hh, i2=i2, Wd=Wd: e.activation(
                            etile2[:, i2, hh * 512:hh * 512 + Wd],
                            p.pb[i2 * 2 + hh][:, 0:Wd], AF.Exp, scale=0.125),
                            reads=(p.pbB[i2 * 2 + hh],),
                            writes=(betw[i2 * 2 + hh],))
            def emit_pv(tkb, hh, hp=hp, st=st, Wd=Wd, ntk=ntk):
                tk0 = st + tkb * 128
                h = 2 * hp + hh
                esl = (tkb % 2) * 2 + hh
                et = etile2[:, tkb % 2, hh * 512:hh * 512 + Wd]
                S.op("t", lambda e, h=h, tk0=tk0, tkb=tkb, et=et, cb=cbank[hh],
                     ntk=ntk, Wd=Wd:
                     e.matmul(cb[0:65, 0:Wd], Dv4[:, tk0 // 128, h, :], et,
                              start=(tkb == 0), stop=(tkb == ntk - 1)),
                     reads=(bD, betw[esl]), writes=(cbankB[hh],))
            def epi(hp=hp, st=st, en=en, oe=oe, Wd=Wd, cbank=cbank, cbankB=cbankB):
                for hh in range(2):
                    S.op("v", lambda e, hh=hh, cb=cbank[hh], Wd=Wd: e.tensor_copy(
                        zzbuf[64:65, hh, 0:Wd], cb[64:65, 0:Wd]),
                        reads=(cbankB[hh],), writes=(bF2,))
                    S.op("v", lambda e, hh=hh, cb=cbank[hh], Wd=Wd: e.tensor_copy(
                        wctx[hh * 64:(hh + 1) * 64, 0:Wd], cb[0:64, 0:Wd]),
                        reads=(cbankB[hh],), writes=(bS2,))
                S.dma("s", zz2[0:2, 0, 0:Wd], zzbuf[64:65, :, 0:Wd], reads=(bF2,), writes=(bF2,))
                S.op("v", lambda e, Wd=Wd: e.reciprocal(zz2[:, 1, 0:Wd], zz2[:, 0, 0:Wd]),
                     reads=(bF2,), writes=(bF2,))
                S.op("v", lambda e, Wd=Wd: e.tensor_copy(zzrb[:, 0:Wd], zz2[:, 1, 0:Wd]),
                     reads=(bF2,), writes=(bF2,))
                bank, bankB = cbank[0], cbankB[0]
                S.op("t", lambda e, bank=bank, Wd=Wd: e.matmul(
                    bank[:, 0:Wd], p.sel2[:], zzrb[:, 0:Wd], start=True, stop=True),
                    reads=(bF2, p.bC), writes=(bankB,))
                S.op("v", lambda e, bank=bank, Wd=Wd: e.tensor_copy(rzbs[:, 0:Wd], bank[:, 0:Wd]),
                     reads=(bankB,), writes=(bF2,))
                S.op("v", lambda e, Wd=Wd: e.tensor_tensor(
                    wn[:, 0:Wd], wctx[:, 0:Wd], rzbs[:, 0:Wd], ALU.mult),
                    reads=(bS2, bF2), writes=(bS2,))
                if st == 0:
                    S.op("v", lambda e, hp=hp, Wd=Wd: e.tensor_copy(
                        p.E[:, hp, 0:Wd], wn[:, 0:Wd]), reads=(bS2,), writes=(bE2,))
                else:
                    nb = oe - st
                    S.op("v", lambda e, hp=hp, st=st, nb=nb: e.tensor_tensor(
                        p.E[:, hp, st:st + nb], p.E[:, hp, st:st + nb],
                        p.alpha1m[:, 0:nb], ALU.mult), reads=(bE2, p.bC), writes=(bE2,))
                    S.op("v", lambda e, nb=nb: e.tensor_tensor(
                        wctx[:, 0:nb], wn[:, 0:nb], p.alpha[:, 0:nb], ALU.mult),
                        reads=(bS2, p.bC), writes=(bS2,))
                    S.op("v", lambda e, hp=hp, st=st, nb=nb: e.tensor_tensor(
                        p.E[:, hp, st:st + nb], p.E[:, hp, st:st + nb], wctx[:, 0:nb],
                        ALU.add), reads=(bE2, bS2), writes=(bE2,))
                    S.op("v", lambda e, hp=hp, oe=oe, en=en, nb=nb: e.tensor_copy(
                        p.E[:, hp, oe:en], wn[:, nb:nb + (en - oe)]),
                        reads=(bS2,), writes=(bE2,))
            for tkb in range(ntk):
                emit_s(tkb)
                if tkb == 1 and p.pend is not None:
                    p.pend()
                    p.pend = None
                if tkb > 0:
                    for hh in range(2):
                        emit_pv(tkb - 1, hh)
            for hh in range(2):
                emit_pv(ntk - 1, hh)
            p.pend = epi

    if p.pend is not None:
        p.pend()
        p.pend = None
    load_w_fm(nc, S, p, io, io["wproj1"], W3, bW)
    S.dma("g", brow[:], io["bproj1"].rearrange("(o n) -> o n", o=1),
          reads=(bV2,), writes=(bV2,))
    ot2 = mem.at(OFF_B, [128, 2, D], F32, f"a_ot2{s}")
    bot = [S.fork(p.bB, f"a_ot{j}") for j in range(2)]
    bst2 = [S.fork(bV2, f"a_st2{t}") for t in range(NTILE)]
    bxt = [S.fork_multi([bF2, bxt[j]], f"a_xtD{j}") for j in range(2)]
    for ti in range(NTILE):
        j2 = ti % 2
        xs = xsl[j2]
        oslot = ot2[:, j2, :]
        banks = [p.pb[(ti % 2) * 2], p.pb[(ti % 2) * 2 + 1]]
        banksB = [p.pbB[(ti % 2) * 2], p.pbB[(ti % 2) * 2 + 1]]
        for nh2 in range(2):
            S.group("t", [
                (lambda e, nh2=nh2, bank=banks[nh2]: e.matmul(
                    bank[:], p.onesrow[:, 0:128], brow[0:1, nh2 * 512:(nh2 + 1) * 512],
                    start=True, stop=False))] + [
                (lambda e, k=k, ti=ti, nh2=nh2, bank=banks[nh2]:
                 e.matmul(bank[:], p.E[:, k, ti * 128:(ti + 1) * 128],
                          W3[:, k, nh2 * 512:(nh2 + 1) * 512],
                          start=False, stop=(k == KC - 1)))
                for k in range(KC)], reads=(bE2, bW, bV2, p.bC), writes=(banksB[nh2],))
        S.dma("q", xs, xh[base + ti * 128: base + (ti + 1) * 128, :], writes=(bxt[j2],))
        S.op("v", lambda e, ti=ti: e.scalar_tensor_tensor(
            nmD[:, ti:ti + 1], stats0[:, ti, 0:1], -1.0, rst0[:, ti:ti + 1],
            ALU.mult, ALU.mult), reads=(bst[ti],), writes=(bst[ti],))
        S.op("s", lambda e, ti=ti, xs=xs, oslot=oslot: e.activation(
            oslot, xs, AF.Identity, scale=rst0[:, ti:ti + 1],
            bias=nmD[:, ti:ti + 1]),
            reads=(bxt[j2], bst[ti]), writes=(bot[j2],))
        for nh2 in range(2):
            S.op("v", lambda e, nh2=nh2, oslot=oslot, bank=banks[nh2]: e.tensor_tensor(
                oslot[:, nh2 * 512:(nh2 + 1) * 512], oslot[:, nh2 * 512:(nh2 + 1) * 512],
                bank[:], ALU.add), reads=(banksB[nh2],), writes=(bot[j2],))
        ln_stats(nc, S, p, oslot, bot[j2], stats2[:, ti, :], bst2[ti],
                 bnt2[:, j2, :], bbn[j2])
        rstd(nc, S, p, stats2[:, ti, 1:2], rst2[:, ti:ti + 1], bst2[ti], bst2[ti])
        S.dma("s", oh[ti * 128:(ti + 1) * 128, :], oslot, reads=(bot[j2],))

    S.absorb(p.bB, bot)
    S.absorb(p.bA, kA + qA)
    S.dma("g", wm1a[:, :, 0:2048].rearrange("p k n -> p k n"),
          io["w11"][:, 0:2048].rearrange("(k p) n -> p k n", p=128), writes=(bA,))
    S.dma("g", wm1b[:, :, 0:2048],
          io["w11"][:, 2048:4096].rearrange("(k p) n -> p k n", p=128), writes=(bB,))
    S.dma("g", wm2c[:, :, :, :],
          io["w21"][0:2048, :].rearrange("(c two p) n -> p c two n", p=128, two=2),
          writes=(bC,))
    S.dma("g", wm2d[:, 0:16 * 1024].rearrange("p (c n) -> p c n", n=1024),
          io["w21"][2048:4096, :].rearrange("(c p) n -> p c n", p=128), writes=(bD,))
    b11c = vv[:, 160:192]
    S.dma("s", b11c, io["b11"].rearrange("(c p) -> p c", p=128),
          reads=(bV2,), writes=(bV2,), slow=True)
    bc_tile(io["ln2_g1"].rearrange("(o n) -> o n", o=1), bcs0[:])
    bc_tile(io["ln2_b1"].rearrange("(o n) -> o n", o=1), bcs1[:])
    S.dma("g", brow[:], io["b21"].rearrange("(o n) -> o n", o=1),
          reads=(bV2,), writes=(bV2,))

    yo2 = mem.at(OFF_W + 8, [128, 2, D], F32, f"a_yo2{s}")
    byo = [S.fork(bW, f"a_yo{j}") for j in range(2)]

    def wm1ap(k, n2):
        if n2 < 16:
            return wm1a[:, k, n2 * 128:(n2 + 1) * 128]
        return wm1b[:, k, (n2 - 16) * 128:(n2 - 15) * 128]

    def wm2ap(k2, nh2):
        if k2 < 16:
            return wm2c[:, k2 // 2, k2 % 2, :][:, nh2 * 512:(nh2 + 1) * 512]
        return wm2d[:, (k2 - 16) * 1024 + nh2 * 512:(k2 - 16) * 1024 + (nh2 + 1) * 512]

    for sp in range(NSPAN):
        for j in range(4):
            ti = sp * 4 + j
            j2 = ti % 2
            xs = xsl[j2]
            S.dma("q", xs, oh[ti * 128:(ti + 1) * 128, :], writes=(bxt[j2],))
            S.op("v", lambda e, ti=ti, j=j, xs=xs: e.tensor_scalar(
                p.Sb[:, j, :], xs, stats2[:, ti, 0:1], rst2[:, ti:ti + 1],
                ALU.subtract, ALU.mult), reads=(bxt[j2], bst2[ti]), writes=(bsb[j],))
            S.op("v", lambda e, j=j: e.tensor_tensor(
                p.Sb[:, j, :], p.Sb[:, j, :], bcs0[:], ALU.mult),
                reads=(bV2,), writes=(bsb[j],))
            S.op("v", lambda e, j=j: e.tensor_tensor(
                p.Sb[:, j, :], p.Sb[:, j, :], bcs1[:], ALU.add),
                reads=(bV2,), writes=(bsb[j],))
        for c in range(KC):
            tpose4(nc, S, p, [p.Sb[:, j, c * 128:(c + 1) * 128] for j in range(4)],
                   [bsb[j] for j in range(4)], x2fm[:, c, 0:512], bW, pt_idx=c % 2)
        for n2 in range(32):
            bi = n2 % 2
            bank, bankB = p.pb[bi], p.pbB[bi]
            S.group("t", [
                (lambda e, k=k, n2=n2, bank=bank:
                 e.matmul(bank[:], wm1ap(k, n2), x2fm[:, k, 0:512],
                          start=(k == 0), stop=(k == KC - 1)))
                for k in range(KC)], reads=(bA, bB, bW), writes=(bankB,))
            S.op("s", lambda e, n2=n2, bank=bank: e.activation(
                hsp[:, n2, :], bank[:], AF.Gelu, bias=b11c[:, n2:n2 + 1]),
                reads=(bankB, bV2), writes=(bE2,))
        for j in range(4):
            ti = sp * 4 + j
            banks = [p.pb[2 + (j % 2) * 2], p.pb[3 + (j % 2) * 2]]
            banksB = [p.pbB[2 + (j % 2) * 2], p.pbB[3 + (j % 2) * 2]]
            for nh2 in range(2):
                S.group("t", [
                    (lambda e, nh2=nh2, bank=banks[nh2]: e.matmul(
                        bank[:], p.onesrow[:, 0:128],
                        brow[0:1, nh2 * 512:(nh2 + 1) * 512], start=True, stop=False))] + [
                    (lambda e, k2=k2, j=j, nh2=nh2, bank=banks[nh2]:
                     e.matmul(bank[:], hsp[:, k2, j * 128:(j + 1) * 128],
                              wm2ap(k2, nh2), start=False, stop=(k2 == 31)))
                    for k2 in range(32)],
                    reads=(bE2, bC, bD, bV2, p.bC), writes=(banksB[nh2],))
            j2 = ti % 2
            xs = xsl[j2]
            yslot = yo2[:, j2, :]
            S.dma("q", xs, oh[ti * 128:(ti + 1) * 128, :], writes=(bxt[j2],))
            for nh2 in range(2):
                S.op("v", lambda e, nh2=nh2, xs=xs, yslot=yslot, bank=banks[nh2]:
                     e.tensor_tensor(
                         yslot[:, nh2 * 512:(nh2 + 1) * 512], bank[:],
                         xs[:, nh2 * 512:(nh2 + 1) * 512], ALU.add),
                     reads=(banksB[nh2], bxt[j2]), writes=(byo[j2],))
            ln_stats(nc, S, p, yslot, byo[j2], stats2[:, ti, :], bst2[ti],
                     bnt2[:, j2, :], bbn[j2])
            rstd(nc, S, p, stats2[:, ti, 1:2], rst2[:, ti:ti + 1], bst2[ti], bst2[ti])
            S.dma("q", xs, xh[base + ti * 128: base + (ti + 1) * 128, :],
                  writes=(bxt[j2],))
            S.op("v", lambda e, ti=ti, yslot=yslot: e.tensor_scalar(
                yslot, yslot, stats2[:, ti, 0:1], rst2[:, ti:ti + 1],
                ALU.subtract, ALU.mult), reads=(bst2[ti],), writes=(byo[j2],))
            S.op("v", lambda e, xs=xs, yslot=yslot: e.tensor_tensor(
                yslot, yslot, xs, ALU.add), reads=(bxt[j2],), writes=(byo[j2],))
            S.dma("s", xh[base + ti * 128: base + (ti + 1) * 128, :], yslot,
                  reads=(byo[j2],))
    S.barrier()


def phase_l1ffn(nc, S, io, p, s, xh):
    base = s * T
    mem = p.mem
    vv = mem.at(OFF_V, [128, 256], F32, f"m_vv{s}")
    stats0 = vv[:, 0:32].rearrange("p (t two) -> p t two", two=2)
    rst0 = vv[:, 32:48]
    A3 = vv[:, 48:64]
    lg2 = vv[:, 64:96].rearrange("p (j c) -> p j c", j=2)
    mx82 = vv[:, 216:232].rearrange("p (j c) -> p j c", j=2)
    ng2 = vv[:, 232:236].rearrange("p (j c) -> p j c", j=2)
    sc1 = vv[:, 90:96]
    be1c = vv[:, 96:128]
    bg_b = vv[:, 128:144]
    bnt = vv[:, 144:156]
    bnt2 = vv[:, 144:168].rearrange("p (j c) -> p j c", j=2)
    gate_all = mem.at(OFF_V + 1, [128, NTILE, 16], F32, f"m_ga{s}")
    gscl = mem.at(OFF_V + 2, [128, NTILE, 16], F32, f"m_gs{s}")
    nsc_b = mem.at(OFF_V + 3, [128, D], BF16, f"m_nscb{s}")
    be2_all = mem.at(OFF_V + 5, [16, D], BF16, f"m_be2{s}")
    wgs = mem.at(OFF_V + 7, [128, KC, 16], BF16, f"m_wgs{s}")
    lgb = mem.at(OFF_V + 7.25, [128, 32], BF16, f"m_lgb{s}")
    gate_fm = mem.at(OFF_S, [16, T], BF16, f"m_gfm{s}")
    abuf = mem.at(OFF_S, [128, 512], BF16, f"m_ab{s}")
    pbuf = mem.at(OFF_S + 1, [128, 512], BF16, f"m_pb{s}")
    bbuf = mem.at(OFF_S + 2, [128, 512], BF16, f"m_bb{s}")
    brow2 = mem.at(OFF_S + 4, [1, 2 * D], BF16, f"m_br2{s}")
    xtile = mem.at(OFF_F, [128, D], F32, f"m_xt{s}")
    work = mem.at(OFF_F + 4, [128, D], F32, f"m_wk{s}")
    yac0 = mem.at(OFF_B, [128, KC, D], F32, f"m_y0{s}")
    yac1 = mem.at(OFF_C, [128, KC, D], F32, f"m_y1{s}")
    sfm = mem.at(OFF_D, [128, KC, T], BF16, f"m_sfm{s}")
    we1v = p.E[:]
    we2v = p.Wb[:]
    bA, bB, bC, bD, bE2 = p.bA, p.bB, p.bCc, p.bD, p.bE
    bW, bV2, bF2, bS2 = p.bW, p.bV, p.bF, p.bS

    def yac(ti):
        return (yac0[:, ti, :] if ti < KC else yac1[:, ti - KC, :])

    S.dma("g", wgs[:, :, :], io["wg"].rearrange("(k p) n -> p k n", p=128),
          writes=(bV2,))
    S.dma("g", brow2[0:1, 0:16], io["bg"].rearrange("(o n) -> o n", o=1),
          reads=(bS2,), writes=(bS2,))
    bcast_row(nc, S, p, brow2[:, 0:16], bS2, bg_b, bV2, 16, bank_i=4)
    S.dma("g", be2_all[:], io["be2"], writes=(bV2,))
    S.dma("g", brow2[0:1, 0:D], io["nsc"].rearrange("(o n) -> o n", o=1),
          reads=(bS2,), writes=(bS2,))
    bcast_row(nc, S, p, brow2[:, 0:D], bS2, nsc_b[:], bV2, D, bank_i=4)

    bxt = [S.fork(bF2, f"m_xt{j}") for j in range(2)]
    bst = [S.fork(bV2, f"m_st{t}") for t in range(NTILE)]
    bbn = [S.fork(bV2, f"m_bn{j}") for j in range(2)]
    bsb = [S.fork(bS2, f"m_sb{j}") for j in range(4)]
    bnt2 = vv[:, 192:216].rearrange("p (j c) -> p j c", j=2)
    xsl = [xtile[:], work[:]]
    for sp in range(NSPAN):
        for j in range(4):
            ti = sp * 4 + j
            j2 = ti % 2
            xs = xsl[j2]
            S.dma("q", xs, xh[base + ti * 128: base + (ti + 1) * 128, :],
                  writes=(bxt[j2],))
            ln_stats(nc, S, p, xs, bxt[j2], stats0[:, ti, :], bst[ti],
                     bnt2[:, j2, :], bbn[j2])
            rstd(nc, S, p, stats0[:, ti, 1:2], rst0[:, ti:ti + 1], bst[ti], bst[ti])
            S.op("v", lambda e, ti=ti, j2=j2: e.tensor_tensor(
                sc1[:, 3 * j2:3 * j2 + 1], rst0[:, ti:ti + 1], rst0[:, ti:ti + 1],
                ALU.mult), reads=(bst[ti],), writes=(bbn[j2],))
            S.op("v", lambda e, ti=ti, j2=j2: e.tensor_tensor(
                sc1[:, 3 * j2 + 1:3 * j2 + 2], sc1[:, 3 * j2:3 * j2 + 1],
                stats0[:, ti, 1:2], ALU.mult), reads=(bst[ti],), writes=(bbn[j2],))
            rstd(nc, S, p, sc1[:, 3 * j2 + 1:3 * j2 + 2], sc1[:, 3 * j2 + 2:3 * j2 + 3],
                 bbn[j2], bbn[j2])
            S.op("v", lambda e, ti=ti, j2=j2: e.tensor_tensor(
                A3[:, ti:ti + 1], rst0[:, ti:ti + 1], sc1[:, 3 * j2 + 2:3 * j2 + 3],
                ALU.mult), reads=(bbn[j2], bst[ti]), writes=(bst[ti],))
            S.op("v", lambda e, ti=ti, xs=xs: e.tensor_scalar(
                yac(ti), xs, stats0[:, ti, 0:1], rst0[:, ti:ti + 1],
                ALU.subtract, ALU.mult),
                reads=(bxt[j2], bst[ti]), writes=(bB if ti < KC else bC,))
            S.op("v", lambda e, ti=ti, j=j, xs=xs: e.tensor_scalar(
                p.Sb[:, j, :], xs, stats0[:, ti, 0:1], A3[:, ti:ti + 1],
                ALU.subtract, ALU.mult), reads=(bxt[j2], bst[ti]), writes=(bsb[j],))
            S.op("v", lambda e, j=j: e.tensor_tensor(
                p.Sb[:, j, :], p.Sb[:, j, :], nsc_b[:], ALU.mult),
                reads=(bV2,), writes=(bsb[j],))
        for c in range(KC):
            tpose4(nc, S, p, [p.Sb[:, j, c * 128:(c + 1) * 128] for j in range(4)],
                   [bsb[j] for j in range(4)], p.A[:, c, sp * SPAN:(sp + 1) * SPAN],
                   bA, pt_idx=c % 2)

    bgv = [S.fork(bV2, f"m_gv{j}") for j in range(2)]
    for ti in range(NTILE):
        bank, bankB = p.pb[ti % 2], p.pbB[ti % 2]
        S.group("t", [
            (lambda e, k=k, ti=ti, bank=bank:
             e.matmul(bank[:, 0:16], p.A[:, k, ti * 128:(ti + 1) * 128],
                      wgs[:, k, :], start=(k == 0), stop=(k == KC - 1)))
            for k in range(KC)], reads=(bA, bV2), writes=(bankB,))
        j2 = ti % 2
        gv = bgv[j2]
        lg = lg2[:, j2, :]
        mx8 = mx82[:, j2, :]
        negm1 = ng2[:, j2, 0:1]
        zrec = ng2[:, j2, 1:2]
        S.op("v", lambda e, lg=lg, bank=bank: e.tensor_tensor(
            lg, bank[:, 0:16], bg_b, ALU.add),
            reads=(bankB, bV2), writes=(gv,))
        S.op("v", lambda e, lg=lg, mx8=mx8: e.max(out=mx8, in_=lg),
             reads=(), writes=(gv,))
        S.op("v", lambda e, mx8=mx8, negm1=negm1: e.tensor_scalar_mul(
            negm1, mx8[:, 0:1], -1.0), reads=(), writes=(gv,))
        S.op("s", lambda e, ti=ti, lg=lg, negm1=negm1: e.activation(
            gate_all[:, ti, :], lg, AF.Exp, bias=negm1),
            reads=(gv,), writes=(gv,))
        S.op("v", lambda e, ti=ti, lg=lg, mx8=mx8: e.tensor_scalar(
            gscl[:, ti, :], lg, mx8[:, 3:4], None, ALU.is_ge),
            reads=(), writes=(gv,))
        S.op("v", lambda e, ti=ti: e.tensor_tensor(
            gate_all[:, ti, :], gate_all[:, ti, :], gscl[:, ti, :], ALU.mult),
            reads=(), writes=(gv,))
        S.op("v", lambda e, ti=ti, zrec=zrec: e.tensor_reduce(
            zrec, gate_all[:, ti, :], AX.X, ALU.add),
            reads=(), writes=(gv,))
        S.op("v", lambda e, zrec=zrec: e.reciprocal(zrec, zrec),
             reads=(), writes=(gv,))
        S.op("v", lambda e, ti=ti, zrec=zrec: e.tensor_scalar(
            gate_all[:, ti, :], gate_all[:, ti, :], zrec, None, ALU.mult),
            reads=(), writes=(gv,))
        S.op("v", lambda e, ti=ti: e.tensor_scalar_mul(
            gscl[:, ti, :], gate_all[:, ti, :], 1.0 / ALPHA_SILU),
            reads=(), writes=(gv,))
        S.op("v", lambda e, ti=ti, j2=j2: e.tensor_copy(
            lgb[:, j2 * 16:(j2 + 1) * 16], gate_all[:, ti, :]),
            reads=(), writes=(gv,))
        S.op("t", lambda e, ti=ti, j2=j2: e.transpose(
            p.pt[j2][0:16, 0:128], lgb[:, j2 * 16:(j2 + 1) * 16], p.ident[:]),
            reads=(gv, p.bC), writes=(p.ptB[j2],))
        S.op("v", lambda e, ti=ti, j2=j2: e.tensor_copy(
            gate_fm[:, ti * 128:(ti + 1) * 128], p.pt[j2][0:16, 0:128]),
            reads=(p.ptB[j2],), writes=(bS2,))
        banks = [p.pb[3 + (ti % 2)], p.pb[5 - (ti % 2)]]
        banksB = [p.pbB[3 + (ti % 2)], p.pbB[5 - (ti % 2)]]
        for nh2 in range(2):
            S.op("t", lambda e, ti=ti, nh2=nh2, bank=banks[nh2]: e.matmul(
                bank[:], gate_fm[:, ti * 128:(ti + 1) * 128],
                be2_all[:, nh2 * 512:(nh2 + 1) * 512], start=True, stop=True),
                reads=(bS2, bV2), writes=(banksB[nh2],))
            S.op("v", lambda e, ti=ti, nh2=nh2, bank=banks[nh2]: e.tensor_tensor(
                yac(ti)[:, nh2 * 512:(nh2 + 1) * 512],
                yac(ti)[:, nh2 * 512:(nh2 + 1) * 512], bank[:], ALU.add),
                reads=(banksB[nh2],), writes=(bB if ti < KC else bC,))

    for ex in range(NEXP):
        S.dma("g", we1v[:, :, 0:2048],
              io["we1"][ex].rearrange("(k p) n -> p k n", p=128), writes=(bE2,))
        S.dma("g", we2v[:, :, 0:1024],
              io["we2"][ex].rearrange("(k p) n -> p k n", p=128), writes=(bW,))
        S.dma("g", brow2[:], io["be1"][ex].rearrange("(o n) -> o n", o=1),
              reads=(bS2,), writes=(bS2,))
        emits = []
        for cc in range(16):
            emits.append(lambda e, cc=cc: e.transpose(
                p.pt[0][:, 2 * cc:2 * cc + 1], brow2[0:1, cc * 128:(cc + 1) * 128],
                p.ident[0:1, 0:1]))
        S.group("t", emits, reads=(bS2, p.bC), writes=(p.ptB[0],))
        S.op("v", lambda e: e.tensor_copy(
            be1c[:, 0:16].rearrange("p c -> p c 1" if False else "p (c o) -> p c o", o=1),
            p.pt[0][:, 0:32].rearrange("p (c two) -> p c two", two=2)[:, :, 0:1]),
            reads=(p.ptB[0],), writes=(bV2,))
        for ic in range(KC):
            for sp in range(NSPAN):
                ba, baB = p.pb[sp % 2], p.pbB[sp % 2]
                bb_, bbB = p.pb[2 + sp % 2], p.pbB[2 + sp % 2]
                S.group("t", [
                    (lambda e, k=k, ic=ic, sp=sp, ba=ba:
                     e.matmul(ba[:], we1v[:, k, ic * 128:(ic + 1) * 128],
                              p.A[:, k, sp * SPAN:(sp + 1) * SPAN],
                              start=(k == 0), stop=(k == KC - 1)))
                    for k in range(KC)], reads=(bE2, bA), writes=(baB,))
                S.group("t", [
                    (lambda e, k=k, ic=ic, sp=sp, bb_=bb_:
                     e.matmul(bb_[:], we1v[:, k, 1024 + ic * 128:1024 + (ic + 1) * 128],
                              p.A[:, k, sp * SPAN:(sp + 1) * SPAN],
                              start=(k == 0), stop=(k == KC - 1)))
                    for k in range(KC)], reads=(bE2, bA), writes=(bbB,))
                S.op("v", lambda e, ic=ic, ba=ba: e.tensor_scalar(
                    abuf[:], ba[:], be1c[:, ic:ic + 1], LIMIT, ALU.add, ALU.min),
                    reads=(baB, bV2), writes=(bS2,))
                S.op("s", lambda e: e.activation(pbuf[:], abuf[:], AF.Silu,
                                                 scale=ALPHA_SILU),
                     reads=(bS2,), writes=(bS2,))
                S.op("v", lambda e, ic=ic, bb_=bb_: e.tensor_scalar(
                    bbuf[:], bb_[:], be1c[:, 8 + ic:8 + ic + 1], -LIMIT,
                    ALU.add, ALU.max), reads=(bbB, bV2), writes=(bS2,))
                S.op("v", lambda e: e.tensor_scalar(
                    bbuf[:], bbuf[:], LIMIT, 1.0, ALU.min, ALU.add),
                    reads=(bS2,), writes=(bS2,))
                S.op("v", lambda e, ic=ic, sp=sp: e.tensor_tensor(
                    sfm[:, ic, sp * SPAN:(sp + 1) * SPAN], pbuf[:], bbuf[:], ALU.mult),
                    reads=(bS2,), writes=(bD,))
        for ti in range(NTILE):
            banks = [p.pb[4], p.pb[5]]
            banksB = [p.pbB[4], p.pbB[5]]
            for nh2 in range(2):
                S.group("t", [
                    (lambda e, ic=ic, ti=ti, nh2=nh2, bank=banks[nh2]:
                     e.matmul(bank[:], sfm[:, ic, ti * 128:(ti + 1) * 128],
                              we2v[:, ic, nh2 * 512:(nh2 + 1) * 512],
                              start=(ic == 0), stop=(ic == KC - 1)))
                    for ic in range(KC)], reads=(bD, bW), writes=(banksB[nh2],))
            for nh2 in range(2):
                S.op("v", lambda e, ti=ti, ex=ex, nh2=nh2, bank=banks[nh2]:
                     e.scalar_tensor_tensor(
                         yac(ti)[:, nh2 * 512:(nh2 + 1) * 512], bank[:],
                         gscl[:, ti, ex:ex + 1],
                         yac(ti)[:, nh2 * 512:(nh2 + 1) * 512], ALU.mult, ALU.add),
                     reads=(banksB[nh2], bV2), writes=(bB if ti < KC else bC,))

    xt2 = mem.at(OFF_A, [128, 2, D], F32, f"m_xt2{s}")
    wk2 = mem.at(OFF_A + 8, [128, 2, D], F32, f"m_wk2{s}")
    bxt = [S.fork(bA, f"m_bxt{j}") for j in range(2)]
    bwk = [S.fork(bA, f"m_bwk{j}") for j in range(2)]
    bstt = [S.fork(bV2, f"m_bst{t}") for t in range(NTILE)]
    bbnt = [S.fork(bV2, f"m_bbnt{j}") for j in range(2)]
    for ti in range(NTILE):
        j = ti % 2
        yb = bB if ti < KC else bC
        ln_stats(nc, S, p, yac(ti), yb, stats0[:, ti, :], bstt[ti],
                 bnt2[:, j, :], bbnt[j])
        rstd(nc, S, p, stats0[:, ti, 1:2], rst0[:, ti:ti + 1], bstt[ti], bstt[ti])
        S.dma("q", xt2[:, j, :], xh[base + ti * 128: base + (ti + 1) * 128, :],
              writes=(bxt[j],))
        S.op("v", lambda e, ti=ti, j=j: e.tensor_scalar(
            wk2[:, j, :], yac(ti), stats0[:, ti, 0:1], rst0[:, ti:ti + 1],
            ALU.subtract, ALU.mult), reads=(bstt[ti], bB if ti < KC else bC),
            writes=(bwk[j],))
        S.op("v", lambda e, j=j: e.tensor_tensor(wk2[:, j, :], wk2[:, j, :],
                                                 xt2[:, j, :], ALU.add),
             reads=(bxt[j],), writes=(bwk[j],))
        S.dma("s", io["y_out"][base + ti * 128: base + (ti + 1) * 128, :],
              wk2[:, j, :], reads=(bwk[j],))
    S.barrier()


def final_store(nc, S, io, p, s, xh):
    base = s * T
    mem = p.mem
    tb = mem.at(OFF_F, [128, 2, D], F32, f"fst{s}")
    bt = [Buf(f"fs{j}") for j in range(2)]
    for ti in range(NTILE):
        j = ti % 2
        S.dma("q", tb[:, j, :], xh[base + ti * 128: base + (ti + 1) * 128, :],
              writes=(bt[j],))
        S.dma("s", io["y_out"][base + ti * 128: base + (ti + 1) * 128, :],
              tb[:, j, :], reads=(bt[j],))
    S.barrier()



def kernel(**inputs):
    w = prep_weights(inputs)
    c = host_constants()
    x = np.ascontiguousarray(np.asarray(inputs["L0_attn_x"], dtype=np.float32))
    nc = build_program()
    in_maps = []
    for core in range(NCORE):
        m = {"x_in": np.ascontiguousarray(
            x[core * NSEQ:(core + 1) * NSEQ].reshape(NSEQ * T, D))}
        for k, v in w.items():
            m[k] = np.ascontiguousarray(np.asarray(v, dtype=np.float32))
        for k, v in c.items():
            m[k] = np.ascontiguousarray(v)
        in_maps.append(m)
    res = run_bass_kernel_spmd(nc, in_maps, core_ids=list(range(NCORE)))
    out = np.empty((NCORE * NSEQ, T, D), dtype=np.float32)
    for core in range(NCORE):
        out[core * NSEQ:(core + 1) * NSEQ] = res.results[core]["y_out"].reshape(NSEQ, T, D)
    return out


# version 19
# speedup vs baseline: 1.0396x; 1.0396x over previous
import numpy as np
import ml_dtypes
from contextlib import ExitStack

import concourse.bass as bass
import concourse.mybir as mybir
from concourse.bass_utils import run_bass_kernel_spmd

F32 = mybir.dt.float32
BF16 = mybir.dt.bfloat16
I32 = mybir.dt.int32
AF = mybir.ActivationFunctionType
ALU = mybir.AluOpType
AX = mybir.AxisListType

D = 1024
T = 2048
NSEQ = 2
NCORE = 8
NH = 16
HD = 64
KC = 8
NTILE = 16
SPAN = 512
NSPAN = 4
EPS = 1e-5
MAGIC = 12582912.0
C256 = 256.499969482421875
NEXP = 16
ALPHA_SILU = 1.702
LIMIT = 7.0

SUBBLOCKS = 4

KB = 1024


def host_constants():
    c = {}
    c["ident"] = np.eye(128, dtype=ml_dtypes.bfloat16)
    P = np.zeros((128, 128), dtype=ml_dtypes.bfloat16)
    for m in range(128):
        P[(m // 64) * 64 + ((m % 64) + 32) % 64, m] = 1
    c["perm64"] = P
    inv_freq = (1.0 / 10000.0 ** (np.arange(0, HD, 2, dtype=np.float32) / HD)).astype(np.float32)
    t = np.arange(T, dtype=np.float32)
    freqs = t[:, None] * inv_freq[None, :]
    emb = np.concatenate([np.sin(freqs), np.cos(freqs)], axis=-1).astype(np.float32)
    sinv = emb[:, 0::2].T
    cosv = emb[:, 1::2].T
    C = np.empty((128, T), dtype=np.float32)
    Sg = np.empty((128, T), dtype=np.float32)
    for b in range(4):
        C[b * 32:(b + 1) * 32] = cosv
        Sg[b * 32:(b + 1) * 32] = (-sinv) if (b % 2 == 0) else sinv
    c["ropeC"] = C.astype(ml_dtypes.bfloat16)
    c["ropeS"] = Sg.astype(ml_dtypes.bfloat16)
    a = np.linspace(0.0, 1.0, 256).astype(np.float32)
    c["alpha"] = np.tile(a, (128, 1)).astype(ml_dtypes.bfloat16)
    c["alpha1m"] = np.tile((1.0 - a).astype(np.float32), (128, 1)).astype(ml_dtypes.bfloat16)
    SEL2 = np.zeros((2, 128), dtype=ml_dtypes.bfloat16)
    SEL2[0, 0:64] = 1
    SEL2[1, 64:128] = 1
    c["sel2"] = SEL2
    SELT = np.zeros((128, KC * 2), dtype=ml_dtypes.bfloat16)
    for hp in range(KC):
        for p_ in range(128):
            SELT[p_, hp * 2 + (p_ // 64)] = 1
    c["selt"] = SELT
    c["onesrow"] = np.ones((1, 128), dtype=ml_dtypes.bfloat16)
    return c


def rope_perm_cols():
    p = []
    for h in range(NH):
        p += [h * HD + 2 * i for i in range(32)]
        p += [h * HD + 2 * i + 1 for i in range(32)]
    return np.array(p, dtype=np.int64)


def prep_weights(inp):
    w = {}
    qk = rope_perm_cols()
    wq0 = np.ascontiguousarray(np.asarray(inp["L0_attn_qkv_w"], dtype=np.float32).T)
    perm = np.concatenate([qk, D + qk, 2 * D + np.arange(D)])
    w["w_qkv0"] = np.ascontiguousarray(wq0[:, perm])
    w["b_qkv0"] = np.ascontiguousarray(np.asarray(inp["L0_attn_qkv_b"], dtype=np.float32)[perm])
    w["w_out0"] = np.ascontiguousarray(np.asarray(inp["L0_attn_out_w"], dtype=np.float32).T)
    w["b_out0"] = np.asarray(inp["L0_attn_out_b"], dtype=np.float32)
    w["rhs0"] = np.asarray(inp["L0_ffn_rhs"], dtype=np.float32)
    for k in ("ln1_g", "ln1_b", "bq", "bk", "bv", "bproj", "ln2_g", "ln2_b", "b1", "b2"):
        w[k + "1"] = np.asarray(inp["L1_attn_" + k], dtype=np.float32)
    for k in ("wq", "wk", "wv", "wproj", "w1", "w2"):
        w[k + "1"] = np.ascontiguousarray(np.asarray(inp["L1_attn_" + k], dtype=np.float32))
    pe = np.concatenate([2 * np.arange(D), 2 * np.arange(D) + 1])
    w["nsc"] = np.asarray(inp["L1_ffn_norm_scale"], dtype=np.float32)
    w["wg"] = np.ascontiguousarray(np.asarray(inp["L1_ffn_wg"], dtype=np.float32))
    w["bg"] = np.asarray(inp["L1_ffn_bg"], dtype=np.float32)
    w["we1"] = np.ascontiguousarray(np.asarray(inp["L1_ffn_w1"], dtype=np.float32)[:, :, pe])
    w["be1"] = np.ascontiguousarray(np.asarray(inp["L1_ffn_b1"], dtype=np.float32)[:, pe])
    w["we2"] = np.ascontiguousarray(np.asarray(inp["L1_ffn_w2"], dtype=np.float32))
    w["be2"] = np.asarray(inp["L1_ffn_b2"], dtype=np.float32)
    return w


class Buf:
    __slots__ = ("w", "r", "name")
    def __init__(self, name=""):
        self.w = None
        self.r = []
        self.name = name


class Sched:
    ENGS = ("t", "v", "s", "g", "q")

    def __init__(self, nc, ctx):
        self.nc = nc
        self.thunks = {e: [] for e in self.ENGS}
        self.sems = {}
        names = ["ct", "cv", "cs", "cg"]
        for q in ("dq", "ds", "dg"):
            names += [f"{q}{i}" for i in range(4)]
        for sn in names:
            self.sems[sn] = ctx.enter_context(nc.semaphore(sn))
        self.counts = {sn: 0 for sn in self.sems}
        self.marks = {e: {sn: 0 for sn in self.sems} for e in self.ENGS}
        self.csem = {"t": "ct", "v": "cv", "s": "cs", "g": "cg"}
        self.qbase = {"q": "dq", "s": "ds", "g": "dg"}
        self.qidx = {"q": 0, "s": 0, "g": 0}

    def _wait(self, e, tokens):
        need = {}
        for tk in tokens:
            if tk is None:
                continue
            sn, v = tk
            if self.marks[e][sn] < v and need.get(sn, 0) < v:
                need[sn] = v
        for sn in sorted(need):
            v = need[sn]
            self.marks[e][sn] = v
            sem = self.sems[sn]
            self.thunks[e].append(lambda eng, sem=sem, v=v: eng.wait_ge(sem, v))

    def _deps(self, reads, writes):
        tks = []
        for b in reads:
            if b.w is not None:
                tks.append(b.w)
        for b in writes:
            if b.w is not None:
                tks.append(b.w)
            tks.extend(b.r)
        return tks

    def _record(self, tok, reads, writes):
        for b in reads:
            b.r.append(tok)
        for b in writes:
            b.w = tok
            b.r = []

    def op(self, e, emit, reads=(), writes=()):
        return self.group(e, [emit], reads, writes)

    def group(self, e, emits, reads=(), writes=()):
        self._wait(e, self._deps(reads, writes))
        sn = self.csem[e]
        self.counts[sn] += 1
        v = self.counts[sn]
        sem = self.sems[sn]
        for em in emits[:-1]:
            self.thunks[e].append(em)
        self.thunks[e].append(lambda eng, em=emits[-1], sem=sem: em(eng).then_inc(sem, 1))
        tok = (sn, v)
        self._record(tok, reads, writes)
        return tok

    def dma(self, e, dst_ap, src_ap, reads=(), writes=(), slow=False):
        sn = f"{self.qbase[e]}{self.qidx[e] % 4}"
        self.qidx[e] += 1
        deps = self._deps(reads, writes)
        if self.counts[sn] > 0:
            deps = deps + [(sn, self.counts[sn])]
        self._wait(e, deps)
        self.counts[sn] += 16
        v = self.counts[sn]
        sem = self.sems[sn]
        self.thunks[e].append(
            lambda eng, d=dst_ap, s=src_ap, sem=sem, slow=slow: eng.dma_start(
                d, s, allow_slow_non_contiguous=slow).then_inc(sem, 16))
        tok = (sn, v)
        self._record(tok, reads, writes)
        return tok

    def fork(self, parent, name=""):
        b = Buf(name)
        b.w = parent.w
        b.r = list(parent.r)
        return b

    def absorb(self, parent, children):
        for c in children:
            if c.w is not None:
                parent.r.append(c.w)
            parent.r.extend(c.r)

    def fork_multi(self, parents, name=""):
        b = Buf(name)
        b.w = None
        r = []
        for pa in parents:
            if pa.w is not None:
                r.append(pa.w)
            r.extend(pa.r)
        b.r = r
        return b

    def barrier(self):
        toks = [(sn, self.counts[sn]) for sn in self.sems if self.counts[sn] > 0]
        for e in self.ENGS:
            self._wait(e, toks)

    def replay(self, block):
        def runner(thunks):
            def f(eng):
                for th in thunks:
                    th(eng)
            return f
        block.sync(runner(self.thunks["q"]))
        block.gpsimd(runner(self.thunks["g"]))
        block.tensor(runner(self.thunks["t"]))
        block.vector(runner(self.thunks["v"]))
        block.scalar(runner(self.thunks["s"]))


class Mem:
    def __init__(self, nc):
        self.nc = nc
        self.base = (nc.sbuf_base + 63) & ~63
        self.top = nc.sbuf_top
        assert self.base + OFF_END * KB <= self.top, \
            f"map too big: {self.base}+{OFF_END}K > {self.top}"
        self.n = 0

    def at(self, off_kb, shape, dtype, name=None):
        self.n += 1
        name = name or f"mv{self.n}"
        t = self.nc.alloc_sbuf_tensor_at(name, list(shape), dtype,
                                         offset=self.base + int(off_kb * KB))
        return t

OFF_CONST = 0
OFF_A = 4
OFF_B = 36
OFF_C = 68
OFF_D = 100
OFF_E = 133
OFF_W = 165
OFF_F = 181
OFF_S = 189
OFF_V = 197
OFF_END = 205


class P:
    pass


def build_program():
    nc = bass.Bass("TRN2", target_bir_lowering=False)
    io = {}
    io["x_in"] = nc.dram_tensor("x_in", [NSEQ * T, D], F32, kind="ExternalInput").ap()
    io["y_out"] = nc.dram_tensor("y_out", [NSEQ * T, D], F32, kind="ExternalOutput").ap()
    xh = nc.dram_tensor("x_hbm", [NSEQ * T, D], F32, kind="Internal").ap()
    oh = nc.dram_tensor("out_hbm", [T, D], F32, kind="Internal").ap()
    wspec = [("w_qkv0", [D, 3 * D]), ("b_qkv0", [3 * D]), ("w_out0", [D, D]), ("b_out0", [D]),
             ("rhs0", [D, D]),
             ("ln1_g1", [D]), ("ln1_b1", [D]), ("wq1", [D, D]), ("bq1", [D]),
             ("wk1", [D, D]), ("bk1", [D]), ("wv1", [D, D]), ("bv1", [D]),
             ("wproj1", [D, D]), ("bproj1", [D]), ("ln2_g1", [D]), ("ln2_b1", [D]),
             ("w11", [D, 4 * D]), ("b11", [4 * D]), ("w21", [4 * D, D]), ("b21", [D]),
             ("nsc", [D]), ("wg", [D, NEXP]), ("bg", [NEXP]),
             ("we1", [NEXP, D, 2 * D]), ("be1", [NEXP, 2 * D]),
             ("we2", [NEXP, D, D]), ("be2", [NEXP, D])]
    for name, shape in wspec:
        io[name] = nc.dram_tensor(name, shape, F32, kind="ExternalInput").ap()
    cspec = [("ident", [128, 128]), ("perm64", [128, 128]), ("ropeC", [128, T]),
             ("ropeS", [128, T]), ("alpha", [128, 256]), ("alpha1m", [128, 256]),
             ("sel2", [2, 128]), ("selt", [128, KC * 2]), ("onesrow", [1, 128])]
    for name, shape in cspec:
        io[name] = nc.dram_tensor(name, shape, BF16, kind="ExternalInput").ap()

    with ExitStack() as ctx:
        S = Sched(nc, ctx)
        build_body(nc, S, io, xh, oh, ctx)
        block = ctx.enter_context(nc.Block())
        S.replay(block)
    return nc


def build_body(nc, S, io, xh, oh, ctx):
    mem = Mem(nc)
    p = P()
    p.mem = mem
    p.ident = mem.at(0, [128, 128], BF16, "identc")
    p.perm64 = mem.at(0.25, [128, 128], BF16, "permc")
    p.selt = mem.at(0.5, [128, KC * 2], BF16, "seltc")
    p.onesrow = mem.at(0.75, [1, 128], BF16, "onesrc")
    p.alpha = mem.at(1.0, [128, 256], BF16, "alphac")
    p.alpha1m = mem.at(1.5, [128, 256], BF16, "alpha1mc")
    p.sel2 = mem.at(2.0, [2, 128], BF16, "sel2c")
    p.epsv = mem.at(3.5, [128, 4], F32, "epsv")
    p.bC = Buf("consts")
    for nm in ("ident", "perm64", "selt", "onesrow", "alpha", "alpha1m", "sel2"):
        S.dma("q", getattr(p, nm)[:], io[nm], writes=(p.bC,))
    S.op("v", lambda e: e.memset(p.epsv[:, 0:1], EPS), writes=(p.bC,))
    S.op("v", lambda e: e.memset(p.epsv[:, 1:2], MAGIC), writes=(p.bC,))

    p.bA, p.bB, p.bCc, p.bD, p.bE = Buf("A"), Buf("B"), Buf("C"), Buf("D"), Buf("E")
    p.bW, p.bF, p.bS, p.bV = Buf("W"), Buf("F"), Buf("S"), Buf("V")
    p.A = mem.at(OFF_A, [128, KC, T], BF16, "A_fm")
    p.B = mem.at(OFF_B, [128, KC, T], BF16, "B_fm")
    p.Cc = mem.at(OFF_C, [128, KC, T], BF16, "C_fm")
    p.Dv = mem.at(OFF_D, [128, NTILE, 65 * NH], BF16, "D_vtm")
    p.E = mem.at(OFF_E, [128, KC, T], BF16, "E_fm")
    p.Wb = mem.at(OFF_W, [128, KC, 1024], BF16, "W_small")
    p.Fa = mem.at(OFF_F, [128, 2, D], F32, "F_f32")
    p.Sb = mem.at(OFF_S, [128, 4, D], BF16, "S_tok")
    p.pp = [ctx.enter_context(nc.psum_tensor(f"psp{i}", [128, 1024], F32)) for i in range(3)]
    p.pb = [p.pp[i // 2][:, (i % 2) * 512:(i % 2 + 1) * 512] for i in range(6)]
    p.ptf = [ctx.enter_context(nc.psum_tensor(f"pstp{i}", [128, 512], F32)) for i in range(2)]
    p.pt = [t[:].bitcast(BF16) for t in p.ptf]
    p.pbB = [Buf(f"psb{i}") for i in range(6)]
    p.pend = None
    p.ptB = [Buf(f"pst{i}") for i in range(2)]

    for s in range(NSEQ):
        phase_l0attn(nc, S, io, p, s, xh)
        if SUBBLOCKS >= 2:
            phase_l0ffn(nc, S, io, p, s, xh)
        if SUBBLOCKS >= 3:
            phase_l1attn(nc, S, io, p, s, xh, oh)
        if SUBBLOCKS >= 4:
            phase_l1ffn(nc, S, io, p, s, xh)
        else:
            final_store(nc, S, io, p, s, xh)
    S.barrier()



def ln_stats(nc, S, p, xt_ap, xt_buf, st_ap, st_buf, bnt_ap, bnt_buf):
    S.group("v", [
        lambda e: e.bn_stats(bnt_ap[:, 0:6], xt_ap[:, 0:512]),
        lambda e: e.bn_stats(bnt_ap[:, 6:12], xt_ap[:, 512:1024]),
    ], reads=(xt_buf,), writes=(bnt_buf,))
    S.op("v", lambda e: e.bn_aggr(st_ap, bnt_ap), reads=(bnt_buf,), writes=(st_buf,))


def rstd(nc, S, p, var_ap, r_ap, rbuf, vbuf, eps_i=0):
    S.op("s", lambda e: e.activation(r_ap, var_ap, AF.Ln, bias=p.epsv[:, eps_i:eps_i + 1]),
         reads=(vbuf, p.bC), writes=(rbuf,))
    S.op("s", lambda e: e.activation(r_ap, r_ap, AF.Exp, scale=-0.5),
         reads=(rbuf,), writes=(rbuf,))


def tok_to_fm(nc, S, p, tok_tiles, dst3, dst_buf, sp, nch=KC):
    for c in range(nch):
        pti = c % 2
        ptn, ptb = p.pt[pti], p.ptB[pti]
        emits = []
        rset = []
        for j in range(4):
            ap, bf = tok_tiles[j]
            emits.append(lambda e, ap=ap, c=c, j=j, ptn=ptn:
                         e.transpose(ptn[:, j * 128:(j + 1) * 128],
                                     ap[:, c * 128:(c + 1) * 128], p.ident[:]))
            if bf not in rset:
                rset.append(bf)
        S.group("t", emits, reads=tuple(rset) + (p.bC,), writes=(ptb,))
        S.op("v", lambda e, c=c, ptn=ptn: e.tensor_copy(
            dst3[:, c, sp * SPAN:(sp + 1) * SPAN], ptn[:, 0:512]),
            reads=(ptb,), writes=(dst_buf,))


def tpose4(nc, S, p, srcs, src_bufs, dst_ap, dst_buf, pt_idx=0):
    ptn, ptb = p.pt[pt_idx], p.ptB[pt_idx]
    emits = []
    for j, ap in enumerate(srcs):
        emits.append(lambda e, ap=ap, j=j, ptn=ptn:
                     e.transpose(ptn[:, j * 128:(j + 1) * 128], ap, p.ident[:]))
    rset = []
    for b in src_bufs:
        if b not in rset:
            rset.append(b)
    S.group("t", emits, reads=tuple(rset) + (p.bC,), writes=(ptb,))
    S.op("v", lambda e, ptn=ptn: e.tensor_copy(dst_ap, ptn[:, 0:512]),
         reads=(ptb,), writes=(dst_buf,))


def work_alias(mem, s, idx):
    return mem.at(OFF_F + 4, [128, D], F32, f"wka{s}_{idx}")[:]


def bcast_row(nc, S, p, row_ap, row_buf, dst_ap, dst_buf, n, bank_i=5):
    bank, bankB = p.pb[bank_i], p.pbB[bank_i]
    for o in range(0, n, 512):
        wdt = min(512, n - o)
        S.op("t", lambda e, o=o, wdt=wdt: e.matmul(
            bank[:, 0:wdt], p.onesrow[:, 0:128], row_ap[:, o:o + wdt],
            start=True, stop=True), reads=(row_buf, p.bC), writes=(bankB,))
        S.op("v", lambda e, o=o, wdt=wdt: e.tensor_copy(dst_ap[:, o:o + wdt], bank[:, 0:wdt]),
             reads=(bankB,), writes=(dst_buf,))


def load_w_fm(nc, S, p, io, src, dst3, dstB, kc=KC):
    return S.dma("g", dst3, src.rearrange("(k p) n -> p k n", p=128), writes=(dstB,))



def phase_l0attn(nc, S, io, p, s, xh):
    x_src = io["x_in"]
    base = s * T
    mem = p.mem
    ropeC = mem.at(OFF_E + 16, [128, T], BF16, f"ropeC{s}")
    ropeS = mem.at(OFF_E + 20, [128, T], BF16, f"ropeS{s}")
    wk_st = mem.at(OFF_E, [128, KC, 1024], BF16, f"wkst{s}")
    vv = mem.at(OFF_V, [128, 256], F32, f"vv{s}")
    qkvb = vv[:, 0:24]
    stats = vv[:, 32:64].rearrange("p (t two) -> p t two", two=2)
    rst = vv[:, 64:80]
    cvec = vv[:, 80:96]
    bnt = vv[:, 96:108]
    bnt2 = vv[:, 96:120].rearrange("p (j c) -> p j c", j=2)
    hs2 = mem.at(OFF_V + 1.0, [2, KC, 2, 8], F32, f"hs2{s}")
    crow = mem.at(OFF_V + 1.25, [1, 16], BF16, f"crow{s}")
    brow = mem.at(OFF_V + 1.5, [1, D], BF16, f"brow{s}")
    bout_b = mem.at(OFF_V + 3.5, [128, D], BF16, f"boutb{s}")
    vb_b = mem.at(OFF_V + 5.5, [128, D], BF16, f"vbb{s}")
    etile = mem.at(OFF_S + 4, [128, 4, 512], BF16, f"et{s}")
    dsc = mem.at(OFF_S, [128, T], BF16, f"dsc{s}")
    dd2 = mem.at(OFF_F, [2, 512], F32, f"dd2{s}")
    rb2 = mem.at(OFF_F + 4, [2, T], BF16, f"rb2{s}")
    cq8 = mem.at(OFF_W, [1, 2, T], BF16, f"cq8{s}")
    kaug = mem.at(OFF_A, [128, 2, 2, T], BF16, f"kaug{s}")
    qaug = mem.at(OFF_A + 16, [128, 2, 2, 512], BF16, f"qaug{s}")
    etile2 = mem.at(OFF_S + 4, [128, 2, 1024], BF16, f"et2{s}")
    ctxu = mem.at(OFF_S, [128, 512], BF16, f"ctxu{s}")
    zzbuf = mem.at(OFF_F, [128, 2, 512], F32, f"zzbuf{s}")
    zz2 = mem.at(OFF_F, [2, 2, 512], F32, f"zz2{s}")
    zzrb = mem.at(OFF_F + 6, [2, 512], BF16, f"zzrb{s}")
    rzbs = mem.at(OFF_F + 4, [128, 512], BF16, f"rzbs{s}")
    otok = mem.at(OFF_F, [128, D], F32, f"otok{s}")
    xtile = mem.at(OFF_F + 4, [128, D], F32, f"xtile{s}")

    bW, bE2, bV2, bF2, bS2 = p.bW, p.bE, p.bV, p.bF, p.bS
    bA, bB, bC, bD = p.bA, p.bB, p.bCc, p.bD
    bet = [S.fork(bS2, f"l0et{j}") for j in range(4)]
    nmA = vv[:, 120:136]
    nmP = vv[:, 136:152]
    W3 = p.Wb[:]
    Dv4 = p.Dv[:].rearrange("p t (h c) -> p t h c", c=65)

    S.dma("q", ropeC[:], io["ropeC"], writes=(bE2,))
    S.dma("q", ropeS[:], io["ropeS"], writes=(bE2,))
    S.dma("s", qkvb, io["b_qkv0"].rearrange("(c p) -> p c", p=128), writes=(bV2,), slow=True)
    S.dma("g", brow[:], io["b_out0"].rearrange("(o n) -> o n", o=1), writes=(bV2,))
    bcast_row(nc, S, p, brow[:], bV2, bout_b[:], bV2, D)
    S.dma("g", brow[:], io["b_qkv0"].rearrange("(o n) -> o n", o=1)[:, 2 * D:3 * D], reads=(bV2,), writes=(bV2,))
    bcast_row(nc, S, p, brow[:], bV2, vb_b[:], bV2, D)
    load_w_fm(nc, S, p, io, io["w_qkv0"][:, 0:D], W3, bW)

    xt2 = xtile[:].tensor
    bxt = [S.fork(bF2, f"l0xt{j}") for j in range(2)]
    bst = [S.fork(bV2, f"l0st{t}") for t in range(NTILE)]
    bbn = [S.fork(bV2, f"l0bn{j}") for j in range(2)]
    bsb = [S.fork(bS2, f"l0sb{j}") for j in range(4)]
    xsl = [xtile[:], otok[:]]
    for sp in range(NSPAN):
        for j in range(4):
            ti = sp * 4 + j
            xs = xsl[ti % 2]
            S.dma("q", xs, x_src[base + ti * 128: base + (ti + 1) * 128, :],
                  writes=(bxt[ti % 2],))
            ln_stats(nc, S, p, xs, bxt[ti % 2], stats[:, ti, :], bst[ti],
                     bnt2[:, ti % 2, :], bbn[ti % 2])
            rstd(nc, S, p, stats[:, ti, 1:2], rst[:, ti:ti + 1], bst[ti], bst[ti])
            S.op("v", lambda e, ti=ti: e.scalar_tensor_tensor(
                nmA[:, ti:ti + 1], stats[:, ti, 0:1], -1.0, rst[:, ti:ti + 1],
                ALU.mult, ALU.mult), reads=(bst[ti],), writes=(bst[ti],))
            S.op("s", lambda e, j=j, ti=ti, xs=xs: e.activation(
                p.Sb[:, j, :], xs, AF.Identity, scale=rst[:, ti:ti + 1],
                bias=nmA[:, ti:ti + 1]),
                reads=(bxt[ti % 2], bst[ti]), writes=(bsb[j],))
        for c in range(KC):
            tpose4(nc, S, p, [p.Sb[:, j, c * 128:(c + 1) * 128] for j in range(4)],
                   [bsb[j] for j in range(4)], p.A[:, c, sp * SPAN:(sp + 1) * SPAN],
                   bA, pt_idx=c % 2)

    load_w_fm(nc, S, p, io, io["w_qkv0"][:, D:2 * D], wk_st[:], bE2)
    for nchunk in range(KC):
        for sp in range(NSPAN):
            bi = (nchunk * NSPAN + sp) % 4
            bank, bankB = p.pb[bi], p.pbB[bi]
            S.group("t", [
                (lambda e, k=k, nchunk=nchunk, sp=sp, bank=bank:
                 e.matmul(bank[:], W3[:, k, nchunk * 128:(nchunk + 1) * 128],
                          p.A[:, k, sp * SPAN:(sp + 1) * SPAN],
                          start=(k == 0), stop=(k == KC - 1)))
                for k in range(KC)], reads=(bW, bA), writes=(bankB,))
            S.op("s", lambda e, nchunk=nchunk, sp=sp, bank=bank:
                 e.activation(p.B[:, nchunk, sp * SPAN:(sp + 1) * SPAN], bank[:],
                              AF.Identity, bias=qkvb[:, nchunk:nchunk + 1]),
                 reads=(bankB, bV2), writes=(bB,))
    load_w_fm(nc, S, p, io, io["w_qkv0"][:, 2 * D:3 * D], W3, bW)
    for nchunk in range(KC):
        for sp in range(NSPAN):
            bi = (nchunk * NSPAN + sp) % 4
            bank, bankB = p.pb[bi], p.pbB[bi]
            S.group("t", [
                (lambda e, k=k, nchunk=nchunk, sp=sp, bank=bank:
                 e.matmul(bank[:], wk_st[:, k, nchunk * 128:(nchunk + 1) * 128],
                          p.A[:, k, sp * SPAN:(sp + 1) * SPAN],
                          start=(k == 0), stop=(k == KC - 1)))
                for k in range(KC)], reads=(bE2, bA), writes=(bankB,))
            S.op("s", lambda e, nchunk=nchunk, sp=sp, bank=bank:
                 e.activation(p.Cc[:, nchunk, sp * SPAN:(sp + 1) * SPAN], bank[:],
                              AF.Identity, bias=qkvb[:, 8 + nchunk:8 + nchunk + 1]),
                 reads=(bankB, bV2), writes=(bC,))
    S.op("v", lambda e: e.memset(Dv4[:, :, :, 64:65], 1.0), writes=(bD,))
    for ti in range(NTILE):
        for nh2 in range(2):
            bi = (ti * 2 + nh2) % 4
            bank, bankB = p.pb[bi], p.pbB[bi]
            S.group("t", [
                (lambda e, k=k, ti=ti, nh2=nh2, bank=bank:
                 e.matmul(bank[:], p.A[:, k, ti * 128:(ti + 1) * 128],
                          W3[:, k, nh2 * 512:(nh2 + 1) * 512],
                          start=(k == 0), stop=(k == KC - 1)))
                for k in range(KC)], reads=(bW, bA), writes=(bankB,))
            S.op("v", lambda e, ti=ti, nh2=nh2, bank=bank: e.scalar_tensor_tensor(
                Dv4[:, ti, 8 * nh2:8 * (nh2 + 1), 0:64],
                bank[:].rearrange("p (h c) -> p h c", c=64), 1.0,
                vb_b[:, nh2 * 512:(nh2 + 1) * 512].rearrange("p (h c) -> p h c", c=64),
                ALU.mult, ALU.add), reads=(bankB, bV2), writes=(bD,))

    ropesc = p.Sb[:].rearrange("p f d -> p (f d)")
    for src3, srcB in ((p.B[:], bB), (p.Cc[:], bC)):
        for c in range(KC):
            S.op("v", lambda e, c=c, src3=src3: e.tensor_tensor(
                ropesc[:, 0:T], src3[:, c, :], ropeC[:], ALU.mult),
                reads=(srcB, bE2), writes=(bS2,))
            for q4 in range(4):
                bi = 4 + q4 % 2
                bank, bankB = p.pb[bi], p.pbB[bi]
                S.op("t", lambda e, c=c, q4=q4, src3=src3, bank=bank: e.matmul(
                    bank[:], p.perm64[:], src3[:, c, q4 * 512:(q4 + 1) * 512],
                    start=True, stop=True), reads=(srcB, p.bC), writes=(bankB,))
                S.op("v", lambda e, c=c, q4=q4, bank=bank: e.tensor_tensor(
                    ropesc[:, T + q4 * 512: T + (q4 + 1) * 512], bank[:],
                    ropeS[:, q4 * 512:(q4 + 1) * 512], ALU.mult),
                    reads=(bankB, bE2), writes=(bS2,))
            S.op("v", lambda e, c=c, src3=src3: e.tensor_tensor(
                src3[:, c, :], ropesc[:, 0:T], ropesc[:, T:2 * T], ALU.add),
                reads=(bS2,), writes=(srcB,))

    for idx, (src3, srcB) in enumerate(((p.B[:], bB), (p.Cc[:], bC))):
        for c in range(KC):
            S.op("v", lambda e, c=c, src3=src3: e.tensor_tensor(
                ropesc[:, 0:T], src3[:, c, :], src3[:, c, :], ALU.mult),
                reads=(srcB,), writes=(bS2,))
            for sp in range(NSPAN):
                bi = 4 + sp % 2
                bank, bankB = p.pb[bi], p.pbB[bi]
                S.op("t", lambda e, c=c, sp=sp, bank=bank: e.matmul(
                    bank[0:2, :], p.selt[:, c * 2:(c + 1) * 2],
                    ropesc[:, sp * SPAN:(sp + 1) * SPAN], start=True, stop=True),
                    reads=(bS2, p.bC), writes=(bankB,))
                S.op("v", lambda e, c=c, sp=sp, idx=idx, bank=bank: e.tensor_reduce(
                    hs2[0:2, c, idx, sp:sp + 1],
                    bank[0:2, 0:512], AX.X, ALU.max),
                    reads=(bankB,), writes=(bV2,))
    for idx in range(2):
        S.op("v", lambda e, idx=idx: e.tensor_reduce(
            hs2[:, :, idx, 4:5], hs2[:, :, idx, 0:4], AX.X, ALU.max),
            reads=(bV2,), writes=(bV2,))
    S.op("v", lambda e: e.scalar_tensor_tensor(hs2[:, :, 0, 6:7], hs2[:, :, 0, 4:5],
                                               1.0, hs2[:, :, 1, 4:5],
                                               ALU.mult, ALU.add),
         reads=(bV2,), writes=(bV2,))
    S.op("v", lambda e: e.tensor_scalar_mul(hs2[:, :, 0, 7:8], hs2[:, :, 0, 6:7], -1.0 / 16.0),
         reads=(bV2,), writes=(bV2,))
    S.op("v", lambda e: e.tensor_scalar_mul(hs2[:, :, 0, 5:6], hs2[:, :, 0, 7:8], 4.0),
         reads=(bV2,), writes=(bV2,))
    S.op("v", lambda e: e.tensor_scalar(hs2[:, :, 0, 3:4], hs2[:, :, 0, 7:8], 8.0, 288.0,
                                        ALU.mult, ALU.add),
         reads=(bV2,), writes=(bV2,))

    bF2 = S.fork_multi([bF2, bxt[0], bxt[1]], "l0Fc")
    kA = [S.fork(p.bA, f"l0ka{j}") for j in range(2)]
    qA = [S.fork(p.bA, f"l0qa{j}") for j in range(2)]
    S.op("v", lambda e: e.memset(kaug[64:65, :, :, :], 1.0), writes=(kA[0], kA[1]))
    for hp in range(KC):
        S.dma("g", kaug[0:64, hp % 2, 0, :], p.Cc[0:64, hp, :],
              reads=(bC,), writes=(kA[hp % 2],))
        S.dma("g", kaug[0:64, hp % 2, 1, :], p.Cc[64:128, hp, :],
              reads=(bC,), writes=(kA[hp % 2],))
        S.op("v", lambda e, hp=hp: e.tensor_tensor(
            dsc[:], p.B[:, hp, :], p.Cc[:, hp, :], ALU.mult),
            reads=(bB, bC), writes=(bS2,))
        for sp in range(NSPAN):
            bank, bankB = p.pb[4], p.pbB[4]
            S.op("t", lambda e, hp=hp, sp=sp, bank=bank: e.matmul(
                bank[0:2, :], p.selt[:, hp * 2:(hp + 1) * 2],
                dsc[:, sp * SPAN:(sp + 1) * SPAN], start=True, stop=True),
                reads=(bS2, p.bC), writes=(bankB,))
            S.op("v", lambda e, hp=hp, sp=sp, bank=bank: e.tensor_scalar(
                dd2[:], bank[0:2, :], -0.5, hs2[0:2, hp, 0, 5:6],
                ALU.mult, ALU.add), reads=(bankB, bV2), writes=(bF2,))
            S.op("v", lambda e, hp=hp, sp=sp: e.tensor_scalar(
                rb2[:, sp * SPAN:(sp + 1) * SPAN], dd2[:], hs2[0:2, hp, 0, 3:4], None,
                ALU.min), reads=(bF2, bV2), writes=(bF2,))
        S.dma("s", cq8[:], rb2[:], reads=(bF2,), writes=(bW,))
        for sp in range(NSPAN):
            if sp % 2 == 0:
                cbank = [p.pb[4], p.pb[5]]
                cbankB = [p.pbB[4], p.pbB[5]]
            else:
                cbank = [p.ptf[0][:], p.ptf[1][:]]
                cbankB = [p.ptB[0], p.ptB[1]]
            S.dma("g", qaug[0:64, sp % 2, 0, :],
                  p.B[0:64, hp, sp * SPAN:(sp + 1) * SPAN],
                  reads=(bB,), writes=(qA[sp % 2],))
            S.dma("g", qaug[0:64, sp % 2, 1, :],
                  p.B[64:128, hp, sp * SPAN:(sp + 1) * SPAN],
                  reads=(bB,), writes=(qA[sp % 2],))
            S.dma("g", qaug[64:65, sp % 2, :, :],
                  cq8[0:1, :, sp * SPAN:(sp + 1) * SPAN],
                  reads=(bW,), writes=(qA[sp % 2],))
            def emit_s(tk, hp=hp, sp=sp):
                i2 = tk % 2
                for hh in range(2):
                    S.op("t", lambda e, hh=hh, tk=tk, i2=i2, hp=hp, sp=sp:
                         e.matmul(p.pb[i2 * 2 + hh][:],
                                  kaug[0:65, hp % 2, hh, tk * 128:(tk + 1) * 128],
                                  qaug[0:65, sp % 2, hh, :],
                                  start=True, stop=True),
                         reads=(kA[hp % 2], qA[sp % 2]),
                         writes=(p.pbB[i2 * 2 + hh],))
                S.op("s", lambda e, i2=i2: e.activation(
                    etile2[:, i2, :], p.pp[i2][:], AF.Exp, scale=0.125),
                    reads=(p.pbB[i2 * 2], p.pbB[i2 * 2 + 1]),
                    writes=(bet[i2 * 2], bet[i2 * 2 + 1]))
            def emit_pv(tk, hh, hp=hp):
                h = 2 * hp + hh
                esl = (tk % 2) * 2 + hh
                et = etile2[:, tk % 2, hh * 512:(hh + 1) * 512]
                S.op("t", lambda e, h=h, tk=tk, et=et, cb=cbank[hh]:
                     e.matmul(cb[0:65, :], Dv4[:, tk, h, :], et,
                              start=(tk == 0), stop=(tk == NTILE - 1)),
                     reads=(bD, bet[esl]), writes=(cbankB[hh],))
            def epi(hp=hp, sp=sp, cbank=cbank, cbankB=cbankB):
                for hh in range(2):
                    S.op("v", lambda e, hh=hh, cb=cbank[hh]: e.tensor_copy(
                        zzbuf[64:65, hh, :], cb[64:65, :]),
                        reads=(cbankB[hh],), writes=(bF2,))
                    S.op("v", lambda e, hh=hh, cb=cbank[hh]: e.tensor_copy(
                        ctxu[hh * 64:(hh + 1) * 64, :], cb[0:64, :]),
                        reads=(cbankB[hh],), writes=(bS2,))
                S.dma("s", zz2[0:2, 0, :], zzbuf[64:65, :, :], reads=(bF2,), writes=(bF2,))
                S.op("v", lambda e: e.reciprocal(zz2[:, 1, :], zz2[:, 0, :]),
                     reads=(bF2,), writes=(bF2,))
                S.op("v", lambda e: e.tensor_copy(zzrb[:], zz2[:, 1, :]),
                     reads=(bF2,), writes=(bF2,))
                bank, bankB = cbank[0], cbankB[0]
                S.op("t", lambda e, bank=bank: e.matmul(
                    bank[:], p.sel2[:], zzrb[:], start=True, stop=True),
                    reads=(bF2, p.bC), writes=(bankB,))
                S.op("v", lambda e, bank=bank: e.tensor_copy(rzbs[:], bank[:]),
                     reads=(bankB,), writes=(bF2,))
                S.op("v", lambda e, hp=hp, sp=sp: e.tensor_tensor(
                    p.E[:, hp, sp * SPAN:(sp + 1) * SPAN],
                    ctxu[:, :], rzbs[:], ALU.mult),
                    reads=(bS2, bF2), writes=(bE2,))
            for tk in range(NTILE):
                emit_s(tk)
                if tk == 2 and p.pend is not None:
                    p.pend()
                    p.pend = None
                if tk > 0:
                    for hh in range(2):
                        emit_pv(tk - 1, hh)
            for hh in range(2):
                emit_pv(NTILE - 1, hh)
            p.pend = epi

    if p.pend is not None:
        p.pend()
        p.pend = None
    load_w_fm(nc, S, p, io, io["w_out0"], W3, bW)
    ot2 = mem.at(OFF_B, [128, 2, D], F32, f"l0ot2{s}")
    bot = [S.fork(p.bB, f"l0ot{j}") for j in range(2)]
    bxt2 = [S.fork(bF2, f"l0px{j}") for j in range(2)]
    bst2 = [S.fork(bV2, f"l0pst{t}") for t in range(NTILE)]
    bbn2 = [S.fork(bV2, f"l0pbn{j}") for j in range(2)]
    for ti in range(NTILE):
        j2 = ti % 2
        oslot = ot2[:, j2, :]
        xs = xsl[j2]
        banks = [p.pb[(ti % 2) * 2], p.pb[(ti % 2) * 2 + 1]]
        banksB = [p.pbB[(ti % 2) * 2], p.pbB[(ti % 2) * 2 + 1]]
        for nh2 in range(2):
            S.group("t", [
                (lambda e, k=k, ti=ti, nh2=nh2, bank=banks[nh2]:
                 e.matmul(bank[:], p.E[:, k, ti * 128:(ti + 1) * 128],
                          W3[:, k, nh2 * 512:(nh2 + 1) * 512],
                          start=(k == 0), stop=(k == KC - 1)))
                for k in range(KC)], reads=(bE2, bW), writes=(banksB[nh2],))
        S.op("v", lambda e, ti=ti, oslot=oslot: e.tensor_tensor(
            oslot, p.pp[ti % 2][:], bout_b[:], ALU.add),
            reads=(banksB[0], banksB[1], bV2), writes=(bot[j2],))
        ln_stats(nc, S, p, oslot, bot[j2], stats[:, ti, :], bst2[ti],
                 bnt2[:, j2, :], bbn2[j2])
        rstd(nc, S, p, stats[:, ti, 1:2], rst[:, ti:ti + 1], bst2[ti], bst2[ti])
        S.dma("q", xs, x_src[base + ti * 128: base + (ti + 1) * 128, :],
              writes=(bxt2[j2],))
        S.op("v", lambda e, ti=ti: e.scalar_tensor_tensor(
            nmP[:, ti:ti + 1], stats[:, ti, 0:1], -1.0, rst[:, ti:ti + 1],
            ALU.mult, ALU.mult), reads=(bst2[ti],), writes=(bst2[ti],))
        S.op("s", lambda e, ti=ti, oslot=oslot: e.activation(
            oslot, oslot, AF.Identity, scale=rst[:, ti:ti + 1],
            bias=nmP[:, ti:ti + 1]), reads=(bst2[ti],), writes=(bot[j2],))
        S.op("v", lambda e, oslot=oslot, xs=xs: e.tensor_tensor(
            oslot, oslot, xs, ALU.add), reads=(bxt2[j2],), writes=(bot[j2],))
        S.dma("s", xh[base + ti * 128: base + (ti + 1) * 128, :], oslot,
              reads=(bot[j2],))
    S.barrier()


def phase_l0ffn(nc, S, io, p, s, xh):
    base = s * T
    mem = p.mem
    vv = mem.at(OFF_V, [128, 256], F32, f"f_vv{s}")
    stats = vv[:, 0:32].rearrange("p (t two) -> p t two", two=2)
    rst = vv[:, 32:48]
    bnt = vv[:, 48:60]
    am = vv[:, 60:61]
    ls_all = vv[:, 64:80]
    rls = vv[:, 80:81]
    rscc = vv[:, 96:104]
    rrsc = vv[:, 104:105]
    rrow = mem.at(OFF_V + 1, [1, D], BF16, f"f_rrow{s}")
    rsc_b = mem.at(OFF_V + 3, [128, D], BF16, f"f_rscb{s}")
    xtile = mem.at(OFF_F, [128, D], F32, f"f_xt{s}")
    work = mem.at(OFF_F + 4, [128, D], F32, f"f_wk{s}")
    qi = mem.at(OFF_C + 16, [128, D], I32, f"f_qi{s}")
    rtile = p.Cc[:]
    rhsT3 = p.B[:]
    qrhs = p.Wb[:]
    bA, bB, bC, bW = p.bA, p.bB, p.bCc, p.bW
    bV2, bF2, bS2 = p.bV, p.bF, p.bS

    S.dma("g", rtile[:, :, 0:1024], io["rhs0"].rearrange("(k p) n -> p k n", p=128),
          writes=(bC,))
    for c in range(KC):
        for ktg in (0, 4):
            tpose4(nc, S, p,
                   [rtile[:, kt, c * 128:(c + 1) * 128] for kt in range(ktg, ktg + 4)],
                   [bC] * 4, rhsT3[:, c, ktg * 128:(ktg + 4) * 128], bB, pt_idx=c % 2)
    for c in range(KC):
        S.op("v", lambda e, c=c: e.tensor_reduce(
            rscc[:, c:c + 1], rhsT3[:, c, 0:1024], AX.X, ALU.max,
            apply_absolute_value=True), reads=(bB,), writes=(bV2,))
        S.op("v", lambda e, c=c: e.tensor_scalar(
            rscc[:, c:c + 1], rscc[:, c:c + 1], 1e-12, 1.0 / 127.0,
            ALU.max, ALU.mult), reads=(bV2,), writes=(bV2,))
        S.op("v", lambda e, c=c: e.reciprocal(rrsc[:], rscc[:, c:c + 1]),
             reads=(bV2,), writes=(bV2,))
        S.op("s", lambda e, c=c: e.activation(
            work[:], rhsT3[:, c, 0:1024], AF.Identity, scale=rrsc[:],
            bias=p.epsv[:, 1:2]), reads=(bB, bV2, p.bC), writes=(bF2,))
        S.op("v", lambda e: e.tensor_scalar(
            work[:], work[:], MAGIC, C256, ALU.subtract, ALU.add),
            reads=(bF2,), writes=(bF2,))
        S.op("v", lambda e: e.tensor_copy(qi[:], work[:]), reads=(bF2,), writes=(bC,))
        S.op("v", lambda e, c=c: e.tensor_scalar(
            rhsT3[:, c, 1024:2048], qi[:], 256.0, -127.0, ALU.subtract, ALU.max),
            reads=(bC,), writes=(bB,))
        S.op("v", lambda e, c=c: e.tensor_scalar(
            rhsT3[:, c, 1024:2048], rhsT3[:, c, 1024:2048], 127.0, None, ALU.min),
            reads=(bB,), writes=(bB,))
    for kt in range(KC):
        for cg in (0, 4):
            tpose4(nc, S, p,
                   [rhsT3[:, c, 1024 + kt * 128: 1024 + (kt + 1) * 128]
                    for c in range(cg, cg + 4)],
                   [bB] * 4, qrhs[:, kt, cg * 128:(cg + 4) * 128], bW, pt_idx=kt % 2)
    for c in range(KC):
        S.dma("g", rrow[0:1, c * 128:(c + 1) * 128], rscc[:, c:c + 1],
              reads=(bV2,), writes=(bV2,), slow=True)
    bcast_row(nc, S, p, rrow[:], bV2, rsc_b[:], bV2, D)

    wk2 = mem.at(OFF_B, [128, 2, D], F32, f"f_wk2{s}")
    qi2 = mem.at(OFF_B + 8, [128, 2, D], I32, f"f_qi2{s}")
    rls2 = vv[:, 80:82]
    bwk = [S.fork(p.bB, f"f_wk{j}") for j in range(2)]
    bqi = [S.fork(p.bB, f"f_qi{j}") for j in range(2)]
    bxt = [S.fork(bF2, f"f_xt{j}") for j in range(2)]
    bst = [S.fork(bV2, f"f_st{t}") for t in range(NTILE)]
    bbn = [S.fork(bV2, f"f_bn{j}") for j in range(2)]
    bsb = [S.fork(bS2, f"f_sb{j}") for j in range(4)]
    bnt2 = vv[:, 128:152].rearrange("p (j c) -> p j c", j=2)
    nmc = vv[:, 105:121]
    nmc3 = vv[:, 176:192]
    xsl = [xtile[:], work[:]]
    for sp in range(NSPAN):
        for j in range(4):
            ti = sp * 4 + j
            j2 = ti % 2
            xs = xsl[j2]
            wks = wk2[:, j2, :]
            qis = qi2[:, j2, :]
            S.dma("q", xs, xh[base + ti * 128: base + (ti + 1) * 128, :],
                  writes=(bxt[j2],))
            ln_stats(nc, S, p, xs, bxt[j2], stats[:, ti, :], bst[ti],
                     bnt2[:, j2, :], bbn[j2])
            rstd(nc, S, p, stats[:, ti, 1:2], rst[:, ti:ti + 1], bst[ti], bst[ti])
            S.op("v", lambda e, ti=ti: e.scalar_tensor_tensor(
                nmc[:, ti:ti + 1], stats[:, ti, 0:1], -1.0, rst[:, ti:ti + 1],
                ALU.mult, ALU.mult), reads=(bst[ti],), writes=(bst[ti],))
            S.op("s", lambda e, ti=ti, xs=xs, wks=wks: e.activation(
                wks, xs, AF.Identity, scale=rst[:, ti:ti + 1],
                bias=nmc[:, ti:ti + 1]),
                reads=(bxt[j2], bst[ti]), writes=(bwk[j2],))
            S.op("v", lambda e, ti=ti, wks=wks: e.tensor_reduce(
                ls_all[:, ti:ti + 1], wks, AX.X, ALU.max, apply_absolute_value=True),
                reads=(bwk[j2],), writes=(bst[ti],))
            S.op("v", lambda e, ti=ti: e.tensor_scalar(
                ls_all[:, ti:ti + 1], ls_all[:, ti:ti + 1], 1e-12, 1.0 / 127.0,
                ALU.max, ALU.mult), reads=(bst[ti],), writes=(bst[ti],))
            S.op("v", lambda e, ti=ti, j2=j2: e.reciprocal(
                rls2[:, j2:j2 + 1], ls_all[:, ti:ti + 1]),
                reads=(bst[ti],), writes=(bbn[j2],))
            S.op("s", lambda e, j2=j2, wks=wks: e.activation(
                wks, wks, AF.Identity, scale=rls2[:, j2:j2 + 1],
                bias=p.epsv[:, 1:2]),
                reads=(bbn[j2], p.bC), writes=(bwk[j2],))
            S.op("v", lambda e, wks=wks: e.tensor_scalar(
                wks, wks, MAGIC, C256, ALU.subtract, ALU.add),
                reads=(), writes=(bwk[j2],))
            S.op("v", lambda e, wks=wks, qis=qis: e.tensor_copy(qis, wks),
                 reads=(bwk[j2],), writes=(bqi[j2],))
            S.op("v", lambda e, j=j, qis=qis: e.tensor_scalar(
                p.Sb[:, j, :], qis, 256.0, -127.0, ALU.subtract, ALU.max),
                reads=(bqi[j2],), writes=(bsb[j],))
            S.op("v", lambda e, j=j: e.tensor_scalar(
                p.Sb[:, j, :], p.Sb[:, j, :], 127.0, None, ALU.min),
                reads=(), writes=(bsb[j],))
        for c in range(KC):
            tpose4(nc, S, p, [p.Sb[:, j, c * 128:(c + 1) * 128] for j in range(4)],
                   [bsb[j] for j in range(4)], p.A[:, c, sp * SPAN:(sp + 1) * SPAN],
                   bA, pt_idx=c % 2)

    for ti in range(NTILE):
        j2 = ti % 2
        xs = xsl[j2]
        wks = wk2[:, j2, :]
        banks = [p.pb[(ti % 2) * 2], p.pb[(ti % 2) * 2 + 1]]
        banksB = [p.pbB[(ti % 2) * 2], p.pbB[(ti % 2) * 2 + 1]]
        for nh2 in range(2):
            S.group("t", [
                (lambda e, k=k, ti=ti, nh2=nh2, bank=banks[nh2]:
                 e.matmul(bank[:], p.A[:, k, ti * 128:(ti + 1) * 128],
                          qrhs[:, k, nh2 * 512:(nh2 + 1) * 512],
                          start=(k == 0), stop=(k == KC - 1)))
                for k in range(KC)], reads=(bA, bW), writes=(banksB[nh2],))
        S.op("v", lambda e, ti=ti, wks=wks: e.scalar_tensor_tensor(
            wks, p.pp[ti % 2][:], ls_all[:, ti:ti + 1],
            rsc_b[:], ALU.mult, ALU.mult),
            reads=(banksB[0], banksB[1], bst[ti], bV2), writes=(bwk[j2],))
        ln_stats(nc, S, p, wks, bwk[j2], stats[:, ti, :], bst[ti],
                 bnt2[:, j2, :], bbn[j2])
        rstd(nc, S, p, stats[:, ti, 1:2], rst[:, ti:ti + 1], bst[ti], bst[ti])
        S.dma("q", xs, xh[base + ti * 128: base + (ti + 1) * 128, :], writes=(bxt[j2],))
        S.op("v", lambda e, ti=ti: e.scalar_tensor_tensor(
            nmc3[:, ti:ti + 1], stats[:, ti, 0:1], -1.0, rst[:, ti:ti + 1],
            ALU.mult, ALU.mult), reads=(bst[ti],), writes=(bst[ti],))
        S.op("s", lambda e, ti=ti, wks=wks: e.activation(
            wks, wks, AF.Identity, scale=rst[:, ti:ti + 1],
            bias=nmc3[:, ti:ti + 1]), reads=(bst[ti],), writes=(bwk[j2],))
        S.op("v", lambda e, wks=wks, xs=xs: e.tensor_tensor(wks, wks, xs, ALU.add),
             reads=(bxt[j2],), writes=(bwk[j2],))
        S.dma("s", xh[base + ti * 128: base + (ti + 1) * 128, :], wks,
              reads=(bwk[j2],))
    S.barrier()


def phase_l1attn(nc, S, io, p, s, xh, oh):
    base = s * T
    mem = p.mem
    vv = mem.at(OFF_V, [128, 256], F32, f"a_vv{s}")
    stats0 = vv[:, 0:32].rearrange("p (t two) -> p t two", two=2)
    rst0 = vv[:, 32:48]
    A1 = vv[:, 48:64]
    stats2 = vv[:, 64:96].rearrange("p (t two) -> p t two", two=2)
    rst2 = vv[:, 96:112]
    bnt = vv[:, 112:124]
    qkb = vv[:, 128:144]
    sc1 = vv[:, 144:160]
    nmA = vv[:, 160:176]
    nmD = vv[:, 216:232]
    hs2 = mem.at(OFF_V + 1, [2, KC, 2, 8], F32, f"a_hs2{s}")
    bcs0 = mem.at(OFF_V + 1.5, [128, D], BF16, f"a_bc0{s}")
    bcs1 = mem.at(OFF_V + 3.5, [128, D], BF16, f"a_bc1{s}")
    brow = mem.at(OFF_V + 5.5, [1, D], BF16, f"a_brow{s}")
    wk_st = mem.at(OFF_E, [128, KC, 1024], BF16, f"a_wkst{s}")
    dsc = mem.at(OFF_S, [128, T], BF16, f"a_dsc{s}")
    wctx = mem.at(OFF_S, [128, 512], BF16, f"a_wctx{s}")
    wn = mem.at(OFF_S + 1, [128, 512], BF16, f"a_wn{s}")
    etile = mem.at(OFF_S + 4, [128, 4, 512], BF16, f"a_et{s}")
    dd2 = mem.at(OFF_F, [2, 512], F32, f"a_dd2{s}")
    rb2 = mem.at(OFF_F + 4, [2, T], BF16, f"a_rb2{s}")
    zzbuf = mem.at(OFF_F, [128, 2, 512], F32, f"a_zzb{s}")
    zz2 = mem.at(OFF_F, [2, 2, 512], F32, f"a_zz2{s}")
    zzrb = mem.at(OFF_F + 6, [2, 512], BF16, f"a_zzrb{s}")
    rzbs = mem.at(OFF_F + 4, [128, 512], BF16, f"a_rzbs{s}")
    cq8 = mem.at(OFF_W, [1, 2, T], BF16, f"a_cq8{s}")
    xtile = mem.at(OFF_F, [128, D], F32, f"a_xt{s}")
    work = mem.at(OFF_F + 4, [128, D], F32, f"a_wk2{s}")
    x2fm = mem.at(OFF_W, [128, KC, 512], BF16, f"a_x2fm{s}")
    wm1a = p.A[:]
    wm1b = p.B[:]
    wm2c = p.Cc[:].rearrange("p c (two n) -> p c two n", two=2)
    wm2d = p.Dv[:].rearrange("p t x -> p (t x)")
    hsp = p.E[:].rearrange("p c (x n) -> p (c x) n", x=4)
    W3 = p.Wb[:]
    Dv4 = p.Dv[:].rearrange("p t (h c) -> p t h c", c=65)
    bA, bB, bC, bD, bE2 = p.bA, p.bB, p.bCc, p.bD, p.bE
    bW, bV2, bF2, bS2 = p.bW, p.bV, p.bF, p.bS

    def bc_tile(row_src, dst, n=D):
        S.dma("g", brow[0:1, 0:n], row_src, reads=(bV2,), writes=(bV2,))
        bcast_row(nc, S, p, brow[:], bV2, dst, bV2, n)

    bc_tile(io["ln1_g1"].rearrange("(o n) -> o n", o=1), bcs0[:])
    bc_tile(io["ln1_b1"].rearrange("(o n) -> o n", o=1), bcs1[:])
    S.dma("s", qkb[:, 0:8], io["bq1"].rearrange("(c p) -> p c", p=128),
          reads=(bV2,), writes=(bV2,), slow=True)
    S.dma("s", qkb[:, 8:16], io["bk1"].rearrange("(c p) -> p c", p=128),
          reads=(bV2,), writes=(bV2,), slow=True)
    load_w_fm(nc, S, p, io, io["wq1"], W3, bW)
    bxt = [S.fork(bF2, f"a_xt{j}") for j in range(2)]
    bst = [S.fork(bV2, f"a_st{t}") for t in range(NTILE)]
    bbn = [S.fork(bV2, f"a_bn{j}") for j in range(2)]
    bsb = [S.fork(bS2, f"a_sb{j}") for j in range(4)]
    bnt2 = vv[:, 192:216].rearrange("p (j c) -> p j c", j=2)
    xsl = [xtile[:], work[:]]
    for sp in range(NSPAN):
        for j in range(4):
            ti = sp * 4 + j
            j2 = ti % 2
            xs = xsl[j2]
            S.dma("q", xs, xh[base + ti * 128: base + (ti + 1) * 128, :],
                  writes=(bxt[j2],))
            ln_stats(nc, S, p, xs, bxt[j2], stats0[:, ti, :], bst[ti],
                     bnt2[:, j2, :], bbn[j2])
            rstd(nc, S, p, stats0[:, ti, 1:2], rst0[:, ti:ti + 1], bst[ti], bst[ti])
            S.op("v", lambda e, ti=ti, j2=j2: e.tensor_tensor(
                sc1[:, 3 * j2:3 * j2 + 1], rst0[:, ti:ti + 1], rst0[:, ti:ti + 1],
                ALU.mult), reads=(bst[ti],), writes=(bbn[j2],))
            S.op("v", lambda e, ti=ti, j2=j2: e.tensor_tensor(
                sc1[:, 3 * j2 + 1:3 * j2 + 2], sc1[:, 3 * j2:3 * j2 + 1],
                stats0[:, ti, 1:2], ALU.mult), reads=(bst[ti],), writes=(bbn[j2],))
            rstd(nc, S, p, sc1[:, 3 * j2 + 1:3 * j2 + 2], sc1[:, 3 * j2 + 2:3 * j2 + 3],
                 bbn[j2], bbn[j2])
            S.op("v", lambda e, ti=ti, j2=j2: e.tensor_tensor(
                A1[:, ti:ti + 1], rst0[:, ti:ti + 1], sc1[:, 3 * j2 + 2:3 * j2 + 3],
                ALU.mult), reads=(bbn[j2], bst[ti]), writes=(bst[ti],))
            S.op("v", lambda e, ti=ti: e.scalar_tensor_tensor(
                nmA[:, ti:ti + 1], stats0[:, ti, 0:1], -1.0, A1[:, ti:ti + 1],
                ALU.mult, ALU.mult), reads=(bst[ti],), writes=(bst[ti],))
            S.op("s", lambda e, j=j, ti=ti, xs=xs: e.activation(
                p.Sb[:, j, :], xs, AF.Identity, scale=A1[:, ti:ti + 1],
                bias=nmA[:, ti:ti + 1]),
                reads=(bxt[j2], bst[ti]), writes=(bsb[j],))
            S.op("v", lambda e, j=j: e.tensor_tensor(
                p.Sb[:, j, :], p.Sb[:, j, :], bcs0[:], ALU.mult),
                reads=(bV2,), writes=(bsb[j],))
            S.op("v", lambda e, j=j: e.tensor_tensor(
                p.Sb[:, j, :], p.Sb[:, j, :], bcs1[:], ALU.add),
                reads=(bV2,), writes=(bsb[j],))
        for c in range(KC):
            tpose4(nc, S, p, [p.Sb[:, j, c * 128:(c + 1) * 128] for j in range(4)],
                   [bsb[j] for j in range(4)], p.A[:, c, sp * SPAN:(sp + 1) * SPAN],
                   bA, pt_idx=c % 2)

    load_w_fm(nc, S, p, io, io["wk1"], wk_st[:], bE2)
    for nchunk in range(KC):
        for sp in range(NSPAN):
            bi = (nchunk * NSPAN + sp) % 4
            bank, bankB = p.pb[bi], p.pbB[bi]
            S.group("t", [
                (lambda e, k=k, nchunk=nchunk, sp=sp, bank=bank:
                 e.matmul(bank[:], W3[:, k, nchunk * 128:(nchunk + 1) * 128],
                          p.A[:, k, sp * SPAN:(sp + 1) * SPAN],
                          start=(k == 0), stop=(k == KC - 1)))
                for k in range(KC)], reads=(bW, bA), writes=(bankB,))
            S.op("s", lambda e, nchunk=nchunk, sp=sp, bank=bank:
                 e.activation(p.B[:, nchunk, sp * SPAN:(sp + 1) * SPAN], bank[:],
                              AF.Identity, bias=qkb[:, nchunk:nchunk + 1]),
                 reads=(bankB, bV2), writes=(bB,))
    load_w_fm(nc, S, p, io, io["wv1"], W3, bW)
    for nchunk in range(KC):
        for sp in range(NSPAN):
            bi = (nchunk * NSPAN + sp) % 4
            bank, bankB = p.pb[bi], p.pbB[bi]
            S.group("t", [
                (lambda e, k=k, nchunk=nchunk, sp=sp, bank=bank:
                 e.matmul(bank[:], wk_st[:, k, nchunk * 128:(nchunk + 1) * 128],
                          p.A[:, k, sp * SPAN:(sp + 1) * SPAN],
                          start=(k == 0), stop=(k == KC - 1)))
                for k in range(KC)], reads=(bE2, bA), writes=(bankB,))
            S.op("s", lambda e, nchunk=nchunk, sp=sp, bank=bank:
                 e.activation(p.Cc[:, nchunk, sp * SPAN:(sp + 1) * SPAN], bank[:],
                              AF.Identity, bias=qkb[:, 8 + nchunk:8 + nchunk + 1]),
                 reads=(bankB, bV2), writes=(bC,))
    S.op("v", lambda e: e.memset(Dv4[:, :, :, 64:65], 1.0), writes=(bD,))
    S.dma("g", brow[:], io["bv1"].rearrange("(o n) -> o n", o=1),
          reads=(bV2,), writes=(bV2,))
    for ti in range(NTILE):
        for nh2 in range(2):
            bi = (ti * 2 + nh2) % 4
            bank, bankB = p.pb[bi], p.pbB[bi]
            S.group("t", [
                (lambda e, nh2=nh2, bank=bank: e.matmul(
                    bank[:], p.onesrow[:, 0:128], brow[0:1, nh2 * 512:(nh2 + 1) * 512],
                    start=True, stop=False))] + [
                (lambda e, k=k, ti=ti, nh2=nh2, bank=bank:
                 e.matmul(bank[:], p.A[:, k, ti * 128:(ti + 1) * 128],
                          W3[:, k, nh2 * 512:(nh2 + 1) * 512],
                          start=False, stop=(k == KC - 1)))
                for k in range(KC)], reads=(bW, bA, bV2, p.bC), writes=(bankB,))
            S.op("v", lambda e, ti=ti, nh2=nh2, bank=bank: e.tensor_copy(
                Dv4[:, ti, 8 * nh2:8 * (nh2 + 1), 0:64],
                bank[:].rearrange("p (h c) -> p h c", c=64)),
                reads=(bankB,), writes=(bD,))

    for idx, (src3, srcB) in enumerate(((p.B[:], bB), (p.Cc[:], bC))):
        for c in range(KC):
            S.op("v", lambda e, c=c, src3=src3: e.tensor_tensor(
                dsc[:], src3[:, c, :], src3[:, c, :], ALU.mult),
                reads=(srcB,), writes=(bS2,))
            for sp in range(NSPAN):
                bi = 4 + sp % 2
                bank, bankB = p.pb[bi], p.pbB[bi]
                S.op("t", lambda e, c=c, sp=sp, bank=bank: e.matmul(
                    bank[0:2, :], p.selt[:, c * 2:(c + 1) * 2],
                    dsc[:, sp * SPAN:(sp + 1) * SPAN], start=True, stop=True),
                    reads=(bS2, p.bC), writes=(bankB,))
                S.op("v", lambda e, c=c, sp=sp, idx=idx, bank=bank: e.tensor_reduce(
                    hs2[0:2, c, idx, sp:sp + 1],
                    bank[0:2, 0:512], AX.X, ALU.max),
                    reads=(bankB,), writes=(bV2,))
    for idx in range(2):
        S.op("v", lambda e, idx=idx: e.tensor_reduce(
            hs2[:, :, idx, 4:5], hs2[:, :, idx, 0:4], AX.X, ALU.max),
            reads=(bV2,), writes=(bV2,))
    S.op("v", lambda e: e.scalar_tensor_tensor(hs2[:, :, 0, 6:7], hs2[:, :, 0, 4:5],
                                               1.0, hs2[:, :, 1, 4:5],
                                               ALU.mult, ALU.add),
         reads=(bV2,), writes=(bV2,))
    S.op("v", lambda e: e.tensor_scalar_mul(hs2[:, :, 0, 7:8], hs2[:, :, 0, 6:7], -1.0 / 16.0),
         reads=(bV2,), writes=(bV2,))
    S.op("v", lambda e: e.tensor_scalar_mul(hs2[:, :, 0, 5:6], hs2[:, :, 0, 7:8], 4.0),
         reads=(bV2,), writes=(bV2,))
    S.op("v", lambda e: e.tensor_scalar(hs2[:, :, 0, 3:4], hs2[:, :, 0, 7:8], 8.0, 288.0,
                                        ALU.mult, ALU.add),
         reads=(bV2,), writes=(bV2,))

    bF2 = S.fork_multi([bF2, bxt[0], bxt[1]], "aFc")
    betw = [S.fork(bS2, f"a_et{j}") for j in range(4)]
    kaug = mem.at(OFF_A, [128, 2, 2, T], BF16, f"a_kaug{s}")
    qaug = mem.at(OFF_A + 16, [128, 2, 2, 512], BF16, f"a_qaug{s}")
    etile2 = mem.at(OFF_S + 4, [128, 2, 1024], BF16, f"a_et2{s}")
    kA = [S.fork(p.bA, f"a_ka{j}") for j in range(2)]
    qA = [S.fork(p.bA, f"a_qa{j}") for j in range(2)]
    S.op("v", lambda e: e.memset(kaug[64:65, :, :, :], 1.0), writes=(kA[0], kA[1]))
    wins = []
    for w in range(8):
        i = 256 * w
        st = max(0, i - 128)
        en = min(T, i + 384)
        oe = min(i + 128, T)
        wins.append((st, en, oe))

    for hp in range(KC):
        S.dma("g", kaug[0:64, hp % 2, 0, :], p.Cc[0:64, hp, :],
              reads=(bC,), writes=(kA[hp % 2],))
        S.dma("g", kaug[0:64, hp % 2, 1, :], p.Cc[64:128, hp, :],
              reads=(bC,), writes=(kA[hp % 2],))
        S.op("v", lambda e, hp=hp: e.tensor_tensor(
            dsc[:], p.B[:, hp, :], p.Cc[:, hp, :], ALU.mult),
            reads=(bB, bC), writes=(bS2,))
        for sp in range(NSPAN):
            bank, bankB = p.pb[4], p.pbB[4]
            S.op("t", lambda e, hp=hp, sp=sp, bank=bank: e.matmul(
                bank[0:2, :], p.selt[:, hp * 2:(hp + 1) * 2],
                dsc[:, sp * SPAN:(sp + 1) * SPAN], start=True, stop=True),
                reads=(bS2, p.bC), writes=(bankB,))
            S.op("v", lambda e, hp=hp, sp=sp, bank=bank: e.tensor_scalar(
                dd2[:], bank[0:2, :], -0.5, hs2[0:2, hp, 0, 5:6],
                ALU.mult, ALU.add), reads=(bankB, bV2), writes=(bF2,))
            S.op("v", lambda e, hp=hp, sp=sp: e.tensor_scalar(
                rb2[:, sp * SPAN:(sp + 1) * SPAN], dd2[:], hs2[0:2, hp, 0, 3:4], None,
                ALU.min), reads=(bF2, bV2), writes=(bF2,))
        S.dma("s", cq8[:], rb2[:], reads=(bF2,), writes=(bW,))
        for wi, (st, en, oe) in enumerate(wins):
            Wd = en - st
            ntk = Wd // 128
            if wi % 2 == 0:
                cbank = [p.pb[4], p.pb[5]]
                cbankB = [p.pbB[4], p.pbB[5]]
            else:
                cbank = [p.ptf[0][:], p.ptf[1][:]]
                cbankB = [p.ptB[0], p.ptB[1]]
            S.dma("g", qaug[0:64, wi % 2, 0, 0:Wd],
                  p.B[0:64, hp, st:en], reads=(bB,), writes=(qA[wi % 2],))
            S.dma("g", qaug[0:64, wi % 2, 1, 0:Wd],
                  p.B[64:128, hp, st:en], reads=(bB,), writes=(qA[wi % 2],))
            S.dma("g", qaug[64:65, wi % 2, :, 0:Wd],
                  cq8[0:1, :, st:en], reads=(bW,), writes=(qA[wi % 2],))
            def emit_s(tkb, hp=hp, wi=wi, st=st, Wd=Wd):
                i2 = tkb % 2
                tk0 = st + tkb * 128
                for hh in range(2):
                    S.op("t", lambda e, hh=hh, i2=i2, tk0=tk0, hp=hp, wi=wi, Wd=Wd:
                         e.matmul(p.pb[i2 * 2 + hh][:, 0:Wd],
                                  kaug[0:65, hp % 2, hh, tk0:tk0 + 128],
                                  qaug[0:65, wi % 2, hh, 0:Wd],
                                  start=True, stop=True),
                         reads=(kA[hp % 2], qA[wi % 2]),
                         writes=(p.pbB[i2 * 2 + hh],))
                if Wd == 512:
                    S.op("s", lambda e, i2=i2: e.activation(
                        etile2[:, i2, :], p.pp[i2][:], AF.Exp, scale=0.125),
                        reads=(p.pbB[i2 * 2], p.pbB[i2 * 2 + 1]),
                        writes=(betw[i2 * 2], betw[i2 * 2 + 1]))
                else:
                    for hh in range(2):
                        S.op("s", lambda e, hh=hh, i2=i2, Wd=Wd: e.activation(
                            etile2[:, i2, hh * 512:hh * 512 + Wd],
                            p.pb[i2 * 2 + hh][:, 0:Wd], AF.Exp, scale=0.125),
                            reads=(p.pbB[i2 * 2 + hh],),
                            writes=(betw[i2 * 2 + hh],))
            def emit_pv(tkb, hh, hp=hp, st=st, Wd=Wd, ntk=ntk):
                tk0 = st + tkb * 128
                h = 2 * hp + hh
                esl = (tkb % 2) * 2 + hh
                et = etile2[:, tkb % 2, hh * 512:hh * 512 + Wd]
                S.op("t", lambda e, h=h, tk0=tk0, tkb=tkb, et=et, cb=cbank[hh],
                     ntk=ntk, Wd=Wd:
                     e.matmul(cb[0:65, 0:Wd], Dv4[:, tk0 // 128, h, :], et,
                              start=(tkb == 0), stop=(tkb == ntk - 1)),
                     reads=(bD, betw[esl]), writes=(cbankB[hh],))
            def epi(hp=hp, st=st, en=en, oe=oe, Wd=Wd, cbank=cbank, cbankB=cbankB):
                for hh in range(2):
                    S.op("v", lambda e, hh=hh, cb=cbank[hh], Wd=Wd: e.tensor_copy(
                        zzbuf[64:65, hh, 0:Wd], cb[64:65, 0:Wd]),
                        reads=(cbankB[hh],), writes=(bF2,))
                    S.op("v", lambda e, hh=hh, cb=cbank[hh], Wd=Wd: e.tensor_copy(
                        wctx[hh * 64:(hh + 1) * 64, 0:Wd], cb[0:64, 0:Wd]),
                        reads=(cbankB[hh],), writes=(bS2,))
                S.dma("s", zz2[0:2, 0, 0:Wd], zzbuf[64:65, :, 0:Wd], reads=(bF2,), writes=(bF2,))
                S.op("v", lambda e, Wd=Wd: e.reciprocal(zz2[:, 1, 0:Wd], zz2[:, 0, 0:Wd]),
                     reads=(bF2,), writes=(bF2,))
                S.op("v", lambda e, Wd=Wd: e.tensor_copy(zzrb[:, 0:Wd], zz2[:, 1, 0:Wd]),
                     reads=(bF2,), writes=(bF2,))
                bank, bankB = cbank[0], cbankB[0]
                S.op("t", lambda e, bank=bank, Wd=Wd: e.matmul(
                    bank[:, 0:Wd], p.sel2[:], zzrb[:, 0:Wd], start=True, stop=True),
                    reads=(bF2, p.bC), writes=(bankB,))
                S.op("v", lambda e, bank=bank, Wd=Wd: e.tensor_copy(rzbs[:, 0:Wd], bank[:, 0:Wd]),
                     reads=(bankB,), writes=(bF2,))
                S.op("v", lambda e, Wd=Wd: e.tensor_tensor(
                    wn[:, 0:Wd], wctx[:, 0:Wd], rzbs[:, 0:Wd], ALU.mult),
                    reads=(bS2, bF2), writes=(bS2,))
                if st == 0:
                    S.op("v", lambda e, hp=hp, Wd=Wd: e.tensor_copy(
                        p.E[:, hp, 0:Wd], wn[:, 0:Wd]), reads=(bS2,), writes=(bE2,))
                else:
                    nb = oe - st
                    S.op("v", lambda e, hp=hp, st=st, nb=nb: e.tensor_tensor(
                        p.E[:, hp, st:st + nb], p.E[:, hp, st:st + nb],
                        p.alpha1m[:, 0:nb], ALU.mult), reads=(bE2, p.bC), writes=(bE2,))
                    S.op("v", lambda e, nb=nb: e.tensor_tensor(
                        wctx[:, 0:nb], wn[:, 0:nb], p.alpha[:, 0:nb], ALU.mult),
                        reads=(bS2, p.bC), writes=(bS2,))
                    S.op("v", lambda e, hp=hp, st=st, nb=nb: e.tensor_tensor(
                        p.E[:, hp, st:st + nb], p.E[:, hp, st:st + nb], wctx[:, 0:nb],
                        ALU.add), reads=(bE2, bS2), writes=(bE2,))
                    S.op("v", lambda e, hp=hp, oe=oe, en=en, nb=nb: e.tensor_copy(
                        p.E[:, hp, oe:en], wn[:, nb:nb + (en - oe)]),
                        reads=(bS2,), writes=(bE2,))
            for tkb in range(ntk):
                emit_s(tkb)
                if tkb == 1 and p.pend is not None:
                    p.pend()
                    p.pend = None
                if tkb > 0:
                    for hh in range(2):
                        emit_pv(tkb - 1, hh)
            for hh in range(2):
                emit_pv(ntk - 1, hh)
            p.pend = epi

    if p.pend is not None:
        p.pend()
        p.pend = None
    load_w_fm(nc, S, p, io, io["wproj1"], W3, bW)
    S.dma("g", brow[:], io["bproj1"].rearrange("(o n) -> o n", o=1),
          reads=(bV2,), writes=(bV2,))
    ot2 = mem.at(OFF_B, [128, 2, D], F32, f"a_ot2{s}")
    bot = [S.fork(p.bB, f"a_ot{j}") for j in range(2)]
    bst2 = [S.fork(bV2, f"a_st2{t}") for t in range(NTILE)]
    bxt = [S.fork_multi([bF2, bxt[j]], f"a_xtD{j}") for j in range(2)]
    for ti in range(NTILE):
        j2 = ti % 2
        xs = xsl[j2]
        oslot = ot2[:, j2, :]
        banks = [p.pb[(ti % 2) * 2], p.pb[(ti % 2) * 2 + 1]]
        banksB = [p.pbB[(ti % 2) * 2], p.pbB[(ti % 2) * 2 + 1]]
        for nh2 in range(2):
            S.group("t", [
                (lambda e, nh2=nh2, bank=banks[nh2]: e.matmul(
                    bank[:], p.onesrow[:, 0:128], brow[0:1, nh2 * 512:(nh2 + 1) * 512],
                    start=True, stop=False))] + [
                (lambda e, k=k, ti=ti, nh2=nh2, bank=banks[nh2]:
                 e.matmul(bank[:], p.E[:, k, ti * 128:(ti + 1) * 128],
                          W3[:, k, nh2 * 512:(nh2 + 1) * 512],
                          start=False, stop=(k == KC - 1)))
                for k in range(KC)], reads=(bE2, bW, bV2, p.bC), writes=(banksB[nh2],))
        S.dma("q", xs, xh[base + ti * 128: base + (ti + 1) * 128, :], writes=(bxt[j2],))
        S.op("v", lambda e, ti=ti: e.scalar_tensor_tensor(
            nmD[:, ti:ti + 1], stats0[:, ti, 0:1], -1.0, rst0[:, ti:ti + 1],
            ALU.mult, ALU.mult), reads=(bst[ti],), writes=(bst[ti],))
        S.op("s", lambda e, ti=ti, xs=xs, oslot=oslot: e.activation(
            oslot, xs, AF.Identity, scale=rst0[:, ti:ti + 1],
            bias=nmD[:, ti:ti + 1]),
            reads=(bxt[j2], bst[ti]), writes=(bot[j2],))
        S.op("v", lambda e, ti=ti, oslot=oslot: e.tensor_tensor(
            oslot, oslot, p.pp[ti % 2][:], ALU.add),
            reads=(banksB[0], banksB[1]), writes=(bot[j2],))
        ln_stats(nc, S, p, oslot, bot[j2], stats2[:, ti, :], bst2[ti],
                 bnt2[:, j2, :], bbn[j2])
        rstd(nc, S, p, stats2[:, ti, 1:2], rst2[:, ti:ti + 1], bst2[ti], bst2[ti])
        S.dma("s", oh[ti * 128:(ti + 1) * 128, :], oslot, reads=(bot[j2],))

    S.absorb(p.bB, bot)
    S.absorb(p.bA, kA + qA)
    S.dma("g", wm1a[:, :, 0:2048].rearrange("p k n -> p k n"),
          io["w11"][:, 0:2048].rearrange("(k p) n -> p k n", p=128), writes=(bA,))
    S.dma("g", wm1b[:, :, 0:2048],
          io["w11"][:, 2048:4096].rearrange("(k p) n -> p k n", p=128), writes=(bB,))
    S.dma("g", wm2c[:, :, :, :],
          io["w21"][0:2048, :].rearrange("(c two p) n -> p c two n", p=128, two=2),
          writes=(bC,))
    S.dma("g", wm2d[:, 0:16 * 1024].rearrange("p (c n) -> p c n", n=1024),
          io["w21"][2048:4096, :].rearrange("(c p) n -> p c n", p=128), writes=(bD,))
    b11c = vv[:, 160:192]
    S.dma("s", b11c, io["b11"].rearrange("(c p) -> p c", p=128),
          reads=(bV2,), writes=(bV2,), slow=True)
    bc_tile(io["ln2_g1"].rearrange("(o n) -> o n", o=1), bcs0[:])
    bc_tile(io["ln2_b1"].rearrange("(o n) -> o n", o=1), bcs1[:])
    S.dma("g", brow[:], io["b21"].rearrange("(o n) -> o n", o=1),
          reads=(bV2,), writes=(bV2,))

    yo2 = mem.at(OFF_W + 8, [128, 2, D], F32, f"a_yo2{s}")
    byo = [S.fork(bW, f"a_yo{j}") for j in range(2)]

    def wm1ap(k, n2):
        if n2 < 16:
            return wm1a[:, k, n2 * 128:(n2 + 1) * 128]
        return wm1b[:, k, (n2 - 16) * 128:(n2 - 15) * 128]

    def wm2ap(k2, nh2):
        if k2 < 16:
            return wm2c[:, k2 // 2, k2 % 2, :][:, nh2 * 512:(nh2 + 1) * 512]
        return wm2d[:, (k2 - 16) * 1024 + nh2 * 512:(k2 - 16) * 1024 + (nh2 + 1) * 512]

    for sp in range(NSPAN):
        for j in range(4):
            ti = sp * 4 + j
            j2 = ti % 2
            xs = xsl[j2]
            S.dma("q", xs, oh[ti * 128:(ti + 1) * 128, :], writes=(bxt[j2],))
            S.op("v", lambda e, ti=ti, j=j, xs=xs: e.tensor_scalar(
                p.Sb[:, j, :], xs, stats2[:, ti, 0:1], rst2[:, ti:ti + 1],
                ALU.subtract, ALU.mult), reads=(bxt[j2], bst2[ti]), writes=(bsb[j],))
            S.op("v", lambda e, j=j: e.tensor_tensor(
                p.Sb[:, j, :], p.Sb[:, j, :], bcs0[:], ALU.mult),
                reads=(bV2,), writes=(bsb[j],))
            S.op("v", lambda e, j=j: e.tensor_tensor(
                p.Sb[:, j, :], p.Sb[:, j, :], bcs1[:], ALU.add),
                reads=(bV2,), writes=(bsb[j],))
        for c in range(KC):
            tpose4(nc, S, p, [p.Sb[:, j, c * 128:(c + 1) * 128] for j in range(4)],
                   [bsb[j] for j in range(4)], x2fm[:, c, 0:512], bW, pt_idx=c % 2)
        for n2 in range(32):
            bi = n2 % 2
            bank, bankB = p.pb[bi], p.pbB[bi]
            S.group("t", [
                (lambda e, k=k, n2=n2, bank=bank:
                 e.matmul(bank[:], wm1ap(k, n2), x2fm[:, k, 0:512],
                          start=(k == 0), stop=(k == KC - 1)))
                for k in range(KC)], reads=(bA, bB, bW), writes=(bankB,))
            S.op("s", lambda e, n2=n2, bank=bank: e.activation(
                hsp[:, n2, :], bank[:], AF.Gelu, bias=b11c[:, n2:n2 + 1]),
                reads=(bankB, bV2), writes=(bE2,))
        for j in range(4):
            ti = sp * 4 + j
            banks = [p.pb[2 + (j % 2) * 2], p.pb[3 + (j % 2) * 2]]
            banksB = [p.pbB[2 + (j % 2) * 2], p.pbB[3 + (j % 2) * 2]]
            for nh2 in range(2):
                S.group("t", [
                    (lambda e, nh2=nh2, bank=banks[nh2]: e.matmul(
                        bank[:], p.onesrow[:, 0:128],
                        brow[0:1, nh2 * 512:(nh2 + 1) * 512], start=True, stop=False))] + [
                    (lambda e, k2=k2, j=j, nh2=nh2, bank=banks[nh2]:
                     e.matmul(bank[:], hsp[:, k2, j * 128:(j + 1) * 128],
                              wm2ap(k2, nh2), start=False, stop=(k2 == 31)))
                    for k2 in range(32)],
                    reads=(bE2, bC, bD, bV2, p.bC), writes=(banksB[nh2],))
            j2 = ti % 2
            xs = xsl[j2]
            yslot = yo2[:, j2, :]
            S.dma("q", xs, oh[ti * 128:(ti + 1) * 128, :], writes=(bxt[j2],))
            for nh2 in range(2):
                S.op("v", lambda e, nh2=nh2, xs=xs, yslot=yslot, bank=banks[nh2]:
                     e.tensor_tensor(
                         yslot[:, nh2 * 512:(nh2 + 1) * 512], bank[:],
                         xs[:, nh2 * 512:(nh2 + 1) * 512], ALU.add),
                     reads=(banksB[nh2], bxt[j2]), writes=(byo[j2],))
            ln_stats(nc, S, p, yslot, byo[j2], stats2[:, ti, :], bst2[ti],
                     bnt2[:, j2, :], bbn[j2])
            rstd(nc, S, p, stats2[:, ti, 1:2], rst2[:, ti:ti + 1], bst2[ti], bst2[ti])
            S.dma("q", xs, xh[base + ti * 128: base + (ti + 1) * 128, :],
                  writes=(bxt[j2],))
            S.op("v", lambda e, ti=ti, yslot=yslot: e.tensor_scalar(
                yslot, yslot, stats2[:, ti, 0:1], rst2[:, ti:ti + 1],
                ALU.subtract, ALU.mult), reads=(bst2[ti],), writes=(byo[j2],))
            S.op("v", lambda e, xs=xs, yslot=yslot: e.tensor_tensor(
                yslot, yslot, xs, ALU.add), reads=(bxt[j2],), writes=(byo[j2],))
            S.dma("s", xh[base + ti * 128: base + (ti + 1) * 128, :], yslot,
                  reads=(byo[j2],))
    S.barrier()


def phase_l1ffn(nc, S, io, p, s, xh):
    base = s * T
    mem = p.mem
    vv = mem.at(OFF_V, [128, 256], F32, f"m_vv{s}")
    stats0 = vv[:, 0:32].rearrange("p (t two) -> p t two", two=2)
    rst0 = vv[:, 32:48]
    A3 = vv[:, 48:64]
    lg2 = vv[:, 64:96].rearrange("p (j c) -> p j c", j=2)
    mx82 = vv[:, 216:232].rearrange("p (j c) -> p j c", j=2)
    ng2 = vv[:, 232:236].rearrange("p (j c) -> p j c", j=2)
    sc1 = vv[:, 90:96]
    be1c = vv[:, 96:128]
    bg_b = vv[:, 128:144]
    bnt = vv[:, 144:156]
    bnt2 = vv[:, 144:168].rearrange("p (j c) -> p j c", j=2)
    gate_all = mem.at(OFF_V + 1, [128, NTILE, 16], F32, f"m_ga{s}")
    gscl = mem.at(OFF_V + 2, [128, NTILE, 16], F32, f"m_gs{s}")
    nsc_b = mem.at(OFF_V + 3, [128, D], BF16, f"m_nscb{s}")
    be2_all = mem.at(OFF_V + 5, [16, D], BF16, f"m_be2{s}")
    wgs = mem.at(OFF_V + 7, [128, KC, 16], BF16, f"m_wgs{s}")
    lgb = mem.at(OFF_V + 7.25, [128, 32], BF16, f"m_lgb{s}")
    gate_fm = mem.at(OFF_S, [16, T], BF16, f"m_gfm{s}")
    abuf = mem.at(OFF_S, [128, 512], BF16, f"m_ab{s}")
    pbuf = mem.at(OFF_S + 1, [128, 512], BF16, f"m_pb{s}")
    bbuf = mem.at(OFF_S + 2, [128, 512], BF16, f"m_bb{s}")
    brow2 = mem.at(OFF_S + 4, [1, 2 * D], BF16, f"m_br2{s}")
    xtile = mem.at(OFF_F, [128, D], F32, f"m_xt{s}")
    work = mem.at(OFF_F + 4, [128, D], F32, f"m_wk{s}")
    yac0 = mem.at(OFF_B, [128, KC, D], F32, f"m_y0{s}")
    yac1 = mem.at(OFF_C, [128, KC, D], F32, f"m_y1{s}")
    sfm = mem.at(OFF_D, [128, KC, T], BF16, f"m_sfm{s}")
    we1v = p.E[:]
    we2v = p.Wb[:]
    bA, bB, bC, bD, bE2 = p.bA, p.bB, p.bCc, p.bD, p.bE
    bW, bV2, bF2, bS2 = p.bW, p.bV, p.bF, p.bS

    def yac(ti):
        return (yac0[:, ti, :] if ti < KC else yac1[:, ti - KC, :])

    S.dma("g", wgs[:, :, :], io["wg"].rearrange("(k p) n -> p k n", p=128),
          writes=(bV2,))
    S.dma("g", brow2[0:1, 0:16], io["bg"].rearrange("(o n) -> o n", o=1),
          reads=(bS2,), writes=(bS2,))
    bcast_row(nc, S, p, brow2[:, 0:16], bS2, bg_b, bV2, 16, bank_i=4)
    S.dma("g", be2_all[:], io["be2"], writes=(bV2,))
    S.dma("g", brow2[0:1, 0:D], io["nsc"].rearrange("(o n) -> o n", o=1),
          reads=(bS2,), writes=(bS2,))
    bcast_row(nc, S, p, brow2[:, 0:D], bS2, nsc_b[:], bV2, D, bank_i=4)

    bxt = [S.fork(bF2, f"m_xt{j}") for j in range(2)]
    bst = [S.fork(bV2, f"m_st{t}") for t in range(NTILE)]
    bbn = [S.fork(bV2, f"m_bn{j}") for j in range(2)]
    bsb = [S.fork(bS2, f"m_sb{j}") for j in range(4)]
    bnt2 = vv[:, 192:216].rearrange("p (j c) -> p j c", j=2)
    xsl = [xtile[:], work[:]]
    for sp in range(NSPAN):
        for j in range(4):
            ti = sp * 4 + j
            j2 = ti % 2
            xs = xsl[j2]
            S.dma("q", xs, xh[base + ti * 128: base + (ti + 1) * 128, :],
                  writes=(bxt[j2],))
            ln_stats(nc, S, p, xs, bxt[j2], stats0[:, ti, :], bst[ti],
                     bnt2[:, j2, :], bbn[j2])
            rstd(nc, S, p, stats0[:, ti, 1:2], rst0[:, ti:ti + 1], bst[ti], bst[ti])
            S.op("v", lambda e, ti=ti, j2=j2: e.tensor_tensor(
                sc1[:, 3 * j2:3 * j2 + 1], rst0[:, ti:ti + 1], rst0[:, ti:ti + 1],
                ALU.mult), reads=(bst[ti],), writes=(bbn[j2],))
            S.op("v", lambda e, ti=ti, j2=j2: e.tensor_tensor(
                sc1[:, 3 * j2 + 1:3 * j2 + 2], sc1[:, 3 * j2:3 * j2 + 1],
                stats0[:, ti, 1:2], ALU.mult), reads=(bst[ti],), writes=(bbn[j2],))
            rstd(nc, S, p, sc1[:, 3 * j2 + 1:3 * j2 + 2], sc1[:, 3 * j2 + 2:3 * j2 + 3],
                 bbn[j2], bbn[j2])
            S.op("v", lambda e, ti=ti, j2=j2: e.tensor_tensor(
                A3[:, ti:ti + 1], rst0[:, ti:ti + 1], sc1[:, 3 * j2 + 2:3 * j2 + 3],
                ALU.mult), reads=(bbn[j2], bst[ti]), writes=(bst[ti],))
            S.op("v", lambda e, ti=ti, xs=xs: e.tensor_scalar(
                yac(ti), xs, stats0[:, ti, 0:1], rst0[:, ti:ti + 1],
                ALU.subtract, ALU.mult),
                reads=(bxt[j2], bst[ti]), writes=(bB if ti < KC else bC,))
            S.op("v", lambda e, ti=ti, j=j, xs=xs: e.tensor_scalar(
                p.Sb[:, j, :], xs, stats0[:, ti, 0:1], A3[:, ti:ti + 1],
                ALU.subtract, ALU.mult), reads=(bxt[j2], bst[ti]), writes=(bsb[j],))
            S.op("v", lambda e, j=j: e.tensor_tensor(
                p.Sb[:, j, :], p.Sb[:, j, :], nsc_b[:], ALU.mult),
                reads=(bV2,), writes=(bsb[j],))
        for c in range(KC):
            tpose4(nc, S, p, [p.Sb[:, j, c * 128:(c + 1) * 128] for j in range(4)],
                   [bsb[j] for j in range(4)], p.A[:, c, sp * SPAN:(sp + 1) * SPAN],
                   bA, pt_idx=c % 2)

    bgv = [S.fork(bV2, f"m_gv{j}") for j in range(2)]
    for ti in range(NTILE):
        bank, bankB = p.pb[ti % 2], p.pbB[ti % 2]
        S.group("t", [
            (lambda e, k=k, ti=ti, bank=bank:
             e.matmul(bank[:, 0:16], p.A[:, k, ti * 128:(ti + 1) * 128],
                      wgs[:, k, :], start=(k == 0), stop=(k == KC - 1)))
            for k in range(KC)], reads=(bA, bV2), writes=(bankB,))
        j2 = ti % 2
        gv = bgv[j2]
        lg = lg2[:, j2, :]
        mx8 = mx82[:, j2, :]
        negm1 = ng2[:, j2, 0:1]
        zrec = ng2[:, j2, 1:2]
        S.op("v", lambda e, lg=lg, bank=bank: e.tensor_tensor(
            lg, bank[:, 0:16], bg_b, ALU.add),
            reads=(bankB, bV2), writes=(gv,))
        S.op("v", lambda e, lg=lg, mx8=mx8: e.max(out=mx8, in_=lg),
             reads=(), writes=(gv,))
        S.op("v", lambda e, mx8=mx8, negm1=negm1: e.tensor_scalar_mul(
            negm1, mx8[:, 0:1], -1.0), reads=(), writes=(gv,))
        S.op("s", lambda e, ti=ti, lg=lg, negm1=negm1: e.activation(
            gate_all[:, ti, :], lg, AF.Exp, bias=negm1),
            reads=(gv,), writes=(gv,))
        S.op("v", lambda e, ti=ti, lg=lg, mx8=mx8: e.tensor_scalar(
            gscl[:, ti, :], lg, mx8[:, 3:4], None, ALU.is_ge),
            reads=(), writes=(gv,))
        S.op("v", lambda e, ti=ti: e.tensor_tensor(
            gate_all[:, ti, :], gate_all[:, ti, :], gscl[:, ti, :], ALU.mult),
            reads=(), writes=(gv,))
        S.op("v", lambda e, ti=ti, zrec=zrec: e.tensor_reduce(
            zrec, gate_all[:, ti, :], AX.X, ALU.add),
            reads=(), writes=(gv,))
        S.op("v", lambda e, zrec=zrec: e.reciprocal(zrec, zrec),
             reads=(), writes=(gv,))
        S.op("v", lambda e, ti=ti, zrec=zrec: e.tensor_scalar(
            gate_all[:, ti, :], gate_all[:, ti, :], zrec, None, ALU.mult),
            reads=(), writes=(gv,))
        S.op("v", lambda e, ti=ti: e.tensor_scalar_mul(
            gscl[:, ti, :], gate_all[:, ti, :], 1.0 / ALPHA_SILU),
            reads=(), writes=(gv,))
        S.op("v", lambda e, ti=ti, j2=j2: e.tensor_copy(
            lgb[:, j2 * 16:(j2 + 1) * 16], gate_all[:, ti, :]),
            reads=(), writes=(gv,))
        S.op("t", lambda e, ti=ti, j2=j2: e.transpose(
            p.pt[j2][0:16, 0:128], lgb[:, j2 * 16:(j2 + 1) * 16], p.ident[:]),
            reads=(gv, p.bC), writes=(p.ptB[j2],))
        S.op("v", lambda e, ti=ti, j2=j2: e.tensor_copy(
            gate_fm[:, ti * 128:(ti + 1) * 128], p.pt[j2][0:16, 0:128]),
            reads=(p.ptB[j2],), writes=(bS2,))
        banks = [p.pb[3 + (ti % 2)], p.pb[5 - (ti % 2)]]
        banksB = [p.pbB[3 + (ti % 2)], p.pbB[5 - (ti % 2)]]
        for nh2 in range(2):
            S.op("t", lambda e, ti=ti, nh2=nh2, bank=banks[nh2]: e.matmul(
                bank[:], gate_fm[:, ti * 128:(ti + 1) * 128],
                be2_all[:, nh2 * 512:(nh2 + 1) * 512], start=True, stop=True),
                reads=(bS2, bV2), writes=(banksB[nh2],))
            S.op("v", lambda e, ti=ti, nh2=nh2, bank=banks[nh2]: e.tensor_tensor(
                yac(ti)[:, nh2 * 512:(nh2 + 1) * 512],
                yac(ti)[:, nh2 * 512:(nh2 + 1) * 512], bank[:], ALU.add),
                reads=(banksB[nh2],), writes=(bB if ti < KC else bC,))

    for ex in range(NEXP):
        S.dma("g", we1v[:, :, 0:2048],
              io["we1"][ex].rearrange("(k p) n -> p k n", p=128), writes=(bE2,))
        S.dma("g", we2v[:, :, 0:1024],
              io["we2"][ex].rearrange("(k p) n -> p k n", p=128), writes=(bW,))
        S.dma("g", brow2[:], io["be1"][ex].rearrange("(o n) -> o n", o=1),
              reads=(bS2,), writes=(bS2,))
        emits = []
        for cc in range(16):
            emits.append(lambda e, cc=cc: e.transpose(
                p.pt[0][:, 2 * cc:2 * cc + 1], brow2[0:1, cc * 128:(cc + 1) * 128],
                p.ident[0:1, 0:1]))
        S.group("t", emits, reads=(bS2, p.bC), writes=(p.ptB[0],))
        S.op("v", lambda e: e.tensor_copy(
            be1c[:, 0:16].rearrange("p c -> p c 1" if False else "p (c o) -> p c o", o=1),
            p.pt[0][:, 0:32].rearrange("p (c two) -> p c two", two=2)[:, :, 0:1]),
            reads=(p.ptB[0],), writes=(bV2,))
        for ic in range(KC):
            for sp in range(NSPAN):
                ba, baB = p.pb[sp % 2], p.pbB[sp % 2]
                bb_, bbB = p.pb[2 + sp % 2], p.pbB[2 + sp % 2]
                S.group("t", [
                    (lambda e, k=k, ic=ic, sp=sp, ba=ba:
                     e.matmul(ba[:], we1v[:, k, ic * 128:(ic + 1) * 128],
                              p.A[:, k, sp * SPAN:(sp + 1) * SPAN],
                              start=(k == 0), stop=(k == KC - 1)))
                    for k in range(KC)], reads=(bE2, bA), writes=(baB,))
                S.group("t", [
                    (lambda e, k=k, ic=ic, sp=sp, bb_=bb_:
                     e.matmul(bb_[:], we1v[:, k, 1024 + ic * 128:1024 + (ic + 1) * 128],
                              p.A[:, k, sp * SPAN:(sp + 1) * SPAN],
                              start=(k == 0), stop=(k == KC - 1)))
                    for k in range(KC)], reads=(bE2, bA), writes=(bbB,))
                S.op("v", lambda e, ic=ic, ba=ba: e.tensor_scalar(
                    abuf[:], ba[:], be1c[:, ic:ic + 1], LIMIT, ALU.add, ALU.min),
                    reads=(baB, bV2), writes=(bS2,))
                S.op("s", lambda e: e.activation(pbuf[:], abuf[:], AF.Silu,
                                                 scale=ALPHA_SILU),
                     reads=(bS2,), writes=(bS2,))
                S.op("v", lambda e, ic=ic, bb_=bb_: e.tensor_scalar(
                    bbuf[:], bb_[:], be1c[:, 8 + ic:8 + ic + 1], -LIMIT,
                    ALU.add, ALU.max), reads=(bbB, bV2), writes=(bS2,))
                S.op("v", lambda e: e.tensor_scalar(
                    bbuf[:], bbuf[:], LIMIT, 1.0, ALU.min, ALU.add),
                    reads=(bS2,), writes=(bS2,))
                S.op("v", lambda e, ic=ic, sp=sp: e.tensor_tensor(
                    sfm[:, ic, sp * SPAN:(sp + 1) * SPAN], pbuf[:], bbuf[:], ALU.mult),
                    reads=(bS2,), writes=(bD,))
        for ti in range(NTILE):
            banks = [p.pb[4], p.pb[5]]
            banksB = [p.pbB[4], p.pbB[5]]
            for nh2 in range(2):
                S.group("t", [
                    (lambda e, ic=ic, ti=ti, nh2=nh2, bank=banks[nh2]:
                     e.matmul(bank[:], sfm[:, ic, ti * 128:(ti + 1) * 128],
                              we2v[:, ic, nh2 * 512:(nh2 + 1) * 512],
                              start=(ic == 0), stop=(ic == KC - 1)))
                    for ic in range(KC)], reads=(bD, bW), writes=(banksB[nh2],))
            for nh2 in range(2):
                S.op("v", lambda e, ti=ti, ex=ex, nh2=nh2, bank=banks[nh2]:
                     e.scalar_tensor_tensor(
                         yac(ti)[:, nh2 * 512:(nh2 + 1) * 512], bank[:],
                         gscl[:, ti, ex:ex + 1],
                         yac(ti)[:, nh2 * 512:(nh2 + 1) * 512], ALU.mult, ALU.add),
                     reads=(banksB[nh2], bV2), writes=(bB if ti < KC else bC,))

    xt2 = mem.at(OFF_A, [128, 2, D], F32, f"m_xt2{s}")
    wk2 = mem.at(OFF_A + 8, [128, 2, D], F32, f"m_wk2{s}")
    bxt = [S.fork(bA, f"m_bxt{j}") for j in range(2)]
    bwk = [S.fork(bA, f"m_bwk{j}") for j in range(2)]
    bstt = [S.fork(bV2, f"m_bst{t}") for t in range(NTILE)]
    bbnt = [S.fork(bV2, f"m_bbnt{j}") for j in range(2)]
    for ti in range(NTILE):
        j = ti % 2
        yb = bB if ti < KC else bC
        ln_stats(nc, S, p, yac(ti), yb, stats0[:, ti, :], bstt[ti],
                 bnt2[:, j, :], bbnt[j])
        rstd(nc, S, p, stats0[:, ti, 1:2], rst0[:, ti:ti + 1], bstt[ti], bstt[ti])
        S.dma("q", xt2[:, j, :], xh[base + ti * 128: base + (ti + 1) * 128, :],
              writes=(bxt[j],))
        S.op("v", lambda e, ti=ti, j=j: e.tensor_scalar(
            wk2[:, j, :], yac(ti), stats0[:, ti, 0:1], rst0[:, ti:ti + 1],
            ALU.subtract, ALU.mult), reads=(bstt[ti], bB if ti < KC else bC),
            writes=(bwk[j],))
        S.op("v", lambda e, j=j: e.tensor_tensor(wk2[:, j, :], wk2[:, j, :],
                                                 xt2[:, j, :], ALU.add),
             reads=(bxt[j],), writes=(bwk[j],))
        S.dma("s", io["y_out"][base + ti * 128: base + (ti + 1) * 128, :],
              wk2[:, j, :], reads=(bwk[j],))
    S.barrier()


def final_store(nc, S, io, p, s, xh):
    base = s * T
    mem = p.mem
    tb = mem.at(OFF_F, [128, 2, D], F32, f"fst{s}")
    bt = [Buf(f"fs{j}") for j in range(2)]
    for ti in range(NTILE):
        j = ti % 2
        S.dma("q", tb[:, j, :], xh[base + ti * 128: base + (ti + 1) * 128, :],
              writes=(bt[j],))
        S.dma("s", io["y_out"][base + ti * 128: base + (ti + 1) * 128, :],
              tb[:, j, :], reads=(bt[j],))
    S.barrier()



def kernel(**inputs):
    w = prep_weights(inputs)
    c = host_constants()
    x = np.ascontiguousarray(np.asarray(inputs["L0_attn_x"], dtype=np.float32))
    nc = build_program()
    in_maps = []
    for core in range(NCORE):
        m = {"x_in": np.ascontiguousarray(
            x[core * NSEQ:(core + 1) * NSEQ].reshape(NSEQ * T, D))}
        for k, v in w.items():
            m[k] = np.ascontiguousarray(np.asarray(v, dtype=np.float32))
        for k, v in c.items():
            m[k] = np.ascontiguousarray(v)
        in_maps.append(m)
    res = run_bass_kernel_spmd(nc, in_maps, core_ids=list(range(NCORE)))
    out = np.empty((NCORE * NSEQ, T, D), dtype=np.float32)
    for core in range(NCORE):
        out[core * NSEQ:(core + 1) * NSEQ] = res.results[core]["y_out"].reshape(NSEQ, T, D)
    return out
